# version 01
# baseline (speedup 1.0000x reference)
import sys
sys.path.insert(0, "/opt/trn_rl_repo")
from collections import defaultdict

import numpy as np
import ml_dtypes

import concourse.bass as bass
import concourse.mybir as mybir
from concourse.bass_utils import run_bass_kernel_spmd

F32 = mybir.dt.float32
BF16 = mybir.dt.bfloat16
AF = mybir.ActivationFunctionType
ALU = mybir.AluOpType
AX = mybir.AxisListType
BF = ml_dtypes.bfloat16

L = 2048
D = 2048
NH, HD, CS = 32, 64, 16
NCH = L // CS
TTT_EPS = 1e-6
LN_EPS = 1e-5

ENGS = ("t", "v", "s", "p", "q")


class Tok:
    __slots__ = ("sem", "val")

    def __init__(self, sem, val):
        self.sem = sem
        self.val = val


class Prog:

    def __init__(self, nc):
        self.nc = nc
        self.ops = {e: [] for e in ENGS}
        self.sems = {}
        self.cnt = defaultdict(int)
        self.waited = {e: defaultdict(int) for e in ENGS}
        self._sem_ctxs = []
        for e in ENGS:
            self._mksem("c" + e)

    def _mksem(self, name):
        ctx = self.nc.semaphore(name)
        h = ctx.__enter__()
        self._sem_ctxs.append(ctx)
        self.sems[name] = h
        return h

    def _wait(self, eng_name, deps):
        need = {}
        for dd in deps:
            if dd is None:
                continue
            if self.waited[eng_name][dd.sem] < dd.val:
                need[dd.sem] = max(need.get(dd.sem, 0), dd.val)
        for sname, val in need.items():
            self.waited[eng_name][sname] = val
            h = self.sems[sname]

            def w(eng, h=h, val=val):
                eng.wait_ge(h, val)

            self.ops[eng_name].append(w)

    def op(self, eng_name, method, *args, deps=(), ret_tok=True, **kwargs):
        self._wait(eng_name, deps)
        if ret_tok:
            cs = "c" + eng_name
            self.cnt[cs] += 1
            val = self.cnt[cs]
            h = self.sems[cs]

            def f(eng, method=method, args=args, kwargs=kwargs, h=h):
                getattr(eng, method)(*args, **kwargs).then_inc(h, 1)

            self.ops[eng_name].append(f)
            return Tok(cs, val)
        else:
            def f(eng, method=method, args=args, kwargs=kwargs):
                getattr(eng, method)(*args, **kwargs)

            self.ops[eng_name].append(f)
            return None

    def dma(self, eng_name, out, in_, deps=(), sem=None, **kwargs):
        ds = sem if sem is not None else "d" + eng_name
        if ds not in self.sems:
            self._mksem(ds)
        prev = Tok(ds, self.cnt[ds]) if self.cnt[ds] else None
        self._wait(eng_name, list(deps) + ([prev] if prev else []))
        self.cnt[ds] += 16
        val = self.cnt[ds]
        h = self.sems[ds]

        def f(eng, out=out, in_=in_, kwargs=kwargs, h=h):
            eng.dma_start(out=out, in_=in_, **kwargs).then_inc(h, 16)

        self.ops[eng_name].append(f)
        return Tok(ds, val)

    def init_consts(self, values):
        toks = []
        for i, v in enumerate(values):
            ap = self.nc.alloc_sbuf_tensor(f"cst_{i}", [128, 1], F32).ap()
            try:
                self.nc.const_aps.register(F32, float(v), ap)
            except Exception:
                continue
            toks.append(self.v("memset", ap, float(v)))
        for e in ENGS:
            self._wait(e, toks)

    def t(self, method, *a, **k):
        return self.op("t", method, *a, **k)

    def v(self, method, *a, **k):
        return self.op("v", method, *a, **k)

    def s(self, method, *a, **k):
        return self.op("s", method, *a, **k)

    def p(self, method, *a, **k):
        return self.op("p", method, *a, **k)

    def finish(self, final_deps):
        self._wait("q", final_deps)
        self.build()

    def build(self):
        with self.nc.Block() as block:
            for e, section in (("t", block.tensor), ("v", block.vector),
                               ("s", block.scalar), ("p", block.gpsimd),
                               ("q", block.sync)):
                ops = self.ops[e]
                if not ops:
                    continue

                def body(eng, ops=ops):
                    for f in ops:
                        f(eng)

                section(body)
        for ctx in self._sem_ctxs:
            ctx.__exit__(None, None, None)


def mmf(i, n):
    return dict(start=(i == 0), stop=(i == n - 1))


def host_consts():
    c = {}
    inv = 1.0 / 10000.0 ** (np.arange(0, HD, 2) / HD)
    th = np.outer(inv, np.arange(L))
    C64 = np.repeat(np.cos(th), 2, axis=0)
    S64 = np.repeat(np.sin(th), 2, axis=0) * np.tile([-1.0, 1.0], 32)[:, None]
    c["ropeC"] = np.tile(C64, (2, 1)).astype(BF)
    c["ropeS"] = np.tile(S64, (2, 1)).astype(BF)
    swap = np.zeros((128, 128), np.float32)
    for i in range(128):
        swap[i, i ^ 1] = 1.0
    c["swapP"] = swap.astype(BF)
    c["ident"] = np.eye(128, dtype=np.float32).astype(BF)
    c["identf"] = np.eye(128, dtype=np.float32)
    eta = 1.0 / np.arange(1, CS + 1)
    em = (np.tril(np.ones((CS, CS))) * eta[:, None]).T.astype(np.float32)
    c["emx"] = np.tile(em, (1, 32)).astype(BF)
    c["ones128"] = np.ones((1, 128), np.float32)
    ob = np.zeros((128, 2), np.float32)
    ob[0:64, 0] = 1.0
    ob[64:128, 1] = 1.0
    c["onesblk"] = ob
    c["onesblkT"] = np.ascontiguousarray(ob.T)
    return c


def ttt_layouts(W1, b1, gw, gb):
    w1blk = np.zeros((128, 2048), np.float32)
    b1t0 = np.zeros((128, 16), np.float32)
    for p in range(16):
        w1blk[0:64, 128 * p:128 * p + 64] = W1[2 * p]
        w1blk[64:128, 128 * p + 64:128 * p + 128] = W1[2 * p + 1]
        b1t0[0:64, p] = b1[2 * p, 0]
        b1t0[64:128, p] = b1[2 * p + 1, 0]
    gcol = np.ascontiguousarray(gw.reshape(2048, 1), dtype=np.float32)
    bcol = np.ascontiguousarray(gb.reshape(2048, 1), dtype=np.float32)
    gbb3 = np.zeros((3, 2048), np.float32)
    gbb3[2] = gb.reshape(2048)
    gflat = gw.reshape(2048)
    for p in range(16):
        gbb3[0, 128 * p:128 * p + 64] = gflat[128 * p:128 * p + 64]
        gbb3[1, 128 * p + 64:128 * p + 128] = gflat[128 * p + 64:128 * p + 128]
    return w1blk, b1t0, gcol, bcol, gbb3


def build_ph1(nc, P, d):
    sb = lambda name, shape, dt: nc.alloc_sbuf_tensor("p1_" + name, shape, dt).ap()
    psum = lambda name, shape, dt: nc.alloc_psum_tensor("p1_" + name, shape, dt).ap()

    xnt = sb("xnt", [128, 16 * L], BF16)
    wsb = sb("wsb", [128, 16 * D], BF16)
    x0a = sb("x0a", [128, D], F32)
    x0b = sb("x0b", [128, D], F32)
    ropeCb = sb("ropeCb", [128, L], BF16)
    ropeSb = sb("ropeSb", [128, L], BF16)
    swapPb = sb("swapPb", [128, 128], BF16)
    identb = sb("identb", [128, 128], BF16)
    gcolb = sb("gcolb", [128, 16], F32)
    bcolb = sb("bcolb", [128, 16], F32)
    stats = sb("stats", [128, 8], F32)
    sqjunk = sb("sqjunk", [128, D], F32)
    xn_n = sb("xn_n", [128, D], BF16)
    qtb = sb("qtb", [128, 512], BF16)
    qsw = sb("qsw", [128, 512], BF16)
    ro1 = sb("ro1", [128, 512], BF16)
    kre = sb("kre", [128, 512], BF16)
    kmv = sb("kmv", [128, 512], F32)

    ps_mm = psum("ps_mm", [128, 512], F32)
    ps_sw = psum("ps_sw", [128, 512], F32)
    ps_tr = psum("ps_tr", [128, 512], BF16)

    lC = P.dma("q", ropeCb, d["ropeC"], sem="sc1")
    lS = P.dma("q", ropeSb, d["ropeS"], sem="sc1")
    lP_ = P.dma("q", swapPb, d["swapP"], sem="sc1")
    lI = P.dma("q", identb, d["ident"], sem="sc1")
    lg = P.dma("q", gcolb, d["gcol"].rearrange("(k p) o -> p (k o)", p=128), sem="sc1",
               allow_slow_non_contiguous=True)
    lb = P.dma("q", bcolb, d["bcol"].rearrange("(k p) o -> p (k o)", p=128), sem="sc1",
               allow_slow_non_contiguous=True)

    def load_w(wap, deps):
        toks = []
        for q in range(4):
            src = wap[512 * q:512 * (q + 1), :].rearrange("(k p) o -> p k o", p=128)
            dst = wsb[:, D * 4 * q:D * 4 * (q + 1)].rearrange("p (k o) -> p k o", k=4)
            toks.append(P.dma("p", dst, src, deps=deps, sem="sw" + str(q)))
        return toks

    prev_chain = [lI]
    tr_done = []
    for i in range(16):
        buf = x0a if i % 2 == 0 else x0b
        war = [tr_done[i - 2]] if i >= 2 else []
        lt = P.dma("q", buf, d["x0"][128 * i:128 * (i + 1), :], deps=war, sem="sx" + str(i % 2))
        t_s = P.v("reduce_sum", stats[:, 0:1], buf, axis=AX.X, deps=[lt] + prev_chain)
        t_sq = P.s("activation", sqjunk, buf, AF.Square, accum_out=stats[:, 1:2],
                   deps=[lt] + prev_chain)
        t_mu = P.v("tensor_scalar", stats[:, 2:3], stats[:, 0:1], 1.0 / D, None, ALU.mult,
                   deps=[t_s])
        t_m2 = P.v("tensor_mul", stats[:, 3:4], stats[:, 2:3], stats[:, 2:3], deps=[t_mu])
        t_va = P.v("scalar_tensor_tensor", stats[:, 4:5], stats[:, 1:2], 1.0 / D, stats[:, 3:4],
                   ALU.mult, ALU.subtract, deps=[t_sq, t_m2])
        t_lnv = P.s("activation", stats[:, 5:6], stats[:, 4:5], AF.Ln, bias=LN_EPS, deps=[t_va])
        t_rs = P.s("activation", stats[:, 6:7], stats[:, 5:6], AF.Exp, scale=-0.5, deps=[t_lnv])
        t_xn = P.v("tensor_scalar", xn_n, buf, stats[:, 2:3], stats[:, 6:7],
                   ALU.subtract, ALU.mult, deps=[t_rs])
        prev_chain = [t_xn, t_sq]
        evs = []
        for j4 in range(4):
            trs = []
            for jj in range(4):
                j = 4 * j4 + jj
                war2 = [evs[-1]] if evs else ([tr_done[i - 1]] if i >= 1 else [])
                trs.append(P.t("transpose", ps_tr[:, 128 * jj:128 * (jj + 1)],
                               xn_n[:, 128 * j:128 * (j + 1)], identb,
                               deps=[t_xn] + war2))
            dview = xnt.rearrange("p (k t) -> p k t", k=16)[:, 4 * j4:4 * j4 + 4,
                                                           128 * i:128 * (i + 1)]
            evs.append(P.v("tensor_copy", dview, ps_tr.rearrange("p (k t) -> p k t", k=4),
                           deps=trs))
        tr_done.append(evs[-1])
    xnt_ready = tr_done + [prev_chain[0]]
    so_xnt = P.dma("q", d["xnt"], xnt, deps=xnt_ready, sem="soxnt")

    def proj(wname, out_dram, kind, wdeps):
        wtoks = load_w(d[wname], wdeps)
        outs = []
        for m in range(16):
            for nb in range(4):
                mms = []
                for k in range(16):
                    mms.append(P.t("matmul", ps_mm,
                                   wsb[:, D * k + 128 * m:D * k + 128 * (m + 1)],
                                   xnt[:, L * k + 512 * nb:L * k + 512 * (nb + 1)],
                                   deps=(wtoks + xnt_ready + outs[-1:] if k == 0 else []),
                                   **mmf(k, 16)))
                if kind == "rope":
                    ev = P.s("copy", qtb, ps_mm, deps=[mms[-1]])
                    sw = P.t("matmul", ps_sw, swapPb, qtb, deps=[ev], start=True, stop=True)
                    ev2 = P.s("copy", qsw, ps_sw, deps=[sw])
                    v1 = P.v("tensor_mul", ro1, qtb, ropeCb[:, 512 * nb:512 * (nb + 1)],
                             deps=[ev, lC])
                    v2 = P.v("tensor_mul", qsw, qsw, ropeSb[:, 512 * nb:512 * (nb + 1)],
                             deps=[ev2, lS])
                    v3 = P.v("tensor_add", qtb, ro1, qsw, deps=[v1, v2])
                    st = P.dma("s", out_dram[128 * m:128 * (m + 1), 512 * nb:512 * (nb + 1)],
                               qtb, deps=[v3], sem="sout")
                    outs.append(st)
                else:
                    lk = P.dma("q", kre, d["ktr"][128 * m:128 * (m + 1), 512 * nb:512 * (nb + 1)],
                               deps=(outs[-1:] if outs else wdeps), sem="skre")
                    v1 = P.v("scalar_tensor_tensor", kmv, ps_mm, -1.0, kre,
                             ALU.mult, ALU.add, deps=[mms[-1], lk])
                    v2 = P.v("tensor_scalar", kmv, kmv, bcolb[:, m:m + 1], gcolb[:, m:m + 1],
                             ALU.add, ALU.mult, deps=[v1, lg, lb])
                    st = P.dma("s", d["bmtp"][128 * m:128 * (m + 1), 512 * nb:512 * (nb + 1)],
                               kmv, deps=[v2], sem="sout")
                    outs.append(st)
        return outs, wtoks

    oq, wq_toks = proj("wq", d["qtr"], "rope", [])
    ok, wk_toks = proj("wk", d["ktr"], "rope", oq)
    ov, wv_toks = proj("wv", None, "bmt", ok)
    return [so_xnt] + oq + ok + ov


def build_ph2(nc, P, d, nsteps=128, stage=99):
    import os as _os
    _skip = _os.environ.get("DEVSKIP", "").split(",")
    sb = lambda name, shape, dt: nc.alloc_sbuf_tensor("p2_" + name, shape, dt).ap()
    ps = lambda name, shape, dt: nc.alloc_psum_tensor("p2_" + name, shape, dt).ap()

    qtrb = sb("qtrb", [128, 16 * 512], BF16)
    ktrb = sb("ktrb", [128, 16 * 512], BF16)
    bmtpb = sb("bmtpb", [128, 16 * 512], F32)
    W1b = sb("W1b", [128, 2048], BF16)
    w1f = sb("w1f", [128, 2048], F32)
    b1T = sb("b1T", [128, 16], F32)
    gcolb = sb("gcolb", [128, 16], F32)
    g2b = sb("g2b", [128, 16], F32)
    emxb = sb("emxb", [16, 512], BF16)
    ones128 = sb("ones128", [1, 128], F32)
    onesblk = sb("onesblk", [128, 2], F32)
    onesblkT = sb("onesblkT", [2, 128], F32)
    gbb3 = sb("gbb3", [3, 2048], F32)
    identb = sb("identb", [128, 128], BF16)
    attnTs = sb("attnTs", [16, 512], BF16)
    kmA = sb("kmA", [128, 256], BF16)
    kmB = sb("kmB", [128, 256], BF16)
    app = sb("app", [16, 512], BF16)
    appt = sb("appt", [16, 512], BF16)
    zb = sb("zb", [128, 512], F32)
    xh = sb("xh", [128, 256], F32)
    xht = sb("xht", [128, 256], F32)
    gx = sb("gx", [128, 512], F32)
    c1b = sb("c1b", [128, 256], F32)
    gradT = sb("gradT", [128, 256], BF16)
    gradN3 = sb("gradN3", [48, 2048], BF16)
    knS3 = sb("knS3", [48, 2048], BF16)
    app2 = sb("app2", [48, 256], BF16)
    statsS = sb("statsS", [2, 512], F32)
    rows = sb("rows", [2, 512], F32)
    rows2 = sb("rows2", [3, 512], F32)
    lnb = sb("lnb", [128, 512], F32)
    lnt = sb("lnt", [128, 256], F32)
    b1red = sb("b1red", [128, 16], F32)

    psA = ps("psA", [128, 512], F32)
    psB = ps("psB", [128, 512], F32)
    psX = ps("psX", [128, 512], F32)
    psT = ps("psT", [16, 1024], BF16)
    psW = ps("psW", [128, 2048], F32)

    LE = 1.0 / 16.0

    if "gcol" in _skip:
        lq0 = P.v("memset", gcolb, 1.0)
    else:
        lq0 = P.dma("q", gcolb, d["gcol"].rearrange("(k p) o -> p (k o)", p=128),
                    sem="scst", allow_slow_non_contiguous=True)
    l1 = P.v("memset", b1T, 0.0) if "b1t0" in _skip else P.dma("q", b1T, d["b1t0"], sem="scst")
    l2 = P.v("memset", emxb, 0.1) if "emx" in _skip else P.dma("q", emxb, d["emx"], sem="scst")
    l3 = P.v("memset", ones128, 1.0) if "ones" in _skip else P.dma("q", ones128, d["ones128"], sem="scst")
    l4 = P.v("memset", onesblk, 0.5) if "ones" in _skip else P.dma("q", onesblk, d["onesblk"], sem="scst")
    l4b = P.v("memset", onesblkT, 0.5) if "ones" in _skip else P.dma("q", onesblkT, d["onesblkT"], sem="scst")
    l5 = P.v("memset", gbb3, 0.2) if "gbb3" in _skip else P.dma("q", gbb3, d["gbb3"], sem="scst")
    l6 = P.v("memset", identb, 0.0) if "ident" in _skip else P.dma("q", identb, d["ident"], sem="scst")
    l7 = P.v("memset", w1f, 0.01) if "w1ld" in _skip else P.dma("q", w1f, d["w1blk"], sem="scst")
    vg2 = P.v("tensor_mul", g2b, gcolb, gcolb, deps=[lq0])
    vr0 = P.v("memset", rows2[:, 0:256], 1.0)
    vr1 = P.v("memset", rows2[:, 256:512], 0.0)
    sh0 = P.s("copy", W1b, w1f, deps=[l7])
    wzero = P.v("memset", psW[0:1, 0:1], 0.0) if "wzero" in _skip else P.v("memset", psW, 0.0)
    mz1 = P.v("memset", gradN3, 0.0)
    mz2 = P.v("memset", knS3, 0.0)
    mz3 = P.v("memset", app2, 0.0)
    mz4 = P.v("memset", kmA[64:128, :], 0.0)
    mz5 = P.v("memset", kmB[0:64, :], 0.0)

    shadow_tok = sh0
    wprev = wzero
    b1_tok = l1
    prev_store = None
    out_toks = []

    def pview(buf, k=16):
        return buf.rearrange("p (k t) -> p k t", k=k)

    nblocks = (nsteps + 31) // 32
    for blk in range(nblocks):
        bs = slice(512 * blk, 512 * (blk + 1))
        war = [prev_store] if prev_store else []
        _e = "p" if "blkload" in _skip else "q"
        lqb = P.dma(_e, qtrb.rearrange("p (k t) -> p k t", k=16),
                    d["qtr"][:, bs].rearrange("(k p) t -> p k t", p=128),
                    deps=war, sem="sqb")
        lkb = P.dma(_e, ktrb.rearrange("p (k t) -> p k t", k=16),
                    d["ktr"][:, bs].rearrange("(k p) t -> p k t", p=128),
                    deps=war, sem="skb")
        lbb = P.dma(_e, bmtpb.rearrange("p (k t) -> p k t", k=16),
                    d["bmtp"][:, bs].rearrange("(k p) t -> p k t", p=128),
                    deps=war, sem="sbb")
        blk_loads = [lqb, lkb, lbb]
        last_fin = None
        for ci in range(min(32, nsteps - 32 * blk)):
            tc = slice(16 * ci, 16 * (ci + 1))

            def qsl(buf, p):
                return buf[:, 512 * p + 16 * ci:512 * p + 16 * (ci + 1)]

            m1toks = []
            dep0 = [shadow_tok] + blk_loads + ([last_fin] if last_fin else [])
            if stage < 11:
                last_fin = shadow_tok
                continue
            _np16 = int(_os.environ.get("NP16", "16"))
            for p in range(_np16):
                w = W1b[:, 128 * p:128 * (p + 1)]
                m1toks.append(P.t("matmul", psA[0:128, 16 * p:16 * (p + 1)], w, qsl(ktrb, p),
                                  deps=(dep0 if p == 0 else []), start=(p == 0),
                                  stop=(p == _np16 - 1)))
                m1toks.append(P.t("matmul", psB[0:128, 16 * p:16 * (p + 1)], w, qsl(qtrb, p),
                                  deps=[], start=(p == 0),
                                  stop=(stage < 12 and p == _np16 - 1)))
            if stage < 11:
                last_fin = m1toks[-1]
                continue
            kv = ktrb.rearrange("p (k t) -> p k t", k=16)[:, :, tc]
            vkA = P.v("tensor_copy", kmA[0:64, :].rearrange("p (k t) -> p k t", k=16),
                      kv[0:64], deps=dep0 + [mz4, mz5])
            vkB = P.v("tensor_copy", kmB[64:128, :].rearrange("p (k t) -> p k t", k=16),
                      kv[64:128], deps=dep0)
            attoks = []
            for h in range(32):
                p, h2 = divmod(h, 2)
                km = kmA if h2 == 0 else kmB
                attoks.append(P.t("matmul", psX[0:16, 32 * p + 16 * h2:32 * p + 16 * (h2 + 1)],
                                  km[:, 16 * p:16 * (p + 1)], qsl(qtrb, p),
                                  deps=([vkA, vkB] if h == 0 else []),
                                  start=(h == 0), stop=(h == 31)))
            eat = P.s("copy", attnTs, psX[0:16, :], deps=attoks)
            ga1 = P.p("tensor_mul", appt, attnTs, emxb, deps=[eat, l2])
            ga2 = P.p("tensor_add", app, appt, emxb, deps=[ga1])
            avA = app.rearrange("q (a c t) -> q a c t", a=16, c=2)[:, :, 0, :]
            avB = app.rearrange("q (a c t) -> q a c t", a=16, c=2)[:, :, 1, :]
            capA = P.p("tensor_copy", app2[0:16, :].rearrange("q (a t) -> q a t", a=16),
                       avA, deps=[ga2, mz3])
            capB = P.p("tensor_copy", app2[32:48, :].rearrange("q (a t) -> q a t", a=16),
                       avB, deps=[ga2])
            if stage < 12:
                last_fin = capB
                continue
            b1bc = b1T.unsqueeze(2).broadcast_to([128, 16, 16])
            vz1 = P.v("tensor_add", pview(zb[:, 0:256]), pview(psA[:, 0:256]), b1bc,
                      deps=m1toks + [b1_tok])
            ssq = P.s("activation", zb[:, 256:512], zb[:, 0:256], AF.Square, deps=[vz1])
            if stage < 13:
                last_fin = ssq
                continue
            sttoks = []
            for p in range(16):
                rhs3 = zb.rearrange("q (a b t) -> q a b t", a=2, b=16)[:, :, p, :]
                sttoks.append(P.t("matmul", psX[0:2, 32 * p:32 * (p + 1)], onesblk, rhs3,
                                  deps=([ssq, eat, l4] if p == 0 else []),
                                  start=(p == 0), stop=(p == 15)))
            est = P.s("copy", statsS, psX[0:2, :], deps=sttoks)
            if stage < 14:
                last_fin = est
                continue
            muA = statsS.rearrange("q (a b t) -> q a b t", a=16, b=2)[:, :, 0, :]
            sqA = statsS.rearrange("q (a b t) -> q a b t", a=16, b=2)[:, :, 1, :]
            vmu = P.v("tensor_scalar", rows[:, 0:256].rearrange("q (a t) -> q a t", a=16),
                      muA, 1.0 / 64, None, ALU.mult, deps=[est])
            vm2 = P.v("tensor_mul", rows[:, 256:512], rows[:, 0:256], rows[:, 0:256],
                      deps=[vmu])
            vva = P.v("scalar_tensor_tensor",
                      rows[:, 256:512].rearrange("q (a t) -> q a t", a=16),
                      sqA, 1.0 / 64, rows[:, 256:512].rearrange("q (a t) -> q a t", a=16),
                      ALU.mult, ALU.subtract, deps=[vm2])
            sln = P.s("activation", rows[:, 256:512], rows[:, 256:512], AF.Ln,
                      bias=TTT_EPS, deps=[vva])
            srs = P.s("activation", rows[:, 256:512], rows[:, 256:512], AF.Exp,
                      scale=-0.5, deps=[sln])
            vnr = P.v("tensor_scalar", rows[:, 0:256], rows[:, 0:256], -1.0, None,
                      ALU.mult, deps=[srs])
            if stage < 15:
                last_fin = vnr
                continue
            bctoks = []
            for p in range(16):
                rhs3 = rows.rearrange("q (c a t) -> q c a t", c=2, a=16)[:, :, p, :]
                bctoks.append(P.t("matmul", psX[:, 32 * p:32 * (p + 1)], onesblkT, rhs3,
                                  deps=([vnr, est, l4b] if p == 0 else []),
                                  start=(p == 0), stop=(p == 15)))
            nrb = psX.rearrange("p (a c t) -> p a c t", a=16, c=2)[:, :, 0, :]
            rstdb = psX.rearrange("p (a c t) -> p a c t", a=16, c=2)[:, :, 1, :]
            vx1 = P.v("tensor_add", pview(xht), pview(zb[:, 0:256]), nrb, deps=bctoks)
            vx2 = P.v("tensor_mul", pview(xh), pview(xht), rstdb, deps=[vx1])
            if stage < 16:
                last_fin = vx2
                continue
            g2bc = g2b.unsqueeze(2).broadcast_to([128, 16, 16])
            vg1 = P.v("tensor_mul", pview(c1b), pview(xh), g2bc, deps=[vx2, vg2])
            bsl = bmtpb.rearrange("p (k t) -> p k t", k=16)[:, :, tc]
            vg1b = P.v("tensor_add", pview(gx[:, 0:256]), pview(c1b), bsl, deps=[vg1])
            vg3 = P.v("tensor_mul", gx[:, 256:512], gx[:, 0:256], xh, deps=[vg1b])
            if stage < 17:
                last_fin = vg3
                continue
            s2toks = []
            for p in range(16):
                rhs3 = gx.rearrange("q (a b t) -> q a b t", a=2, b=16)[:, :, p, :]
                s2toks.append(P.t("matmul", psA[0:2, 32 * p:32 * (p + 1)], onesblk, rhs3,
                                  deps=([vg3, vz1] if p == 0 else []),
                                  start=(p == 0), stop=(p == 15)))
            es2 = P.s("activation", statsS, psA[0:2, :], AF.Copy, scale=1.0 / 64,
                      deps=s2toks + [vmu, vva])
            b2toks = []
            for p in range(16):
                rhs3 = statsS.rearrange("q (a b t) -> q a b t", a=16, b=2)[:, p, :, :]
                b2toks.append(P.t("matmul", psA[:, 32 * p:32 * (p + 1)], onesblkT, rhs3,
                                  deps=([es2] if p == 0 else []), start=(p == 0), stop=(p == 15)))
            s1b = psA.rearrange("p (a c t) -> p a c t", a=16, c=2)[:, :, 0, :]
            s2b = psA.rearrange("p (a c t) -> p a c t", a=16, c=2)[:, :, 1, :]
            if stage < 18:
                last_fin = b2toks[-1]
                continue
            vc1 = P.v("tensor_sub", pview(c1b), s1b, pview(gx[:, 0:256]), deps=b2toks)
            vc2 = P.v("tensor_mul", pview(xht), pview(xh), s2b, deps=b2toks + [vx2])
            vc3 = P.v("tensor_add", c1b, c1b, xht, deps=[vc1, vc2])
            vgr = P.v("tensor_mul", pview(gradT), pview(c1b), rstdb, deps=[vc3])
            if stage < 2:
                last_fin = vgr
                shadow_tok = shadow_tok
                continue
            trdeps = [vgr]
            evt = []
            for half in range(2):
                trs = []
                for pp in range(8):
                    p = 8 * half + pp
                    trs.append(P.t("matmul", psT[0:16, 128 * pp:128 * (pp + 1)],
                                   gradT[:, 16 * p:16 * (p + 1)], identb,
                                   deps=(trdeps if pp == 0 else []),
                                   start=(pp == 0), stop=(pp == 7), is_transpose=True))
                pv = psT.rearrange("q (a c f) -> q a c f", a=8, c=2)
                hsl = slice(1024 * half, 1024 * (half + 1))
                e1 = P.s("copy",
                         gradN3[0:16, hsl].rearrange("q (a c f) -> q a c f", a=8, c=2)[:, :, 0, :],
                         pv[:, :, 0, :], deps=trs + [mz1])
                e2 = P.s("copy",
                         gradN3[32:48, hsl].rearrange("q (a c f) -> q a c f", a=8, c=2)[:, :, 1, :],
                         pv[:, :, 1, :], deps=trs)
                evt += [e1, e2]
                trdeps = [e2]
            kn_evt = []
            for half in range(2):
                trs = []
                for pp in range(8):
                    p = 8 * half + pp
                    trs.append(P.t("matmul", psT[0:16, 128 * pp:128 * (pp + 1)],
                                   qsl(ktrb, p), identb,
                                   deps=(trdeps if pp == 0 else []),
                                   start=(pp == 0), stop=(pp == 7), is_transpose=True))
                pv = psT.rearrange("q (a c f) -> q a c f", a=8, c=2)
                hsl = slice(1024 * half, 1024 * (half + 1))
                e1 = P.s("activation",
                         knS3[0:16, hsl].rearrange("q (a c f) -> q a c f", a=8, c=2)[:, :, 0, :],
                         pv[:, :, 0, :], AF.Copy, scale=LE, deps=trs + [mz2])
                e2 = P.s("activation",
                         knS3[32:48, hsl].rearrange("q (a c f) -> q a c f", a=8, c=2)[:, :, 1, :],
                         pv[:, :, 1, :], AF.Copy, scale=LE, deps=trs)
                kn_evt += [e1, e2]
                trdeps = [e2]
            if stage < 3:
                last_fin = kn_evt[-1]
                continue
            zctoks = []
            wutoks = []
            for p in range(16):
                gsl = gradN3[0:48, 128 * p:128 * (p + 1)]
                asl = app2[0:48, 16 * p:16 * (p + 1)]
                zctoks.append(P.t("matmul", psB[0:128, 16 * p:16 * (p + 1)],
                                  gsl, asl,
                                  deps=([evt[-1], capA, capB] if p == 0 else []),
                                  start=False, stop=(p == 15)))
                ksl = knS3[0:48, 128 * p:128 * (p + 1)]
                wutoks.append(P.t("matmul", psW[0:128, 128 * p:128 * (p + 1)],
                                  ksl, gsl,
                                  deps=([kn_evt[-1], wprev] if p == 0 else []),
                                  start=True, stop=True))
            if stage < 4:
                vwu = P.v("tensor_add", w1f, w1f, psW, deps=wutoks + [shadow_tok])
                wprev = vwu
                shadow_tok = P.s("copy", W1b, w1f, deps=[vwu] + m1toks)
                last_fin = shadow_tok
                continue
            vz2 = P.v("tensor_add", pview(lnb[:, 0:256]), pview(psB[:, 0:256]), b1bc,
                      deps=zctoks + [b1_tok, vgr])
            ssq2 = P.s("activation", lnb[:, 256:512], lnb[:, 0:256], AF.Square, deps=[vz2])
            st2 = []
            for p in range(16):
                rhs3 = lnb.rearrange("q (a b t) -> q a b t", a=2, b=16)[:, :, p, :]
                st2.append(P.t("matmul", psX[0:2, 32 * p:32 * (p + 1)], onesblk, rhs3,
                               deps=([ssq2, vgr, vx1] if p == 0 else []),
                               start=(p == 0), stop=(p == 15)))
            es3 = P.s("copy", statsS, psX[0:2, :], deps=st2 + [vc1, vc2])
            vmu2 = P.v("tensor_scalar", rows[:, 0:256].rearrange("q (a t) -> q a t", a=16),
                       statsS.rearrange("q (a b t) -> q a b t", a=16, b=2)[:, :, 0, :],
                       1.0 / 64, None, ALU.mult, deps=[es3, vnr])
            vm22 = P.v("tensor_mul", rows[:, 256:512], rows[:, 0:256], rows[:, 0:256],
                       deps=[vmu2])
            vva2 = P.v("scalar_tensor_tensor",
                       rows[:, 256:512].rearrange("q (a t) -> q a t", a=16),
                       statsS.rearrange("q (a b t) -> q a b t", a=16, b=2)[:, :, 1, :],
                       1.0 / 64, rows[:, 256:512].rearrange("q (a t) -> q a t", a=16),
                       ALU.mult, ALU.subtract, deps=[vm22])
            sln2 = P.s("activation", rows[:, 256:512], rows[:, 256:512], AF.Ln,
                       bias=TTT_EPS, deps=[vva2])
            srs2 = P.s("activation", rows[:, 256:512], rows[:, 256:512], AF.Exp,
                       scale=-0.5, deps=[sln2])
            vq1 = P.v("scalar_tensor_tensor", rows2[0:2, 0:256], rows[:, 0:256], -1.0,
                      rows[:, 256:512], ALU.mult, ALU.mult, deps=[srs2, vr0, vr1])
            vq2 = P.v("tensor_copy", rows2[0:2, 256:512], rows[:, 256:512], deps=[srs2])
            b3toks = []
            for p in range(16):
                rhs3 = rows2.rearrange("q (c a t) -> q c a t", c=2, a=16)[:, :, p, :]
                b3toks.append(P.t("matmul", psX[:, 32 * p:32 * (p + 1)],
                                  gbb3[:, 128 * p:128 * (p + 1)], rhs3,
                                  deps=([vq1, vq2, es3, l5] if p == 0 else []),
                                  start=(p == 0), stop=(p == 15)))
            qb = psX.rearrange("p (a c t) -> p a c t", a=16, c=2)[:, :, 0, :]
            pb = psX.rearrange("p (a c t) -> p a c t", a=16, c=2)[:, :, 1, :]
            vfl = P.v("tensor_mul", pview(lnt), pview(lnb[:, 0:256]), pb, deps=b3toks)
            vf2 = P.v("tensor_add", pview(lnt), pview(lnt), qb, deps=[vfl])
            qv = qtrb.rearrange("p (k t) -> p k t", k=16)[:, :, tc]
            vfin = P.v("tensor_add", qv, pview(lnt), qv, deps=[vf2])
            vbr = P.v("reduce_sum", b1red, pview(gradT), axis=AX.X, deps=[vgr, vz2])
            b1_tok = P.v("scalar_tensor_tensor", b1T, b1red, LE, b1T, ALU.mult, ALU.add,
                         deps=[vbr, vz1, vz2])
            vwu = P.v("tensor_add", w1f, w1f, psW, deps=wutoks + [shadow_tok])
            wprev = vwu
            shadow_tok = P.s("copy", W1b, w1f, deps=[vwu] + m1toks)
            last_fin = vfin
        if "tout" in _skip:
            st = P.dma("q", d["tout"][0:128, 0:16], qtrb[:, 0:16], deps=[last_fin], sem="stout")
        else:
            st = P.dma("q", d["tout"][:, bs].rearrange("(k p) t -> p k t", p=128),
                       qtrb.rearrange("p (k t) -> p k t", k=16),
                       deps=[last_fin], sem="stout")
        prev_store = st
        out_toks.append(st)
    if "w1dbg" in d and "dbg" not in _skip:
        out_toks.append(P.dma("q", d["w1dbg"], W1b, deps=[shadow_tok], sem="sdbg"))
        out_toks.append(P.dma("q", d["b1dbg"], b1T, deps=[b1_tok], sem="sdbg"))
    dbgbufs = dict(zb=zb, xh=xh, gx=gx, c1b=c1b, gradT=gradT, gradN3=gradN3,
                   knS3=knS3, app2=app2, attnTs=attnTs, app=app, lnb=lnb, lnt=lnt,
                   statsS=statsS, rows=rows, rows2=rows2, b1red=b1red, w1f=w1f)
    for nm, buf in dbgbufs.items():
        if "dbg_" + nm in d:
            out_toks.append(P.dma("q", d["dbg_" + nm], buf, deps=[shadow_tok, b1_tok],
                                  sem="sdbg"))
    return out_toks





def build_ph3a(nc, P, d):
    sb = lambda n, s, t: nc.alloc_sbuf_tensor("p3a_" + n, s, t).ap()
    ps = lambda n, s, t: nc.alloc_psum_tensor("p3a_" + n, s, t).ap()
    wsb = sb("wsb", [128, 16 * D], BF16)
    tsl = sb("tsl", [128, 16 * 512], BF16)
    ev1 = sb("ev1", [128, 512], BF16)
    ev2 = sb("ev2", [128, 512], BF16)
    psm = [ps("psm0", [128, 512], F32), ps("psm1", [128, 512], F32)]

    wtoks = []
    for q in range(4):
        src = d["wo"][512 * q:512 * (q + 1), :].rearrange("(k p) o -> p k o", p=128)
        dst = wsb[:, D * 4 * q:D * 4 * (q + 1)].rearrange("p (k o) -> p k o", k=4)
        wtoks.append(P.dma("p", dst, src, sem="w" + str(q)))
    outs = []
    prev_st = {0: [], 1: []}
    prev_lt = []
    for nb in range(4):
        lt = P.dma("q", tsl.rearrange("p (k t) -> p k t", k=16),
                   d["tout"][:, 512 * nb:512 * (nb + 1)].rearrange("(k p) t -> p k t", p=128),
                   deps=prev_lt, sem="stsl")
        prev_lt = []
        for m in range(16):
            par = m % 2
            pp = psm[par]
            evb = ev1 if par == 0 else ev2
            mms = []
            for k in range(16):
                mms.append(P.t("matmul", pp, wsb[:, D * k + 128 * m:D * k + 128 * (m + 1)],
                               tsl[:, 512 * k:512 * (k + 1)],
                               deps=(wtoks + [lt] + prev_st[par] if k == 0 else []),
                               **mmf(k, 16)))
            e = P.s("copy", evb, pp, deps=mms + prev_st[par])
            st = P.dma("q", d["brT"][128 * m:128 * (m + 1), 512 * nb:512 * (nb + 1)], evb,
                       deps=[e], sem="sbr" + str(par))
            prev_st[par] = [st]
            prev_lt.append(st)
            outs.append(st)
    return outs


def build_ph3b(nc, P, d):
    sb = lambda n, s, t: nc.alloc_sbuf_tensor("p3b_" + n, s, t).ap()
    ps = lambda n, s, t: nc.alloc_psum_tensor("p3b_" + n, s, t).ap()
    wtT = sb("wtT", [128, 16 * D], BF16)
    wld = [sb("wld0", [128, D], F32), sb("wld1", [128, D], F32)]
    wq1 = sb("wq1", [128, D], F32)
    wq2 = sb("wq2", [128, D], F32)
    wqb = sb("wqb", [128, D], BF16)
    absn = sb("absn", [128, 16], F32)
    acol = sb("acol", [128, 1], F32)
    rscol = sb("rscol", [128, 1], F32)
    sccol = sb("sccol", [128, 1], F32)
    sval = sb("sval", [1, 1], F32)
    svs = sb("svs", [1, 1], F32)
    onesc = sb("onesc", [128, 1], F32)
    ones128 = sb("ones128", [1, 128], F32)
    xslc = sb("xslc", [128, 16 * 512], BF16)
    brs = sb("brs", [128, 512], BF16)
    trb = sb("trb", [128, 512], BF16)
    bfull = sb("bfull", [128, 512], F32)
    x1buf = [sb("x1b" + str(i), [128, D], F32) for i in range(4)]
    x0t = sb("x0t", [128, D], F32)
    x1t = sb("x1t", [128, D], F32)
    x1n = sb("x1n", [128, D], BF16)
    fbcol = sb("fbcol", [128, 16], F32)
    stats = sb("stats", [128, 8], F32)
    sqjunk = sb("sqjunk", [128, D], F32)
    identb = sb("identb", [128, 128], BF16)
    identf = sb("identf", [128, 128], F32)
    ps_m = ps("ps_m", [128, 512], F32)
    ps_s = ps("ps_s", [128, 512], F32)
    ps_t = ps("ps_t", [128, 512], F32)
    ps_tb = ps("ps_tb", [128, 512], BF16)

    lI = P.dma("q", identb, d["ident"], sem="sc")
    lIf = P.dma("q", identf, d["identf"], sem="sc")
    lfb = P.dma("q", fbcol, d["fbias"].rearrange("(k p) o -> p (k o)", p=128), sem="sc",
                allow_slow_non_contiguous=True)
    lo = P.dma("q", ones128, d["ones128"], sem="sc")
    oz = P.v("memset", onesc, 1.0)
    abst = []
    for i in range(16):
        buf = wld[i % 2]
        war = [abst[i - 2]] if i >= 2 else []
        lw = P.dma("q", buf, d["fw"][128 * i:128 * (i + 1), :], deps=war, sem="sw" + str(i % 2))
        abst.append(P.s("activation", sqjunk, buf, AF.Abs, accum_out=absn[:, i:i + 1],
                        deps=[lw] + ([abst[i - 1]] if i else [])))
    r1 = P.v("reduce_sum", acol, absn, axis=AX.X, deps=abst)
    mm1 = P.t("matmul", ps_s[0:1, 0:1], onesc, acol, deps=[r1, oz], start=True, stop=True)
    e1 = P.s("activation", sval, ps_s[0:1, 0:1], AF.Copy, scale=1.0 / (D * D), deps=[mm1])
    e5 = P.s("activation", svs, ps_s[0:1, 0:1], AF.Copy, scale=1.0 / (D * D), deps=[mm1])
    e2 = P.v("tensor_scalar", sval, sval, 1e-10, None, ALU.add, deps=[e1])
    e3 = P.v("reciprocal", sval, sval, deps=[e2])
    mm2 = P.t("matmul", ps_s[0:128, 4:5], ones128, sval, deps=[e3, lo, e1, e5], start=True,
              stop=True)
    eb = P.v("tensor_copy", rscol, ps_s[0:128, 4:5], deps=[mm2])
    mm3 = P.t("matmul", ps_s[0:128, 6:7], ones128, svs, deps=[e5, mm2, eb], start=True, stop=True)
    eb2 = P.v("tensor_copy", sccol, ps_s[0:128, 6:7], deps=[mm3])
    trdone = []
    for i in range(16):
        buf = wld[i % 2]
        war = [trdone[i - 2]] if i >= 2 else [abst[-1]]
        lw = P.dma("q", buf, d["fw"][128 * i:128 * (i + 1), :], deps=war, sem="sw" + str(i % 2))
        q1 = P.v("tensor_scalar", wq1, buf, rscol, None, ALU.mult,
                 deps=[lw, eb] + ([trdone[i - 1]] if i else []))
        q2 = P.v("tensor_scalar", wq2, wq1, 0.5, None, ALU.is_gt, deps=[q1])
        q3 = P.v("tensor_scalar", wq1, wq1, -0.5, None, ALU.is_lt, deps=[q2])
        q4 = P.v("tensor_sub", wqb, wq2, wq1, deps=[q3])
        evs = []
        for j4 in range(4):
            trs = []
            for jj in range(4):
                j = 4 * j4 + jj
                war2 = [evs[-1]] if evs else ([trdone[i - 1]] if i else [])
                trs.append(P.t("matmul", ps_tb[:, 128 * jj:128 * (jj + 1)],
                               wqb[:, 128 * j:128 * (j + 1)], identb,
                               deps=[q4, lI] + war2, start=(jj == 0), stop=(jj == 3),
                               is_transpose=True))
            dv = wtT.rearrange("p (k o) -> p k o", k=16)[:, 4 * j4:4 * j4 + 4,
                                                        128 * i:128 * (i + 1)]
            evs.append(P.v("tensor_copy", dv, ps_tb.rearrange("p (k o) -> p k o", k=4),
                           deps=trs))
        trdone.append(evs[-1])
    outs = []
    prev = {}
    for nb in range(4):
        lx = P.dma("q", xslc.rearrange("p (k t) -> p k t", k=16),
                   d["xnt"].rearrange("p (k t) -> p k t", k=16)[:, :, 512 * nb:512 * (nb + 1)],
                   deps=prev.get("m", []), sem="sxs")
        for m in range(16):
            mms = []
            for k in range(16):
                mms.append(P.t("matmul", ps_m, wtT[:, D * k + 128 * m:D * k + 128 * (m + 1)],
                               xslc[:, 512 * k:512 * (k + 1)],
                               deps=([trdone[-1], lx] + prev.get("m", []) if k == 0 else []),
                               **mmf(k, 16)))
            lb = P.dma("q", brs, d["brT"][128 * m:128 * (m + 1), 512 * nb:512 * (nb + 1)],
                       deps=prev.get("b", []), sem="sbrl")
            c1 = P.v("scalar_tensor_tensor", bfull, ps_m, sccol, brs, ALU.mult, ALU.add,
                     deps=mms + [lb, eb2])
            c2 = P.v("tensor_scalar", bfull, bfull, fbcol[:, m:m + 1], None, ALU.add,
                     deps=[c1, lfb])
            prev["m"] = [c2]
            prev["b"] = [c2]
            lastav = prev.get("lastav", [])
            for tt in range(4):
                trt = P.t("matmul", ps_t[:, 128 * tt:128 * (tt + 1)],
                          bfull[:, 128 * tt:128 * (tt + 1)], identf,
                          deps=[c2, lIf] + prev.get("t" + str(tt), []) + lastav,
                          start=True, stop=True, is_transpose=True)
                av = P.v("tensor_copy", x1buf[tt][:, 128 * m:128 * (m + 1)],
                         ps_t[:, 128 * tt:128 * (tt + 1)], deps=[trt])
                lastav = [av]
                prev["t" + str(tt)] = [av]
            prev["lastav"] = lastav
        for tt in range(4):
            i = 4 * nb + tt
            lx0 = P.dma("q", x0t, d["x0"][128 * i:128 * (i + 1), :],
                        deps=prev.get("x0", []), sem="sx0")
            a1 = P.v("tensor_add", x1t, x1buf[tt], x0t,
                     deps=[lx0] + prev["t" + str(tt)] + prev.get("x1w", []))
            prev["x0"] = [a1]
            prev["t" + str(tt)].append(a1)
            st1 = P.dma("q", d["x1"][128 * i:128 * (i + 1), :], x1t, deps=[a1], sem="sx1")
            t_s = P.v("reduce_sum", stats[:, 0:1], x1t, axis=AX.X,
                      deps=[a1] + prev.get("stat", []))
            t_sq = P.s("activation", sqjunk, x1t, AF.Square, accum_out=stats[:, 1:2],
                       deps=[a1] + prev.get("sq", []))
            t_mu = P.v("tensor_scalar", stats[:, 2:3], stats[:, 0:1], 1.0 / D, None,
                       ALU.mult, deps=[t_s])
            t_m2 = P.v("tensor_mul", stats[:, 3:4], stats[:, 2:3], stats[:, 2:3], deps=[t_mu])
            t_va = P.v("scalar_tensor_tensor", stats[:, 4:5], stats[:, 1:2], 1.0 / D,
                       stats[:, 3:4], ALU.mult, ALU.subtract, deps=[t_sq, t_m2])
            t_ln = P.s("activation", stats[:, 5:6], stats[:, 4:5], AF.Ln, bias=LN_EPS,
                       deps=[t_va])
            t_rs = P.s("activation", stats[:, 6:7], stats[:, 5:6], AF.Exp, scale=-0.5,
                       deps=[t_ln])
            t_xn = P.v("tensor_scalar", x1n, x1t, stats[:, 2:3], stats[:, 6:7],
                       ALU.subtract, ALU.mult, deps=[t_rs, st1])
            prev["x1w"] = [t_xn]
            prev["sq"] = [t_sq]
            prev["stat"] = [t_xn]
            for j4 in range(4):
                trs = []
                for jj in range(4):
                    j = 4 * j4 + jj
                    trs.append(P.t("matmul", ps_tb[:, 128 * jj:128 * (jj + 1)],
                                   x1n[:, 128 * j:128 * (j + 1)], identb,
                                   deps=[t_xn] + prev.get("xtr", []), start=(jj == 0),
                                   stop=(jj == 3), is_transpose=True))
                bb2 = P.v("tensor_copy", trb.rearrange("p (f t) -> p f t", f=4),
                          ps_tb.rearrange("p (f t) -> p f t", f=4),
                          deps=trs + prev.get("xtrw", []))
                std = P.dma("q",
                            d["xn1t"].rearrange("p (k t) -> p k t", k=16)[:, 4 * j4:4 * j4 + 4,
                                                                          128 * i:128 * (i + 1)],
                            trb.rearrange("p (f t) -> p f t", f=4), deps=[bb2], sem="sxnw")
                prev["xtr"] = [bb2]
                prev["xtrw"] = [std]
                outs.append(std)
            outs.append(st1)
    return outs


def build_ph4(nc, P, d):
    sb = lambda n, s, t: nc.alloc_sbuf_tensor("p4_" + n, s, t).ap()
    ps = lambda n, s, t: nc.alloc_psum_tensor("p4_" + n, s, t).ap()
    xn1t = sb("xn1t", [128, 16 * L], BF16)
    xcb = sb("xcb", [128, L], BF16)
    xcsl = [sb("xcsl0", [128, 512], BF16), sb("xcsl1", [128, 512], BF16)]
    wldA = sb("wldA", [128, D], F32)
    wldB = sb("wldB", [128, D], F32)
    wcolA = sb("wcolA", [128, 2048], BF16)
    wcolB = sb("wcolB", [128, 2048], BF16)
    wbf = sb("wbf", [128, D], BF16)
    hsb = sb("hsb", [128, L], F32)
    cacc = sb("cacc", [128, L], F32)
    xpwT = sb("xpwT", [128, 32 * 160], BF16)
    xpld = sb("xpld", [128, 4096], F32)
    wrT = sb("wrT", [128, 16 * 16], BF16)
    wrld = sb("wrld", [16, D], F32)
    brcol = sb("brcol", [16, 1], F32)
    cwcol = sb("cwcol", [128, 32 * 4], F32)
    cbcol = sb("cbcol", [128, 32], F32)
    lgT = sb("lgT", [16, L], F32)
    lgN = sb("lgN", [128, 16], F32)
    lgW = sb("lgW", [128, 16], F32)
    mx = sb("mx", [128, 1], F32)
    msk = sb("msk", [128, 16], F32)
    mskT = sb("mskT", [16, L], F32)
    identb = sb("identb", [128, 128], BF16)
    identf = sb("identf", [128, 128], F32)
    tsev = sb("tsev", [128, 512], BF16)
    bcev = sb("bcev", [32, 512], F32)
    ps_m = [ps("ps_m0", [128, 512], F32), ps("ps_m1", [128, 512], F32)]
    ps_r = ps("ps_r", [128, 512], F32)
    ps_tb = ps("ps_tb", [128, 512], BF16)
    ps_ts = ps("ps_ts", [128, 512], F32)
    ps_bc = ps("ps_bc", [32, 512], F32)

    lI = P.dma("q", identb, d["ident"], sem="sc")
    lIf = P.dma("q", identf, d["identf"], sem="sc")
    lxn = P.dma("q", xn1t, d["xn1t"], sem="sxn")
    lbr = P.dma("q", brcol, d["br"], sem="sc")
    lcw = P.dma("q", cwcol.rearrange("p (k c) -> p k c", k=32),
                d["convw"].rearrange("(k p) c -> p k c", p=128), sem="sc")
    lcb = P.dma("q", cbcol, d["convb"].rearrange("(k p) o -> p (k o)", p=128), sem="sc",
                allow_slow_non_contiguous=True)

    lwr = P.dma("q", wrld, d["wr"], sem="sc")
    wrbf = sb("wrbf", [16, D], BF16)
    cwr = P.v("tensor_copy", wrbf, wrld, deps=[lwr])
    wr_tr = []
    for k in range(16):
        t1 = P.t("matmul", ps_tb[:, 0:16], wrbf[0:16, 128 * k:128 * (k + 1)],
                 identb[0:16, 0:16],
                 deps=[cwr, lI] + wr_tr[-1:], start=True, stop=True, is_transpose=True)
        wr_tr.append(P.v("tensor_copy", wrT[:, 16 * k:16 * (k + 1)], ps_tb[:, 0:16],
                         deps=[t1]))
    mtoks = []
    for nb in range(4):
        mms = []
        for k in range(16):
            mms.append(P.t("matmul", ps_r[0:16, 0:512], wrT[:, 16 * k:16 * (k + 1)],
                           xn1t[:, L * k + 512 * nb:L * k + 512 * (nb + 1)],
                           deps=(wr_tr + [lxn] + mtoks[-1:] if k == 0 else []),
                           **mmf(k, 16)))
        mtoks.append(P.v("tensor_scalar", lgT[:, 512 * nb:512 * (nb + 1)],
                         ps_r[0:16, 0:512], brcol, None, ALU.add, deps=mms + [lbr]))
    mprev = mtoks[-1:]
    mskb = sb("mskb", [128, 16], BF16)
    for i in range(16):
        t1 = P.t("matmul", ps_ts[:, 0:16], lgT[0:16, 128 * i:128 * (i + 1)],
                 identf[0:16, 0:16],
                 deps=mprev + [lIf], start=True, stop=True, is_transpose=True)
        c1 = P.v("tensor_copy", lgN, ps_ts[:, 0:16], deps=[t1] + mprev)
        c2 = P.v("tensor_copy", lgW, lgN, deps=[c1])
        rprev = [c2]
        for it in range(3):
            r1 = P.v("reduce_max", mx, lgW, axis=AX.X, deps=rprev)
            r2 = P.v("tensor_scalar", msk, lgW, mx, None, ALU.is_ge, deps=[r1])
            r3 = P.v("scalar_tensor_tensor", lgW, msk, -1e30, lgW, ALU.mult, ALU.add,
                     deps=[r2])
            rprev = [r3]
        r4 = P.v("reduce_max", mx, lgW, axis=AX.X, deps=rprev)
        r5 = P.v("tensor_scalar", msk, lgN, mx, None, ALU.is_ge, deps=[r4, c2])
        cb = P.v("tensor_copy", mskb, msk, deps=[r5])
        t2 = P.t("matmul", ps_tb[0:16, 16:144], mskb, identb, deps=[cb] + mprev + [lI],
                 start=True, stop=True, is_transpose=True)
        mprev = [P.v("tensor_copy", mskT[0:16, 128 * i:128 * (i + 1)], ps_tb[0:16, 16:144],
                     deps=[t2])]
    smk = P.dma("q", d["maskt"], mskT, deps=mprev, sem="smk")

    lxp = P.dma("q", xpld, d["xpw"][0:128, :], sem="sxp")
    xpbf = sb("xpbf", [128, 4096], BF16)
    xpbf2 = sb("xpbf2", [32, 4096], BF16)
    xpld2 = sb("xpld2", [32, 4096], F32)
    cx1 = P.v("tensor_copy", xpbf, xpld, deps=[lxp])
    lxp2 = P.dma("q", xpld2, d["xpw"][128:160, :], sem="sxp2")
    cx2 = P.v("tensor_copy", xpbf2, xpld2, deps=[lxp2])
    xp_tr = []
    for k in range(32):
        t1 = P.t("matmul", ps_tb[:, 144:272], xpbf[:, 128 * k:128 * (k + 1)], identb,
                 deps=[cx1, lI] + (xp_tr[-1:] if xp_tr else mprev), start=True, stop=True,
                 is_transpose=True)
        e1 = P.v("tensor_copy", xpwT[:, 160 * k:160 * k + 128], ps_tb[:, 144:272], deps=[t1])
        t2 = P.t("matmul", ps_tb[:, 272:304], xpbf2[0:32, 128 * k:128 * (k + 1)],
                 identb[0:32, 0:32],
                 deps=[cx2, e1], start=True, stop=True, is_transpose=True)
        e2 = P.v("tensor_copy", xpwT[:, 160 * k + 128:160 * (k + 1)], ps_tb[:, 272:304],
                 deps=[t2])
        xp_tr.append(e2)

    outs = []
    wprev = {0: [], 1: []}
    cprev = []
    prev_trs = xp_tr[-1:]
    for m in range(64):
        par = m % 2
        wldb = wldA if par == 0 else wldB
        wcol = wcolA if par == 0 else wcolB
        lw = P.dma("q", wldb, d["ipw"][128 * m:128 * (m + 1), :], deps=wprev[par],
                   sem="swp" + str(par))
        cb_ = P.v("tensor_copy", wbf, wldb, deps=[lw] + prev_trs)
        trs = []
        for j4 in range(4):
            tt4 = []
            for jj in range(4):
                j = 4 * j4 + jj
                tt4.append(P.t("matmul", ps_tb[:, 128 * jj:128 * (jj + 1)],
                               wbf[:, 128 * j:128 * (j + 1)], identb,
                               deps=([cb_] + (trs[-1:] if trs else prev_trs) if jj == 0
                                     else []), start=(jj == 0),
                               stop=(jj == 3), is_transpose=True))
            dv = wcol.rearrange("p (k o) -> p k o", k=16)[:, 4 * j4:4 * j4 + 4, :]
            trs.append(P.v("tensor_copy", dv, ps_tb.rearrange("p (k o) -> p k o", k=4),
                           deps=tt4))
        prev_trs = [trs[-1]]
        mmdone = []
        for nb in range(4):
            pp = ps_m[nb % 2]
            mms = []
            for k in range(16):
                mms.append(P.t("matmul", pp, wcol[:, 128 * k:128 * (k + 1)],
                               xn1t[:, L * k + 512 * nb:L * k + 512 * (nb + 1)],
                               deps=(trs + [lxn] + wprev[nb % 2] if k == 0 else []),
                               **mmf(k, 16)))
            ev = P.s("copy", hsb[:, 512 * nb:512 * (nb + 1)], pp, deps=mms + cprev[-1:])
            wprev[nb % 2] = [ev]
            mmdone.append(ev)
        wprev[par] = wprev[par] + [trs[-1]]
        if m < 32:
            a0 = P.v("tensor_scalar", cacc, hsb, cwcol[:, 4 * m + 3:4 * m + 4], None,
                     ALU.mult, deps=mmdone + cprev[-1:])
            a1 = P.v("scalar_tensor_tensor", cacc[:, 1:L], hsb[:, 0:L - 1],
                     cwcol[:, 4 * m + 2:4 * m + 3], cacc[:, 1:L], ALU.mult, ALU.add,
                     deps=[a0, lcw])
            a2 = P.v("scalar_tensor_tensor", cacc[:, 2:L], hsb[:, 0:L - 2],
                     cwcol[:, 4 * m + 1:4 * m + 2], cacc[:, 2:L], ALU.mult, ALU.add,
                     deps=[a1])
            a3 = P.v("scalar_tensor_tensor", cacc[:, 3:L], hsb[:, 0:L - 3],
                     cwcol[:, 4 * m + 0:4 * m + 1], cacc[:, 3:L], ALU.mult, ALU.add,
                     deps=[a2])
            s1 = P.s("activation", cacc, cacc, AF.Silu, bias=cbcol[:, m:m + 1],
                     deps=[a3, lcb])
            st = P.dma("q", d["xct"][128 * m:128 * (m + 1), :], cacc, deps=[s1], sem="sxc")
            cc = P.v("tensor_copy", xcb, cacc, deps=[s1] + cprev[1:2])
            st2 = P.dma("q", d["xcbt"][128 * m:128 * (m + 1), :], xcb, deps=[cc], sem="sxcb")
            cprev = [st, st2]
            outs.append(st)
        else:
            s1 = P.s("activation", cacc, hsb, AF.Silu, deps=mmdone + cprev[-1:])
            st = P.dma("q", d["sgt"][128 * (m - 32):128 * (m - 31), :], cacc, deps=[s1],
                       sem="ssg")
            cprev = [st]
            outs.append(st)
    tprev = []
    lprev = {0: cprev[1:2], 1: cprev[1:2]}
    for nb in range(4):
        mms = []
        mms2 = []
        for k in range(32):
            par = k % 2
            lxs = P.dma("q", xcsl[par],
                        d["xcbt"][128 * k:128 * (k + 1), 512 * nb:512 * (nb + 1)],
                        deps=lprev[par], sem="sxl" + str(par))
            mms.append(P.t("matmul", ps_ts, xpwT[:, 160 * k:160 * k + 128],
                           xcsl[par],
                           deps=(xp_tr + [lxs] + tprev if k == 0 else [lxs]),
                           **mmf(k, 32)))
            mms2.append(P.t("matmul", ps_bc, xpwT[:, 160 * k + 128:160 * (k + 1)],
                            xcsl[par], deps=[], **mmf(k, 32)))
            lprev[par] = [mms2[-1]]
        e1 = P.s("copy", tsev, ps_ts, deps=mms + tprev)
        e2 = P.v("tensor_copy", bcev, ps_bc, deps=mms2 + tprev)
        st1 = P.dma("q", d["tst"][:, 512 * nb:512 * (nb + 1)], tsev, deps=[e1], sem="sts")
        st2 = P.dma("q", d["btct"][:, 512 * nb:512 * (nb + 1)], bcev, deps=[e2], sem="sbc")
        tprev = [st1, st2]
        outs += [st1, st2]
    outs.append(smk)
    return outs


def build_ph5(nc, P, d, TB=512):
    import os as _os
    _ntb = int(_os.environ.get("NTB5", "99"))
    _ndk = int(_os.environ.get("NDK5", "32"))
    _sk5 = _os.environ.get("SKIP5", "").split(",")
    sb = lambda n, s, t: nc.alloc_sbuf_tensor("p5_" + n, s, t).ap()
    ps = lambda n, s, t: nc.alloc_psum_tensor("p5_" + n, s, t).ap()
    NTB = L // TB
    tst = sb("tst", [128, L], BF16)
    dtwT = sb("dtwT", [128, 32 * 128], BF16)
    dtld = sb("dtld", [128, 128], F32)
    dtbf = sb("dtbf", [128, 128], BF16)
    dtbcol = sb("dtbcol", [128, 32], F32)
    acol = sb("acol", [128, 32 * 16], F32)
    aload = sb("aload", [128, 16], F32)
    dcol = sb("dcol", [128, 32], F32)
    btcr = sb("btcr", [1, 32 * 512], F32)
    ones128 = sb("ones128", [1, 128], F32)
    Bbc = sb("Bbc", [128, 16 * TB], BF16)
    Cbc = sb("Cbc", [128, 16 * TB], BF16)
    xc = [sb("xc0", [128, TB], F32), sb("xc1", [128, TB], F32)]
    sg = [sb("sg0", [128, TB], F32), sb("sg1", [128, TB], F32)]
    dt_ = sb("dt", [128, TB], F32)
    dtev = sb("dtev", [128, TB], F32)
    dtt = sb("dtt", [128, TB], F32)
    dtx = sb("dtx", [128, TB], F32)
    dA = sb("dA", [128, TB], F32)
    uu = sb("uu", [128, TB], F32)
    hh = sb("hh", [128, TB], F32)
    ytmp = sb("ytmp", [128, TB], F32)
    yacc = sb("yacc", [128, TB], F32)
    y3 = sb("y3", [128, TB], BF16)
    state = sb("state", [128, 32 * 16], F32)
    ps_dt = ps("ps_dt", [128, 512], F32)
    ps_bc = ps("ps_bc", [128, 512], F32)
    ps_tb5 = ps("ps_tb5", [128, 128], BF16)

    identb5 = sb("identb5", [128, 128], BF16)
    lI5 = P.dma("q", identb5, d["ident"], sem="sc5")
    l1 = P.dma("q", tst, d["tst"], sem="sc5")
    l2 = P.dma("q", dtbcol, d["dtb"].rearrange("(k p) o -> p (k o)", p=128), sem="sc5",
               allow_slow_non_contiguous=True)
    l3 = P.dma("q", dcol, d["dcol"].rearrange("(k p) o -> p (k o)", p=128), sem="sc5",
               allow_slow_non_contiguous=True)
    l5 = P.dma("q", ones128, d["ones128"], sem="sc5")
    zst = P.v("memset", state, 0.0)
    atoks = []
    if "acol" in _sk5:
        atoks.append(P.v("memset", acol, -1.0))
    else:
        for k in range(32):
            la = P.dma("q", aload, d["alog"][128 * k:128 * (k + 1), :],
                       deps=atoks[-1:], sem="sal")
            e1 = P.s("activation", acol[:, 16 * k:16 * (k + 1)], aload, AF.Exp,
                     deps=[la] + atoks[-1:])
            atoks.append(P.v("tensor_scalar", acol[:, 16 * k:16 * (k + 1)],
                             acol[:, 16 * k:16 * (k + 1)], -1.0, None, ALU.mult, deps=[e1]))
    dtoks = []
    if "dtw" in _sk5:
        dtoks.append(P.v("memset", dtwT, 0.01))
    for k in (range(32) if "dtw" not in _sk5 else []):
        ld = P.dma("q", dtld, d["dtw"][128 * k:128 * (k + 1), :], deps=dtoks[-1:],
                   sem="sdw")
        cf = P.v("tensor_copy", dtbf, dtld, deps=[ld] + dtoks[-1:])
        tr = P.t("matmul", ps_tb5, dtbf, identb5, deps=[cf, lI5] + dtoks[-1:],
                 start=True, stop=True, is_transpose=True)
        dtoks.append(P.v("tensor_copy", dtwT[:, 128 * k:128 * (k + 1)], ps_tb5,
                         deps=[tr]))
    prev = {}
    outs = []
    for tb in range(min(NTB, _ntb)):
        ts_ = slice(TB * tb, TB * (tb + 1))
        if "bcdma" in _sk5:
            cbt = P.v("memset", btcr, 0.1, deps=prev.get("bcrow", []))
        else:
            cbt = P.dma("q", btcr.rearrange("o (r t) -> o r t", r=32), d["btct"][:, ts_],
                        deps=prev.get("bcrow", []), sem="sbcr")
        prev["bcrow"] = []
        bctoks = []
        if "bc" in _sk5 or "bcmm" in _sk5:
            bctoks.append(P.v("memset", Bbc, 0.1, deps=[cbt]))
            bctoks.append(P.v("memset", Cbc, 0.1))
        for n in (range(16) if ("bc" not in _sk5 and "bcmm" not in _sk5) else []):
            for which, buf in ((0, Bbc), (1, Cbc)):
                ri = 16 * which + n
                row = btcr[0:1, TB * ri:TB * (ri + 1)]
                m1 = P.t("matmul", ps_bc[:, 0:TB], ones128, row,
                         deps=([cbt, l5] + prev.get("bc", []) if (n == 0 and which == 0)
                               else bctoks[-1:]),
                         start=True, stop=True)
                bctoks.append(P.s("copy", buf[:, TB * n:TB * (n + 1)], ps_bc[:, 0:TB],
                                  deps=[m1]))
        prev["bcrow"] = [bctoks[-1]]
        prev["bc"] = []
        for dk in range(_ndk):
            par = dk % 2
            lxc = P.dma("q", xc[par], d["xct"][128 * dk:128 * (dk + 1), ts_],
                        deps=prev.get("xc" + str(par), []), sem="sxc" + str(par))
            lsg = P.dma("q", sg[par], d["sgt"][128 * dk:128 * (dk + 1), ts_],
                        deps=prev.get("sg" + str(par), []), sem="ssg" + str(par))
            if "sp" in _sk5:
                v2 = P.v("memset", dt_, 0.4, deps=prev.get("dt_", []) + prev.get("dt", []))
                s1 = v2
                prev["dt"] = [v2]
                prev["dtt"] = [v2]
            else:
                mdt = P.t("matmul", ps_dt[:, 0:TB], dtwT[:, 128 * dk:128 * (dk + 1)],
                          tst[:, ts_], deps=[dtoks[-1], l1] + prev.get("dt", []),
                          start=True, stop=True)
                ev0 = P.v("tensor_copy", dtev, ps_dt[:, 0:TB],
                          deps=[mdt] + prev.get("dtev", []))
                dtp = dtev
                s1 = P.s("activation", dtt, dtp, AF.Abs,
                         bias=dtbcol[:, dk:dk + 1],
                         deps=[mdt, l2] + prev.get("dtt", []) + ([ev0] if ev0 else []))
                s2 = P.s("activation", dtt, dtt, AF.Exp, scale=-1.0, deps=[s1])
                s3 = P.s("activation", dtt, dtt, AF.Ln, bias=1.0, deps=[s2])
                v1 = P.v("tensor_scalar", dt_, dtp, dtbcol[:, dk:dk + 1], 0.0,
                         ALU.add, ALU.max, deps=[mdt] + prev.get("dt_", []) +
                         ([ev0] if ev0 else []))
                v2 = P.v("tensor_add", dt_, dt_, dtt, deps=[v1, s3])
                prev["dtev"] = [s1, v1]
                prev["dt"] = [v2, s1]
                prev["dtt"] = [v2]
            v3 = P.v("tensor_mul", dtx, dt_, xc[par], deps=[v2, lxc] +
                     prev.get("dtx", []))
            prev["dtx"] = []
            yz = prev.get("yacc", [])
            if "scan" in _sk5:
                vy = P.v("memset", yacc, 0.5, deps=yz + prev.get("hh", []))
                prev["hh"] = [vy]
            for n in (range(16) if "scan" not in _sk5 else []):
                if "dA" in _sk5:
                    sA = P.v("memset", dA, 0.9, deps=[v2] + prev.get("dA", []))
                else:
                    sA = P.s("activation", dA, dt_, AF.Exp,
                             scale=acol[:, 16 * dk + n:16 * dk + n + 1],
                             deps=[v2, atoks[-1]] + prev.get("dA", []))
                vu = P.v("tensor_mul", uu, dtx, Bbc[:, TB * n:TB * (n + 1)],
                         deps=[v3, bctoks[-1]] + prev.get("uu", []))
                if "hsc" in _sk5:
                    vh = P.v("tensor_add", hh, dA, uu, deps=[sA, vu] + prev.get("hh", []))
                else:
                    vh = P.v("tensor_tensor_scan", hh, dA, uu,
                             state[:, 16 * dk + n:16 * dk + n + 1], ALU.mult, ALU.add,
                             deps=[sA, vu] + prev.get("hh", []))
                if "stc" in _sk5:
                    vst = vh
                else:
                    vst = P.s("copy", state[:, 16 * dk + n:16 * dk + n + 1],
                              hh[:, TB - 1:TB], deps=[vh])
                prev["dA"] = [vh]
                prev["uu"] = [vh]
                if n == 0:
                    vy = P.v("tensor_mul", yacc, hh, Cbc[:, TB * n:TB * (n + 1)],
                             deps=[vh] + yz)
                else:
                    vy1 = P.v("tensor_mul", ytmp, hh, Cbc[:, TB * n:TB * (n + 1)],
                              deps=[vh] + prev.get("yt", []))
                    if "yadd" in _sk5:
                        vy = P.v("tensor_add", yacc, yacc, ytmp, deps=[vy1, vy])
                    else:
                        vy = P.p("tensor_add", yacc, yacc, ytmp, deps=[vy1, vy])
                    prev["yt"] = [vy]
                prev["hh"] = [vy, vst]
            if "y3c" in _sk5:
                w1 = P.v("memset", yacc, 0.2, deps=[vy, l3, lxc])
            else:
                w1 = P.v("scalar_tensor_tensor", yacc, xc[par], dcol[:, dk:dk + 1], yacc,
                         ALU.mult, ALU.add, deps=[vy, l3, lxc])
            w2 = P.v("tensor_mul", y3, yacc, sg[par], deps=[w1, lsg] +
                     prev.get("y3", []))
            st = P.dma("q", d["y3t"][128 * dk:128 * (dk + 1), ts_], y3, deps=[w2],
                       sem="sy3")
            prev["y3"] = [st]
            prev["yacc"] = [w2]
            prev["xc" + str(par)] = [v3, w1]
            prev["sg" + str(par)] = [w2]
            prev["dt_"] = [v3, sA]
            outs.append(st)
        prev["bc"] = [w2]
    return outs


def build_ph6a(nc, P, d):
    sb = lambda n, s, t: nc.alloc_sbuf_tensor("p6a_" + n, s, t).ap()
    ps = lambda n, s, t: nc.alloc_psum_tensor("p6a_" + n, s, t).ap()
    xn1t = sb("xn1t", [128, 16 * L], BF16)
    wld = [sb("wld0", [128, D], F32), sb("wld1", [128, D], F32)]
    wcol = [sb("wcol0", [128, D], BF16), sb("wcol1", [128, D], BF16)]
    wbf = sb("wbf", [128, D], BF16)
    mrow1 = sb("mrow1", [1, L], F32)
    maskp = sb("maskp", [128, L], BF16)
    gel = sb("gel", [128, 512], F32)
    hm = sb("hm", [128, 512], BF16)
    ones128 = sb("ones128", [1, 128], F32)
    identb = sb("identb", [128, 128], BF16)
    ps_m = [ps("ps_m0", [128, 512], F32), ps("ps_m1", [128, 512], F32)]
    ps_b = ps("ps_b", [128, 512], F32)
    ps_tb = ps("ps_tb", [128, 512], BF16)

    lI = P.dma("q", identb, d["ident"], sem="sc")
    lxn = P.dma("q", xn1t, d["xn1t"], sem="sxn")
    l5 = P.dma("q", ones128, d["ones128"], sem="sc")
    outs = []
    prev = {}
    mcur = [None]
    for m in range(64):
        par = m % 2
        blk = m // 4
        lw = P.dma("q", wld[par], d["w1f1"][128 * m:128 * (m + 1), :],
                   deps=prev.get("w" + str(par), []), sem="sw" + str(par))
        cb_ = P.v("tensor_copy", wbf, wld[par], deps=[lw] + prev.get("tr", []))
        trs = []
        for j4 in range(4):
            tt4 = []
            for jj in range(4):
                j = 4 * j4 + jj
                tt4.append(P.t("matmul", ps_tb[:, 128 * jj:128 * (jj + 1)],
                               wbf[:, 128 * j:128 * (j + 1)], identb,
                               deps=([cb_, lI] + (trs[-1:] if trs else prev.get("tr", []))
                                     if jj == 0 else []),
                               start=(jj == 0), stop=(jj == 3), is_transpose=True))
            dv = wcol[par].rearrange("p (k o) -> p k o", k=16)[:, 4 * j4:4 * j4 + 4, :]
            trs.append(P.v("tensor_copy", dv, ps_tb.rearrange("p (k o) -> p k o", k=4),
                           deps=tt4))
        prev["tr"] = [trs[-1]]
        prev["w" + str(par)] = [trs[-1]]
        if m % 4 == 0:
            lmk = P.dma("q", mrow1, d["maskt"][blk:blk + 1, :],
                        deps=prev.get("mk", []), sem="smk")
            mtk = prev.get("mk", [])
            for a_ in range(4):
                mm = P.t("matmul", ps_b, ones128, mrow1[0:1, 512 * a_:512 * (a_ + 1)],
                         deps=[l5, lmk] + mtk, start=True, stop=True)
                mtk = [P.s("copy", maskp[:, 512 * a_:512 * (a_ + 1)], ps_b, deps=[mm])]
            mcur[0] = mtk[0]
            prev["mk"] = [mcur[0]]
        for nb in range(4):
            pp = ps_m[nb % 2]
            mms = []
            for k in range(16):
                mms.append(P.t("matmul", pp, wcol[par][:, 128 * k:128 * (k + 1)],
                               xn1t[:, L * k + 512 * nb:L * k + 512 * (nb + 1)],
                               deps=(trs + [lxn] + prev.get("e" + str(nb % 2), [])
                                     if k == 0 else []), **mmf(k, 16)))
            g1 = P.s("activation", gel, pp, AF.Gelu, deps=mms + prev.get("gel", []))
            g2 = P.v("tensor_mul", hm, gel, maskp[:, 512 * nb:512 * (nb + 1)],
                     deps=[g1, mcur[0]] + prev.get("hm", []))
            st = P.dma("q", d["hmt"][128 * m:128 * (m + 1), 512 * nb:512 * (nb + 1)], hm,
                       deps=[g2], sem="shm")
            prev["e" + str(nb % 2)] = [g1]
            prev["gel"] = [g2]
            prev["hm"] = [st]
            outs.append(st)
    return outs


def build_ph6b(nc, P, d):
    sb = lambda n, s, t: nc.alloc_sbuf_tensor("p6b_" + n, s, t).ap()
    ps = lambda n, s, t: nc.alloc_psum_tensor("p6b_" + n, s, t).ap()
    w2all = sb("w2all", [128, 64 * 512], BF16)
    opall = sb("opall", [128, 32 * 512], BF16)
    wld = sb("wld", [128, 4 * 128], F32)
    wbf = sb("wbf", [128, 4 * 128], BF16)
    rsl = [sb("rsl0", [128, 512], BF16), sb("rsl1", [128, 512], BF16)]
    brsl = sb("brsl", [128, 512], F32)
    trbuf = sb("trbuf", [128, 512], F32)
    x1t = sb("x1t", [128, D], F32)
    brt = sb("brt", [128, D], F32)
    outt = sb("outt", [128, D], F32)
    identf = sb("identf", [128, 128], F32)
    identb = sb("identb", [128, 128], BF16)
    ps_a = [ps("ps_a" + str(i), [128, 512], F32) for i in range(4)]
    ps_t = ps("ps_t", [128, 512], F32)
    ps_tb = ps("ps_tb", [128, 512], BF16)

    lIf = P.dma("q", identf, d["identf"], sem="sc")
    lI = P.dma("q", identb, d["ident"], sem="sc")
    outs = []
    prev = {}
    for fg in range(4):
        fr = slice(512 * fg, 512 * (fg + 1))
        ptoks = []
        for src_t, nk, wall in ((d["w2f"], 64, w2all), (d["opw"], 32, opall)):
            for kk in range(nk):
                lw = P.dma("q", wld.rearrange("p (g c) -> p g c", g=4),
                           src_t[fr, 128 * kk:128 * (kk + 1)].rearrange(
                               "(g p) c -> p g c", p=128),
                           deps=prev.get("wld", []), sem="swl")
                cw = P.v("tensor_copy", wbf, wld, deps=[lw] + prev.get("wtr", []))
                trs = []
                for g in range(4):
                    trs.append(P.t("matmul", ps_tb[:, 128 * g:128 * (g + 1)],
                                   wbf[:, 128 * g:128 * (g + 1)], identb,
                                   deps=([cw, lI] + prev.get("wcp", []) if g == 0 else []),
                                   start=(g == 0), stop=(g == 3), is_transpose=True))
                cp = P.v("tensor_copy", wall[:, 512 * kk:512 * (kk + 1)], ps_tb, deps=trs)
                prev["wld"] = [cp]
                prev["wtr"] = [cp]
                prev["wcp"] = [cp]
        wready = prev["wcp"]
        for nb in range(4):
            tsl = slice(512 * nb, 512 * (nb + 1))
            mmtoks = {0: [], 1: [], 2: [], 3: []}
            for k in range(96):
                par = k % 2
                if k < 64:
                    lr = P.dma("q", rsl[par],
                               d["hmt"][128 * k:128 * (k + 1), tsl],
                               deps=prev.get("r" + str(par), []), sem="sr" + str(par))
                    wsl = w2all[:, 512 * k:512 * (k + 1)]
                else:
                    dk = k - 64
                    lr = P.dma("q", rsl[par],
                               d["y3t"][128 * dk:128 * (dk + 1), tsl],
                               deps=prev.get("r" + str(par), []), sem="sr" + str(par))
                    wsl = opall[:, 512 * dk:512 * (dk + 1)]
                ftoks = []
                for fi in range(4):
                    ftoks.append(P.t("matmul", ps_a[fi], wsl[:, 128 * fi:128 * (fi + 1)],
                                     rsl[par],
                                     deps=(wready + [lr] + prev.get("ev", [])
                                           if k == 0 and fi == 0 else ([lr] if fi == 0
                                                                       else [])),
                                     start=(k == 0), stop=(k == 95)))
                    mmtoks[fi] = ftoks[-1:]
                prev["r" + str(par)] = [ftoks[-1]]
            evs = []
            for fi in range(4):
                e1 = P.v("tensor_copy", brsl, ps_a[fi], deps=mmtoks[3] + prev.get("br", []))
                trs = []
                for u in range(4):
                    trs.append(P.t("matmul", ps_t[:, 128 * u:128 * (u + 1)],
                                   brsl[:, 128 * u:128 * (u + 1)], identf,
                                   deps=([e1, lIf] + prev.get("trr", []) if u == 0 else []),
                                   start=(u == 0), stop=(u == 3), is_transpose=True))
                e2 = P.v("tensor_copy", trbuf, ps_t, deps=trs + prev.get("trdma", []))
                prev["trr"] = [e2]
                prev["br"] = [e2]
                st = P.dma("q",
                           d["brn"][512 * nb:512 * (nb + 1),
                                    512 * fg + 128 * fi:512 * fg + 128 * (fi + 1)].rearrange(
                               "(u p) f -> p u f", p=128),
                           trbuf.rearrange("p (u f) -> p u f", u=4), deps=[e2], sem="sbn")
                prev["trdma"] = [st]
                evs.append(st)
            prev["ev"] = evs
        prev["wcp"] = prev["wcp"] + prev["ev"]
    fprev = prev["ev"]
    for i in range(16):
        l1 = P.dma("q", x1t, d["x1"][128 * i:128 * (i + 1), :],
                   deps=fprev if i == 0 else prev.get("f1", []), sem="sfx")
        l2 = P.dma("q", brt, d["brn"][128 * i:128 * (i + 1), :],
                   deps=fprev if i == 0 else prev.get("f2", []), sem="sfb")
        a1 = P.v("tensor_add", outt, x1t, brt, deps=[l1, l2] + prev.get("fo", []))
        st = P.dma("q", d["out"][128 * i:128 * (i + 1), :], outt, deps=[a1], sem="sfo")
        prev["f1"] = [a1]
        prev["f2"] = [a1]
        prev["fo"] = [st]
        outs.append(st)
    return outs


def _ph1_nc(consts):
    nc = bass.Bass("TRN2", target_bir_lowering=False, debug=False)
    d = {}
    d["x0"] = nc.dram_tensor("x0", [L, D], F32, kind="ExternalInput").ap()
    for w in ["wq", "wk", "wv"]:
        d[w] = nc.dram_tensor(w, [D, D], F32, kind="ExternalInput").ap()
    d["gcol"] = nc.dram_tensor("gcol", [D, 1], F32, kind="ExternalInput").ap()
    d["bcol"] = nc.dram_tensor("bcol", [D, 1], F32, kind="ExternalInput").ap()
    for cn in ["ropeC", "ropeS", "swapP", "ident"]:
        d[cn] = nc.dram_tensor(cn, list(consts[cn].shape), BF16, kind="ExternalInput").ap()
    d["xnt"] = nc.dram_tensor("xnt", [128, 16 * L], BF16, kind="ExternalOutput").ap()
    d["qtr"] = nc.dram_tensor("qtr", [D, L], BF16, kind="ExternalOutput").ap()
    d["ktr"] = nc.dram_tensor("ktr", [D, L], BF16, kind="ExternalOutput").ap()
    d["bmtp"] = nc.dram_tensor("bmtp", [D, L], F32, kind="ExternalOutput").ap()
    P = Prog(nc)
    P.init_consts([LN_EPS, -0.5])
    P.finish(build_ph1(nc, P, d))
    return nc


def _ph2_nc():
    nc = bass.Bass("TRN2", target_bir_lowering=False, debug=False)
    d = {}
    d["qtr"] = nc.dram_tensor("qtr", [D, L], BF16, kind="ExternalInput").ap()
    d["ktr"] = nc.dram_tensor("ktr", [D, L], BF16, kind="ExternalInput").ap()
    d["bmtp"] = nc.dram_tensor("bmtp", [D, L], F32, kind="ExternalInput").ap()
    d["gcol"] = nc.dram_tensor("gcol", [D, 1], F32, kind="ExternalInput").ap()
    d["w1blk"] = nc.dram_tensor("w1blk", [128, 2048], F32, kind="ExternalInput").ap()
    d["b1t0"] = nc.dram_tensor("b1t0", [128, 16], F32, kind="ExternalInput").ap()
    d["gbb3"] = nc.dram_tensor("gbb3", [3, 2048], F32, kind="ExternalInput").ap()
    d["emx"] = nc.dram_tensor("emx", [16, 512], BF16, kind="ExternalInput").ap()
    d["ones128"] = nc.dram_tensor("ones128", [1, 128], F32, kind="ExternalInput").ap()
    d["onesblk"] = nc.dram_tensor("onesblk", [128, 2], F32, kind="ExternalInput").ap()
    d["onesblkT"] = nc.dram_tensor("onesblkT", [2, 128], F32, kind="ExternalInput").ap()
    d["ident"] = nc.dram_tensor("ident", [128, 128], BF16, kind="ExternalInput").ap()
    d["tout"] = nc.dram_tensor("tout", [D, L], BF16, kind="ExternalOutput").ap()
    P = Prog(nc)
    P.init_consts([TTT_EPS, -0.5, 1.0 / 16, 1.0 / 64])
    P.finish(build_ph2(nc, P, d))
    return nc


def _ph3a_nc():
    nc = bass.Bass("TRN2", target_bir_lowering=False, debug=False)
    d = {}
    d["tout"] = nc.dram_tensor("tout", [D, L], BF16, kind="ExternalInput").ap()
    d["wo"] = nc.dram_tensor("wo", [D, D], F32, kind="ExternalInput").ap()
    d["brT"] = nc.dram_tensor("brT", [D, L], BF16, kind="ExternalOutput").ap()
    P = Prog(nc)
    P.finish(build_ph3a(nc, P, d))
    return nc


def _ph3b_nc():
    nc = bass.Bass("TRN2", target_bir_lowering=False, debug=False)
    d = {}
    d["fw"] = nc.dram_tensor("fw", [D, D], F32, kind="ExternalInput").ap()
    d["fbias"] = nc.dram_tensor("fbias", [D, 1], F32, kind="ExternalInput").ap()
    d["xnt"] = nc.dram_tensor("xnt", [128, 16 * L], BF16, kind="ExternalInput").ap()
    d["brT"] = nc.dram_tensor("brT", [D, L], BF16, kind="ExternalInput").ap()
    d["x0"] = nc.dram_tensor("x0", [L, D], F32, kind="ExternalInput").ap()
    d["ident"] = nc.dram_tensor("ident", [128, 128], BF16, kind="ExternalInput").ap()
    d["identf"] = nc.dram_tensor("identf", [128, 128], F32, kind="ExternalInput").ap()
    d["ones128"] = nc.dram_tensor("ones128", [1, 128], F32, kind="ExternalInput").ap()
    d["x1"] = nc.dram_tensor("x1", [L, D], F32, kind="ExternalOutput").ap()
    d["xn1t"] = nc.dram_tensor("xn1t", [128, 16 * L], BF16, kind="ExternalOutput").ap()
    P = Prog(nc)
    P.init_consts([LN_EPS, -0.5])
    P.finish(build_ph3b(nc, P, d))
    return nc


def _ph4_nc():
    nc = bass.Bass("TRN2", target_bir_lowering=False, debug=False)
    d = {}
    d["xn1t"] = nc.dram_tensor("xn1t", [128, 16 * L], BF16, kind="ExternalInput").ap()
    d["ipw"] = nc.dram_tensor("ipw", [8192, D], F32, kind="ExternalInput").ap()
    d["convw"] = nc.dram_tensor("convw", [4096, 4], F32, kind="ExternalInput").ap()
    d["convb"] = nc.dram_tensor("convb", [4096, 1], F32, kind="ExternalInput").ap()
    d["xpw"] = nc.dram_tensor("xpw", [160, 4096], F32, kind="ExternalInput").ap()
    d["wr"] = nc.dram_tensor("wr", [16, D], F32, kind="ExternalInput").ap()
    d["br"] = nc.dram_tensor("br", [16, 1], F32, kind="ExternalInput").ap()
    d["ident"] = nc.dram_tensor("ident", [128, 128], BF16, kind="ExternalInput").ap()
    d["identf"] = nc.dram_tensor("identf", [128, 128], F32, kind="ExternalInput").ap()
    d["xct"] = nc.dram_tensor("xct", [4096, L], F32, kind="ExternalOutput").ap()
    d["sgt"] = nc.dram_tensor("sgt", [4096, L], F32, kind="ExternalOutput").ap()
    d["tst"] = nc.dram_tensor("tst", [128, L], BF16, kind="ExternalOutput").ap()
    d["btct"] = nc.dram_tensor("btct", [32, L], F32, kind="ExternalOutput").ap()
    d["maskt"] = nc.dram_tensor("maskt", [16, L], F32, kind="ExternalOutput").ap()
    d["xcbt"] = nc.dram_tensor("xcbt", [4096, L], BF16, kind="Internal").ap()
    P = Prog(nc)
    P.init_consts([LN_EPS, -0.5])
    P.finish(build_ph4(nc, P, d))
    return nc


def _ph5_nc():
    nc = bass.Bass("TRN2", target_bir_lowering=False, debug=False)
    d = {}
    d["xct"] = nc.dram_tensor("xct", [4096, L], F32, kind="ExternalInput").ap()
    d["sgt"] = nc.dram_tensor("sgt", [4096, L], F32, kind="ExternalInput").ap()
    d["tst"] = nc.dram_tensor("tst", [128, L], BF16, kind="ExternalInput").ap()
    d["btct"] = nc.dram_tensor("btct", [32, L], F32, kind="ExternalInput").ap()
    d["dtw"] = nc.dram_tensor("dtw", [4096, 128], F32, kind="ExternalInput").ap()
    d["dtb"] = nc.dram_tensor("dtb", [4096, 1], F32, kind="ExternalInput").ap()
    d["alog"] = nc.dram_tensor("alog", [4096, 16], F32, kind="ExternalInput").ap()
    d["dcol"] = nc.dram_tensor("dcol", [4096, 1], F32, kind="ExternalInput").ap()
    d["ones128"] = nc.dram_tensor("ones128", [1, 128], F32, kind="ExternalInput").ap()
    d["ident"] = nc.dram_tensor("ident", [128, 128], BF16, kind="ExternalInput").ap()
    d["y3t"] = nc.dram_tensor("y3t", [4096, L], BF16, kind="ExternalOutput").ap()
    P = Prog(nc)
    P.init_consts([1.0, -1.0, -0.5])
    P.finish(build_ph5(nc, P, d))
    return nc


def _ph6a_nc():
    nc = bass.Bass("TRN2", target_bir_lowering=False, debug=False)
    d = {}
    d["xn1t"] = nc.dram_tensor("xn1t", [128, 16 * L], BF16, kind="ExternalInput").ap()
    d["w1f1"] = nc.dram_tensor("w1f1", [8192, D], F32, kind="ExternalInput").ap()
    d["maskt"] = nc.dram_tensor("maskt", [16, L], F32, kind="ExternalInput").ap()
    d["ident"] = nc.dram_tensor("ident", [128, 128], BF16, kind="ExternalInput").ap()
    d["ones128"] = nc.dram_tensor("ones128", [1, 128], F32, kind="ExternalInput").ap()
    d["hmt"] = nc.dram_tensor("hmt", [8192, L], BF16, kind="ExternalOutput").ap()
    P = Prog(nc)
    P.finish(build_ph6a(nc, P, d))
    return nc


def _ph6b_nc():
    nc = bass.Bass("TRN2", target_bir_lowering=False, debug=False)
    d = {}
    d["hmt"] = nc.dram_tensor("hmt", [8192, L], BF16, kind="ExternalInput").ap()
    d["w2f"] = nc.dram_tensor("w2f", [D, 8192], F32, kind="ExternalInput").ap()
    d["y3t"] = nc.dram_tensor("y3t", [4096, L], BF16, kind="ExternalInput").ap()
    d["opw"] = nc.dram_tensor("opw", [D, 4096], F32, kind="ExternalInput").ap()
    d["x1"] = nc.dram_tensor("x1", [L, D], F32, kind="ExternalInput").ap()
    d["ident"] = nc.dram_tensor("ident", [128, 128], BF16, kind="ExternalInput").ap()
    d["identf"] = nc.dram_tensor("identf", [128, 128], F32, kind="ExternalInput").ap()
    d["brn"] = nc.dram_tensor("brn", [L, D], F32, kind="Internal").ap()
    d["out"] = nc.dram_tensor("out", [L, D], F32, kind="ExternalOutput").ap()
    P = Prog(nc)
    P.finish(build_ph6b(nc, P, d))
    return nc


def kernel(**inputs):
    NC = 8
    cores = list(range(NC))
    consts = host_consts()
    x0s = [np.ascontiguousarray(np.asarray(inputs["L0_attn_hidden_states"])[b],
                                dtype=np.float32) for b in range(NC)]
    w1blk, b1t0, gcol, bcol, gbb3 = ttt_layouts(
        np.asarray(inputs["L0_attn_W1"], np.float32),
        np.asarray(inputs["L0_attn_b1"], np.float32),
        np.asarray(inputs["L0_attn_ttt_norm_weight"], np.float32),
        np.asarray(inputs["L0_attn_ttt_norm_bias"], np.float32))
    f32 = lambda k: np.ascontiguousarray(np.asarray(inputs[k], np.float32))
    col = lambda k: np.ascontiguousarray(np.asarray(inputs[k], np.float32).reshape(-1, 1))

    def launch(nc, maps, outnames):
        res = run_bass_kernel_spmd(nc, maps, core_ids=cores)
        return [{k: np.asarray(res.results[c][k]) for k in outnames} for c in range(NC)]

    base1 = {"wq": f32("L0_attn_wq"), "wk": f32("L0_attn_wk"), "wv": f32("L0_attn_wv"),
             "gcol": gcol, "bcol": bcol}
    base1.update({cn: consts[cn] for cn in ["ropeC", "ropeS", "swapP", "ident"]})
    r1 = launch(_ph1_nc(consts), [dict(base1, x0=x0s[b]) for b in range(NC)],
                ["xnt", "qtr", "ktr", "bmtp"])
    base2 = {"gcol": gcol, "w1blk": w1blk, "b1t0": b1t0, "gbb3": gbb3,
             "emx": consts["emx"], "ones128": consts["ones128"],
             "onesblk": consts["onesblk"], "onesblkT": consts["onesblkT"],
             "ident": consts["ident"]}
    r2 = launch(_ph2_nc(), [dict(base2, qtr=r1[b]["qtr"], ktr=r1[b]["ktr"],
                                 bmtp=r1[b]["bmtp"]) for b in range(NC)], ["tout"])
    r3a = launch(_ph3a_nc(), [{"tout": r2[b]["tout"], "wo": f32("L0_attn_wo")}
                              for b in range(NC)], ["brT"])
    base3 = {"fw": f32("L0_ffn_weight"), "fbias": col("L0_ffn_bias"),
             "ident": consts["ident"], "identf": consts["identf"],
             "ones128": consts["ones128"]}
    r3b = launch(_ph3b_nc(), [dict(base3, xnt=r1[b]["xnt"], brT=r3a[b]["brT"],
                                   x0=x0s[b]) for b in range(NC)], ["x1", "xn1t"])
    base4 = {"ipw": f32("L1_attn_in_proj_w"), "convw": f32("L1_attn_conv_w"),
             "convb": col("L1_attn_conv_b"), "xpw": f32("L1_attn_x_proj_w"),
             "wr": f32("L1_ffn_Wr"), "br": col("L1_ffn_br"),
             "ident": consts["ident"], "identf": consts["identf"]}
    r4 = launch(_ph4_nc(), [dict(base4, xn1t=r3b[b]["xn1t"]) for b in range(NC)],
                ["xct", "sgt", "tst", "btct", "maskt"])
    base5 = {"dtw": f32("L1_attn_dt_proj_w"), "dtb": col("L1_attn_dt_proj_b"),
             "alog": f32("L1_attn_A_log"), "dcol": col("L1_attn_D"),
             "ones128": consts["ones128"], "ident": consts["ident"]}
    r5 = launch(_ph5_nc(), [dict(base5, xct=r4[b]["xct"], sgt=r4[b]["sgt"],
                                 tst=r4[b]["tst"], btct=r4[b]["btct"])
                            for b in range(NC)], ["y3t"])
    base6a = {"w1f1": f32("L1_ffn_W1"), "ident": consts["ident"],
              "ones128": consts["ones128"]}
    r6a = launch(_ph6a_nc(), [dict(base6a, xn1t=r3b[b]["xn1t"], maskt=r4[b]["maskt"])
                              for b in range(NC)], ["hmt"])
    base6b = {"w2f": f32("L1_ffn_W2"), "opw": f32("L1_attn_out_proj_w"),
              "ident": consts["ident"], "identf": consts["identf"]}
    r6b = launch(_ph6b_nc(), [dict(base6b, hmt=r6a[b]["hmt"], y3t=r5[b]["y3t"],
                                   x1=r3b[b]["x1"]) for b in range(NC)], ["out"])
    return np.stack([r6b[b]["out"].astype(np.float32) for b in range(NC)], axis=0)


# version 02
# speedup vs baseline: 1.0032x; 1.0032x over previous
import sys
sys.path.insert(0, "/opt/trn_rl_repo")
from collections import defaultdict

import numpy as np
import ml_dtypes

import concourse.bass as bass
import concourse.mybir as mybir
from concourse.bass_utils import run_bass_kernel_spmd

F32 = mybir.dt.float32
BF16 = mybir.dt.bfloat16
AF = mybir.ActivationFunctionType
ALU = mybir.AluOpType
AX = mybir.AxisListType
BF = ml_dtypes.bfloat16

L = 2048
D = 2048
NH, HD, CS = 32, 64, 16
NCH = L // CS
TTT_EPS = 1e-6
LN_EPS = 1e-5

ENGS = ("t", "v", "s", "p", "q")


class Tok:
    __slots__ = ("sem", "val")

    def __init__(self, sem, val):
        self.sem = sem
        self.val = val


class Prog:

    def __init__(self, nc):
        self.nc = nc
        self.ops = {e: [] for e in ENGS}
        self.sems = {}
        self.cnt = defaultdict(int)
        self.waited = {e: defaultdict(int) for e in ENGS}
        self._sem_ctxs = []
        for e in ENGS:
            self._mksem("c" + e)

    def _mksem(self, name):
        ctx = self.nc.semaphore(name)
        h = ctx.__enter__()
        self._sem_ctxs.append(ctx)
        self.sems[name] = h
        return h

    def _wait(self, eng_name, deps):
        need = {}
        for dd in deps:
            if dd is None:
                continue
            if self.waited[eng_name][dd.sem] < dd.val:
                need[dd.sem] = max(need.get(dd.sem, 0), dd.val)
        for sname, val in need.items():
            self.waited[eng_name][sname] = val
            h = self.sems[sname]

            def w(eng, h=h, val=val):
                eng.wait_ge(h, val)

            self.ops[eng_name].append(w)

    def op(self, eng_name, method, *args, deps=(), ret_tok=True, **kwargs):
        self._wait(eng_name, deps)
        if ret_tok:
            cs = "c" + eng_name
            self.cnt[cs] += 1
            val = self.cnt[cs]
            h = self.sems[cs]

            def f(eng, method=method, args=args, kwargs=kwargs, h=h):
                getattr(eng, method)(*args, **kwargs).then_inc(h, 1)

            self.ops[eng_name].append(f)
            return Tok(cs, val)
        else:
            def f(eng, method=method, args=args, kwargs=kwargs):
                getattr(eng, method)(*args, **kwargs)

            self.ops[eng_name].append(f)
            return None

    def dma(self, eng_name, out, in_, deps=(), sem=None, **kwargs):
        ds = sem if sem is not None else "d" + eng_name
        if ds not in self.sems:
            self._mksem(ds)
        prev = Tok(ds, self.cnt[ds]) if self.cnt[ds] else None
        self._wait(eng_name, list(deps) + ([prev] if prev else []))
        self.cnt[ds] += 16
        val = self.cnt[ds]
        h = self.sems[ds]

        def f(eng, out=out, in_=in_, kwargs=kwargs, h=h):
            eng.dma_start(out=out, in_=in_, **kwargs).then_inc(h, 16)

        self.ops[eng_name].append(f)
        return Tok(ds, val)

    def init_consts(self, values):
        toks = []
        for i, v in enumerate(values):
            ap = self.nc.alloc_sbuf_tensor(f"cst_{i}", [128, 1], F32).ap()
            try:
                self.nc.const_aps.register(F32, float(v), ap)
            except Exception:
                continue
            toks.append(self.v("memset", ap, float(v)))
        for e in ENGS:
            self._wait(e, toks)

    def t(self, method, *a, **k):
        return self.op("t", method, *a, **k)

    def v(self, method, *a, **k):
        return self.op("v", method, *a, **k)

    def s(self, method, *a, **k):
        return self.op("s", method, *a, **k)

    def p(self, method, *a, **k):
        return self.op("p", method, *a, **k)

    def barrier(self):
        toks = [Tok(s, self.cnt[s]) for s in list(self.sems) if self.cnt[s] > 0]
        for e in ENGS:
            self._wait(e, toks)

    def finish(self, final_deps):
        self._wait("q", final_deps)
        self.build()

    def build(self):
        with self.nc.Block() as block:
            for e, section in (("t", block.tensor), ("v", block.vector),
                               ("s", block.scalar), ("p", block.gpsimd),
                               ("q", block.sync)):
                ops = self.ops[e]
                if not ops:
                    continue

                def body(eng, ops=ops):
                    for f in ops:
                        f(eng)

                section(body)
        for ctx in self._sem_ctxs:
            ctx.__exit__(None, None, None)


def mmf(i, n):
    return dict(start=(i == 0), stop=(i == n - 1))


def host_consts():
    c = {}
    inv = 1.0 / 10000.0 ** (np.arange(0, HD, 2) / HD)
    th = np.outer(inv, np.arange(L))
    C64 = np.repeat(np.cos(th), 2, axis=0)
    S64 = np.repeat(np.sin(th), 2, axis=0) * np.tile([-1.0, 1.0], 32)[:, None]
    c["ropeC"] = np.tile(C64, (2, 1)).astype(BF)
    c["ropeS"] = np.tile(S64, (2, 1)).astype(BF)
    swap = np.zeros((128, 128), np.float32)
    for i in range(128):
        swap[i, i ^ 1] = 1.0
    c["swapP"] = swap.astype(BF)
    c["ident"] = np.eye(128, dtype=np.float32).astype(BF)
    c["identf"] = np.eye(128, dtype=np.float32)
    eta = 1.0 / np.arange(1, CS + 1)
    em = (np.tril(np.ones((CS, CS))) * eta[:, None]).T.astype(np.float32)
    c["emx"] = np.tile(em, (1, 32)).astype(BF)
    c["ones128"] = np.ones((1, 128), np.float32)
    ob = np.zeros((128, 2), np.float32)
    ob[0:64, 0] = 1.0
    ob[64:128, 1] = 1.0
    c["onesblk"] = ob
    c["onesblkT"] = np.ascontiguousarray(ob.T)
    return c


def ttt_layouts(W1, b1, gw, gb):
    w1blk = np.zeros((128, 2048), np.float32)
    b1t0 = np.zeros((128, 16), np.float32)
    for p in range(16):
        w1blk[0:64, 128 * p:128 * p + 64] = W1[2 * p]
        w1blk[64:128, 128 * p + 64:128 * p + 128] = W1[2 * p + 1]
        b1t0[0:64, p] = b1[2 * p, 0]
        b1t0[64:128, p] = b1[2 * p + 1, 0]
    gcol = np.ascontiguousarray(gw.reshape(2048, 1), dtype=np.float32)
    bcol = np.ascontiguousarray(gb.reshape(2048, 1), dtype=np.float32)
    gbb3 = np.zeros((3, 2048), np.float32)
    gbb3[2] = gb.reshape(2048)
    gflat = gw.reshape(2048)
    for p in range(16):
        gbb3[0, 128 * p:128 * p + 64] = gflat[128 * p:128 * p + 64]
        gbb3[1, 128 * p + 64:128 * p + 128] = gflat[128 * p + 64:128 * p + 128]
    return w1blk, b1t0, gcol, bcol, gbb3


def build_ph1(nc, P, d, stk=None):
    sb = (lambda name, shape, dt: stk.enter_context(nc.sbuf_tensor("p1_" + name, shape, dt)).ap()) if stk is not None else (lambda name, shape, dt: nc.alloc_sbuf_tensor("p1_" + name, shape, dt).ap())
    psum = (lambda name, shape, dt: stk.enter_context(nc.psum_tensor("p1_" + name, shape, dt)).ap()) if stk is not None else (lambda name, shape, dt: nc.alloc_psum_tensor("p1_" + name, shape, dt).ap())

    xnt = sb("xnt", [128, 16 * L], BF16)
    wsb = sb("wsb", [128, 16 * D], BF16)
    x0a = sb("x0a", [128, D], F32)
    x0b = sb("x0b", [128, D], F32)
    ropeCb = sb("ropeCb", [128, L], BF16)
    ropeSb = sb("ropeSb", [128, L], BF16)
    swapPb = sb("swapPb", [128, 128], BF16)
    identb = sb("identb", [128, 128], BF16)
    gcolb = sb("gcolb", [128, 16], F32)
    bcolb = sb("bcolb", [128, 16], F32)
    stats = sb("stats", [128, 8], F32)
    sqjunk = sb("sqjunk", [128, D], F32)
    xn_n = sb("xn_n", [128, D], BF16)
    qtb = sb("qtb", [128, 512], BF16)
    qsw = sb("qsw", [128, 512], BF16)
    ro1 = sb("ro1", [128, 512], BF16)
    kre = sb("kre", [128, 512], BF16)
    kmv = sb("kmv", [128, 512], F32)

    ps_mm = psum("ps_mm", [128, 512], F32)
    ps_sw = psum("ps_sw", [128, 512], F32)
    ps_tr = psum("ps_tr", [128, 512], BF16)

    lC = P.dma("q", ropeCb, d["ropeC"], sem="sc1")
    lS = P.dma("q", ropeSb, d["ropeS"], sem="sc1")
    lP_ = P.dma("q", swapPb, d["swapP"], sem="sc1")
    lI = P.dma("q", identb, d["ident"], sem="sc1")
    lg = P.dma("q", gcolb, d["gcol"].rearrange("(k p) o -> p (k o)", p=128), sem="sc1",
               allow_slow_non_contiguous=True)
    lb = P.dma("q", bcolb, d["bcol"].rearrange("(k p) o -> p (k o)", p=128), sem="sc1",
               allow_slow_non_contiguous=True)

    def load_w(wap, deps):
        toks = []
        for q in range(4):
            src = wap[512 * q:512 * (q + 1), :].rearrange("(k p) o -> p k o", p=128)
            dst = wsb[:, D * 4 * q:D * 4 * (q + 1)].rearrange("p (k o) -> p k o", k=4)
            toks.append(P.dma("p", dst, src, deps=deps, sem="sw" + str(q)))
        return toks

    prev_chain = [lI]
    tr_done = []
    for i in range(16):
        buf = x0a if i % 2 == 0 else x0b
        war = [tr_done[i - 2]] if i >= 2 else []
        lt = P.dma("q", buf, d["x0"][128 * i:128 * (i + 1), :], deps=war, sem="sx" + str(i % 2))
        t_s = P.v("reduce_sum", stats[:, 0:1], buf, axis=AX.X, deps=[lt] + prev_chain)
        t_sq = P.s("activation", sqjunk, buf, AF.Square, accum_out=stats[:, 1:2],
                   deps=[lt] + prev_chain)
        t_mu = P.v("tensor_scalar", stats[:, 2:3], stats[:, 0:1], 1.0 / D, None, ALU.mult,
                   deps=[t_s])
        t_m2 = P.v("tensor_mul", stats[:, 3:4], stats[:, 2:3], stats[:, 2:3], deps=[t_mu])
        t_va = P.v("scalar_tensor_tensor", stats[:, 4:5], stats[:, 1:2], 1.0 / D, stats[:, 3:4],
                   ALU.mult, ALU.subtract, deps=[t_sq, t_m2])
        t_lnv = P.s("activation", stats[:, 5:6], stats[:, 4:5], AF.Ln, bias=LN_EPS, deps=[t_va])
        t_rs = P.s("activation", stats[:, 6:7], stats[:, 5:6], AF.Exp, scale=-0.5, deps=[t_lnv])
        t_xn = P.v("tensor_scalar", xn_n, buf, stats[:, 2:3], stats[:, 6:7],
                   ALU.subtract, ALU.mult, deps=[t_rs])
        prev_chain = [t_xn, t_sq]
        evs = []
        for j4 in range(4):
            trs = []
            for jj in range(4):
                j = 4 * j4 + jj
                war2 = [evs[-1]] if evs else ([tr_done[i - 1]] if i >= 1 else [])
                trs.append(P.t("transpose", ps_tr[:, 128 * jj:128 * (jj + 1)],
                               xn_n[:, 128 * j:128 * (j + 1)], identb,
                               deps=[t_xn] + war2))
            dview = xnt.rearrange("p (k t) -> p k t", k=16)[:, 4 * j4:4 * j4 + 4,
                                                           128 * i:128 * (i + 1)]
            evs.append(P.v("tensor_copy", dview, ps_tr.rearrange("p (k t) -> p k t", k=4),
                           deps=trs))
        tr_done.append(evs[-1])
    xnt_ready = tr_done + [prev_chain[0]]
    so_xnt = P.dma("q", d["xnt"], xnt, deps=xnt_ready, sem="soxnt")

    def proj(wname, out_dram, kind, wdeps):
        wtoks = load_w(d[wname], wdeps)
        outs = []
        for m in range(16):
            for nb in range(4):
                mms = []
                for k in range(16):
                    mms.append(P.t("matmul", ps_mm,
                                   wsb[:, D * k + 128 * m:D * k + 128 * (m + 1)],
                                   xnt[:, L * k + 512 * nb:L * k + 512 * (nb + 1)],
                                   deps=(wtoks + xnt_ready + outs[-1:] if k == 0 else []),
                                   **mmf(k, 16)))
                if kind == "rope":
                    ev = P.s("copy", qtb, ps_mm, deps=[mms[-1]])
                    sw = P.t("matmul", ps_sw, swapPb, qtb, deps=[ev], start=True, stop=True)
                    ev2 = P.s("copy", qsw, ps_sw, deps=[sw])
                    v1 = P.v("tensor_mul", ro1, qtb, ropeCb[:, 512 * nb:512 * (nb + 1)],
                             deps=[ev, lC])
                    v2 = P.v("tensor_mul", qsw, qsw, ropeSb[:, 512 * nb:512 * (nb + 1)],
                             deps=[ev2, lS])
                    v3 = P.v("tensor_add", qtb, ro1, qsw, deps=[v1, v2])
                    st = P.dma("s", out_dram[128 * m:128 * (m + 1), 512 * nb:512 * (nb + 1)],
                               qtb, deps=[v3], sem="sout")
                    outs.append(st)
                else:
                    lk = P.dma("q", kre, d["ktr"][128 * m:128 * (m + 1), 512 * nb:512 * (nb + 1)],
                               deps=(outs[-1:] if outs else wdeps), sem="skre")
                    v1 = P.v("scalar_tensor_tensor", kmv, ps_mm, -1.0, kre,
                             ALU.mult, ALU.add, deps=[mms[-1], lk])
                    v2 = P.v("tensor_scalar", kmv, kmv, bcolb[:, m:m + 1], gcolb[:, m:m + 1],
                             ALU.add, ALU.mult, deps=[v1, lg, lb])
                    st = P.dma("s", d["bmtp"][128 * m:128 * (m + 1), 512 * nb:512 * (nb + 1)],
                               kmv, deps=[v2], sem="sout")
                    outs.append(st)
        return outs, wtoks

    oq, wq_toks = proj("wq", d["qtr"], "rope", [])
    ok, wk_toks = proj("wk", d["ktr"], "rope", oq)
    ov, wv_toks = proj("wv", None, "bmt", ok)
    return [so_xnt] + oq + ok + ov


def build_ph2(nc, P, d, stk=None, nsteps=128, stage=99):
    import os as _os
    _skip = _os.environ.get("DEVSKIP", "").split(",")
    sb = (lambda name, shape, dt: stk.enter_context(nc.sbuf_tensor("p2_" + name, shape, dt)).ap()) if stk is not None else (lambda name, shape, dt: nc.alloc_sbuf_tensor("p2_" + name, shape, dt).ap())
    ps = (lambda name, shape, dt: stk.enter_context(nc.psum_tensor("p2_" + name, shape, dt)).ap()) if stk is not None else (lambda name, shape, dt: nc.alloc_psum_tensor("p2_" + name, shape, dt).ap())

    qtrb = sb("qtrb", [128, 16 * 512], BF16)
    ktrb = sb("ktrb", [128, 16 * 512], BF16)
    bmtpb = sb("bmtpb", [128, 16 * 512], F32)
    W1b = sb("W1b", [128, 2048], BF16)
    w1f = sb("w1f", [128, 2048], F32)
    b1T = sb("b1T", [128, 16], F32)
    gcolb = sb("gcolb", [128, 16], F32)
    g2b = sb("g2b", [128, 16], F32)
    emxb = sb("emxb", [16, 512], BF16)
    ones128 = sb("ones128", [1, 128], F32)
    onesblk = sb("onesblk", [128, 2], F32)
    onesblkT = sb("onesblkT", [2, 128], F32)
    gbb3 = sb("gbb3", [3, 2048], F32)
    identb = sb("identb", [128, 128], BF16)
    attnTs = sb("attnTs", [16, 512], BF16)
    kmA = sb("kmA", [128, 256], BF16)
    kmB = sb("kmB", [128, 256], BF16)
    app = sb("app", [16, 512], BF16)
    appt = sb("appt", [16, 512], BF16)
    zb = sb("zb", [128, 512], F32)
    xh = sb("xh", [128, 256], F32)
    xht = sb("xht", [128, 256], F32)
    gx = sb("gx", [128, 512], F32)
    c1b = sb("c1b", [128, 256], F32)
    gradT = sb("gradT", [128, 256], BF16)
    gradN3 = sb("gradN3", [48, 2048], BF16)
    knS3 = sb("knS3", [48, 2048], BF16)
    app2 = sb("app2", [48, 256], BF16)
    statsS = sb("statsS", [2, 512], F32)
    rows = sb("rows", [2, 512], F32)
    rows2 = sb("rows2", [3, 512], F32)
    lnb = sb("lnb", [128, 512], F32)
    lnt = sb("lnt", [128, 256], F32)
    b1red = sb("b1red", [128, 16], F32)

    psA = ps("psA", [128, 512], F32)
    psB = ps("psB", [128, 512], F32)
    psX = ps("psX", [128, 512], F32)
    psT = ps("psT", [16, 1024], BF16)
    psW = ps("psW", [128, 2048], F32)

    LE = 1.0 / 16.0

    if "gcol" in _skip:
        lq0 = P.v("memset", gcolb, 1.0)
    else:
        lq0 = P.dma("q", gcolb, d["gcol"].rearrange("(k p) o -> p (k o)", p=128),
                    sem="scst", allow_slow_non_contiguous=True)
    l1 = P.v("memset", b1T, 0.0) if "b1t0" in _skip else P.dma("q", b1T, d["b1t0"], sem="scst")
    l2 = P.v("memset", emxb, 0.1) if "emx" in _skip else P.dma("q", emxb, d["emx"], sem="scst")
    l3 = P.v("memset", ones128, 1.0) if "ones" in _skip else P.dma("q", ones128, d["ones128"], sem="scst")
    l4 = P.v("memset", onesblk, 0.5) if "ones" in _skip else P.dma("q", onesblk, d["onesblk"], sem="scst")
    l4b = P.v("memset", onesblkT, 0.5) if "ones" in _skip else P.dma("q", onesblkT, d["onesblkT"], sem="scst")
    l5 = P.v("memset", gbb3, 0.2) if "gbb3" in _skip else P.dma("q", gbb3, d["gbb3"], sem="scst")
    l6 = P.v("memset", identb, 0.0) if "ident" in _skip else P.dma("q", identb, d["ident"], sem="scst")
    l7 = P.v("memset", w1f, 0.01) if "w1ld" in _skip else P.dma("q", w1f, d["w1blk"], sem="scst")
    vg2 = P.v("tensor_mul", g2b, gcolb, gcolb, deps=[lq0])
    vr0 = P.v("memset", rows2[:, 0:256], 1.0)
    vr1 = P.v("memset", rows2[:, 256:512], 0.0)
    sh0 = P.s("copy", W1b, w1f, deps=[l7])
    wzero = P.v("memset", psW[0:1, 0:1], 0.0) if "wzero" in _skip else P.v("memset", psW, 0.0)
    mz1 = P.v("memset", gradN3, 0.0)
    mz2 = P.v("memset", knS3, 0.0)
    mz3 = P.v("memset", app2, 0.0)
    mz4 = P.v("memset", kmA[64:128, :], 0.0)
    mz5 = P.v("memset", kmB[0:64, :], 0.0)

    shadow_tok = sh0
    wprev = wzero
    b1_tok = l1
    prev_store = None
    out_toks = []

    def pview(buf, k=16):
        return buf.rearrange("p (k t) -> p k t", k=k)

    nblocks = (nsteps + 31) // 32
    for blk in range(nblocks):
        bs = slice(512 * blk, 512 * (blk + 1))
        war = [prev_store] if prev_store else []
        _e = "p" if "blkload" in _skip else "q"
        lqb = P.dma(_e, qtrb.rearrange("p (k t) -> p k t", k=16),
                    d["qtr"][:, bs].rearrange("(k p) t -> p k t", p=128),
                    deps=war, sem="sqb")
        lkb = P.dma(_e, ktrb.rearrange("p (k t) -> p k t", k=16),
                    d["ktr"][:, bs].rearrange("(k p) t -> p k t", p=128),
                    deps=war, sem="skb")
        lbb = P.dma(_e, bmtpb.rearrange("p (k t) -> p k t", k=16),
                    d["bmtp"][:, bs].rearrange("(k p) t -> p k t", p=128),
                    deps=war, sem="sbb")
        blk_loads = [lqb, lkb, lbb]
        last_fin = None
        for ci in range(min(32, nsteps - 32 * blk)):
            tc = slice(16 * ci, 16 * (ci + 1))

            def qsl(buf, p):
                return buf[:, 512 * p + 16 * ci:512 * p + 16 * (ci + 1)]

            m1toks = []
            dep0 = [shadow_tok] + blk_loads + ([last_fin] if last_fin else [])
            if stage < 11:
                last_fin = shadow_tok
                continue
            _np16 = int(_os.environ.get("NP16", "16"))
            for p in range(_np16):
                w = W1b[:, 128 * p:128 * (p + 1)]
                m1toks.append(P.t("matmul", psA[0:128, 16 * p:16 * (p + 1)], w, qsl(ktrb, p),
                                  deps=(dep0 if p == 0 else []), start=(p == 0),
                                  stop=(p == _np16 - 1)))
                m1toks.append(P.t("matmul", psB[0:128, 16 * p:16 * (p + 1)], w, qsl(qtrb, p),
                                  deps=[], start=(p == 0),
                                  stop=(stage < 12 and p == _np16 - 1)))
            if stage < 11:
                last_fin = m1toks[-1]
                continue
            kv = ktrb.rearrange("p (k t) -> p k t", k=16)[:, :, tc]
            vkA = P.v("tensor_copy", kmA[0:64, :].rearrange("p (k t) -> p k t", k=16),
                      kv[0:64], deps=dep0 + [mz4, mz5])
            vkB = P.v("tensor_copy", kmB[64:128, :].rearrange("p (k t) -> p k t", k=16),
                      kv[64:128], deps=dep0)
            attoks = []
            for h in range(32):
                p, h2 = divmod(h, 2)
                km = kmA if h2 == 0 else kmB
                attoks.append(P.t("matmul", psX[0:16, 32 * p + 16 * h2:32 * p + 16 * (h2 + 1)],
                                  km[:, 16 * p:16 * (p + 1)], qsl(qtrb, p),
                                  deps=([vkA, vkB] if h == 0 else []),
                                  start=(h == 0), stop=(h == 31)))
            eat = P.s("copy", attnTs, psX[0:16, :], deps=attoks)
            ga1 = P.p("tensor_mul", appt, attnTs, emxb, deps=[eat, l2])
            ga2 = P.p("tensor_add", app, appt, emxb, deps=[ga1])
            avA = app.rearrange("q (a c t) -> q a c t", a=16, c=2)[:, :, 0, :]
            avB = app.rearrange("q (a c t) -> q a c t", a=16, c=2)[:, :, 1, :]
            capA = P.p("tensor_copy", app2[0:16, :].rearrange("q (a t) -> q a t", a=16),
                       avA, deps=[ga2, mz3])
            capB = P.p("tensor_copy", app2[32:48, :].rearrange("q (a t) -> q a t", a=16),
                       avB, deps=[ga2])
            if stage < 12:
                last_fin = capB
                continue
            b1bc = b1T.unsqueeze(2).broadcast_to([128, 16, 16])
            vz1 = P.v("tensor_add", pview(zb[:, 0:256]), pview(psA[:, 0:256]), b1bc,
                      deps=m1toks + [b1_tok])
            ssq = P.s("activation", zb[:, 256:512], zb[:, 0:256], AF.Square, deps=[vz1])
            if stage < 13:
                last_fin = ssq
                continue
            sttoks = []
            for p in range(16):
                rhs3 = zb.rearrange("q (a b t) -> q a b t", a=2, b=16)[:, :, p, :]
                sttoks.append(P.t("matmul", psX[0:2, 32 * p:32 * (p + 1)], onesblk, rhs3,
                                  deps=([ssq, eat, l4] if p == 0 else []),
                                  start=(p == 0), stop=(p == 15)))
            est = P.s("copy", statsS, psX[0:2, :], deps=sttoks)
            if stage < 14:
                last_fin = est
                continue
            muA = statsS.rearrange("q (a b t) -> q a b t", a=16, b=2)[:, :, 0, :]
            sqA = statsS.rearrange("q (a b t) -> q a b t", a=16, b=2)[:, :, 1, :]
            vmu = P.v("tensor_scalar", rows[:, 0:256].rearrange("q (a t) -> q a t", a=16),
                      muA, 1.0 / 64, None, ALU.mult, deps=[est])
            vm2 = P.v("tensor_mul", rows[:, 256:512], rows[:, 0:256], rows[:, 0:256],
                      deps=[vmu])
            vva = P.v("scalar_tensor_tensor",
                      rows[:, 256:512].rearrange("q (a t) -> q a t", a=16),
                      sqA, 1.0 / 64, rows[:, 256:512].rearrange("q (a t) -> q a t", a=16),
                      ALU.mult, ALU.subtract, deps=[vm2])
            sln = P.s("activation", rows[:, 256:512], rows[:, 256:512], AF.Ln,
                      bias=TTT_EPS, deps=[vva])
            srs = P.s("activation", rows[:, 256:512], rows[:, 256:512], AF.Exp,
                      scale=-0.5, deps=[sln])
            vnr = P.v("tensor_scalar", rows[:, 0:256], rows[:, 0:256], -1.0, None,
                      ALU.mult, deps=[srs])
            if stage < 15:
                last_fin = vnr
                continue
            bctoks = []
            for p in range(16):
                rhs3 = rows.rearrange("q (c a t) -> q c a t", c=2, a=16)[:, :, p, :]
                bctoks.append(P.t("matmul", psX[:, 32 * p:32 * (p + 1)], onesblkT, rhs3,
                                  deps=([vnr, est, l4b] if p == 0 else []),
                                  start=(p == 0), stop=(p == 15)))
            nrb = psX.rearrange("p (a c t) -> p a c t", a=16, c=2)[:, :, 0, :]
            rstdb = psX.rearrange("p (a c t) -> p a c t", a=16, c=2)[:, :, 1, :]
            vx1 = P.v("tensor_add", pview(xht), pview(zb[:, 0:256]), nrb, deps=bctoks)
            vx2 = P.v("tensor_mul", pview(xh), pview(xht), rstdb, deps=[vx1])
            if stage < 16:
                last_fin = vx2
                continue
            g2bc = g2b.unsqueeze(2).broadcast_to([128, 16, 16])
            vg1 = P.v("tensor_mul", pview(c1b), pview(xh), g2bc, deps=[vx2, vg2])
            bsl = bmtpb.rearrange("p (k t) -> p k t", k=16)[:, :, tc]
            vg1b = P.v("tensor_add", pview(gx[:, 0:256]), pview(c1b), bsl, deps=[vg1])
            vg3 = P.v("tensor_mul", gx[:, 256:512], gx[:, 0:256], xh, deps=[vg1b])
            if stage < 17:
                last_fin = vg3
                continue
            s2toks = []
            for p in range(16):
                rhs3 = gx.rearrange("q (a b t) -> q a b t", a=2, b=16)[:, :, p, :]
                s2toks.append(P.t("matmul", psA[0:2, 32 * p:32 * (p + 1)], onesblk, rhs3,
                                  deps=([vg3, vz1] if p == 0 else []),
                                  start=(p == 0), stop=(p == 15)))
            es2 = P.s("activation", statsS, psA[0:2, :], AF.Copy, scale=1.0 / 64,
                      deps=s2toks + [vmu, vva])
            b2toks = []
            for p in range(16):
                rhs3 = statsS.rearrange("q (a b t) -> q a b t", a=16, b=2)[:, p, :, :]
                b2toks.append(P.t("matmul", psA[:, 32 * p:32 * (p + 1)], onesblkT, rhs3,
                                  deps=([es2] if p == 0 else []), start=(p == 0), stop=(p == 15)))
            s1b = psA.rearrange("p (a c t) -> p a c t", a=16, c=2)[:, :, 0, :]
            s2b = psA.rearrange("p (a c t) -> p a c t", a=16, c=2)[:, :, 1, :]
            if stage < 18:
                last_fin = b2toks[-1]
                continue
            vc1 = P.v("tensor_sub", pview(c1b), s1b, pview(gx[:, 0:256]), deps=b2toks)
            vc2 = P.v("tensor_mul", pview(xht), pview(xh), s2b, deps=b2toks + [vx2])
            vc3 = P.v("tensor_add", c1b, c1b, xht, deps=[vc1, vc2])
            vgr = P.v("tensor_mul", pview(gradT), pview(c1b), rstdb, deps=[vc3])
            if stage < 2:
                last_fin = vgr
                shadow_tok = shadow_tok
                continue
            trdeps = [vgr]
            evt = []
            for half in range(2):
                trs = []
                for pp in range(8):
                    p = 8 * half + pp
                    trs.append(P.t("matmul", psT[0:16, 128 * pp:128 * (pp + 1)],
                                   gradT[:, 16 * p:16 * (p + 1)], identb,
                                   deps=(trdeps if pp == 0 else []),
                                   start=(pp == 0), stop=(pp == 7), is_transpose=True))
                pv = psT.rearrange("q (a c f) -> q a c f", a=8, c=2)
                hsl = slice(1024 * half, 1024 * (half + 1))
                e1 = P.s("copy",
                         gradN3[0:16, hsl].rearrange("q (a c f) -> q a c f", a=8, c=2)[:, :, 0, :],
                         pv[:, :, 0, :], deps=trs + [mz1])
                e2 = P.s("copy",
                         gradN3[32:48, hsl].rearrange("q (a c f) -> q a c f", a=8, c=2)[:, :, 1, :],
                         pv[:, :, 1, :], deps=trs)
                evt += [e1, e2]
                trdeps = [e2]
            kn_evt = []
            for half in range(2):
                trs = []
                for pp in range(8):
                    p = 8 * half + pp
                    trs.append(P.t("matmul", psT[0:16, 128 * pp:128 * (pp + 1)],
                                   qsl(ktrb, p), identb,
                                   deps=(trdeps if pp == 0 else []),
                                   start=(pp == 0), stop=(pp == 7), is_transpose=True))
                pv = psT.rearrange("q (a c f) -> q a c f", a=8, c=2)
                hsl = slice(1024 * half, 1024 * (half + 1))
                e1 = P.s("activation",
                         knS3[0:16, hsl].rearrange("q (a c f) -> q a c f", a=8, c=2)[:, :, 0, :],
                         pv[:, :, 0, :], AF.Copy, scale=LE, deps=trs + [mz2])
                e2 = P.s("activation",
                         knS3[32:48, hsl].rearrange("q (a c f) -> q a c f", a=8, c=2)[:, :, 1, :],
                         pv[:, :, 1, :], AF.Copy, scale=LE, deps=trs)
                kn_evt += [e1, e2]
                trdeps = [e2]
            if stage < 3:
                last_fin = kn_evt[-1]
                continue
            zctoks = []
            wutoks = []
            for p in range(16):
                gsl = gradN3[0:48, 128 * p:128 * (p + 1)]
                asl = app2[0:48, 16 * p:16 * (p + 1)]
                zctoks.append(P.t("matmul", psB[0:128, 16 * p:16 * (p + 1)],
                                  gsl, asl,
                                  deps=([evt[-1], capA, capB] if p == 0 else []),
                                  start=False, stop=(p == 15)))
                ksl = knS3[0:48, 128 * p:128 * (p + 1)]
                wutoks.append(P.t("matmul", psW[0:128, 128 * p:128 * (p + 1)],
                                  ksl, gsl,
                                  deps=([kn_evt[-1], wprev] if p == 0 else []),
                                  start=True, stop=True))
            if stage < 4:
                vwu = P.v("tensor_add", w1f, w1f, psW, deps=wutoks + [shadow_tok])
                wprev = vwu
                shadow_tok = P.s("copy", W1b, w1f, deps=[vwu] + m1toks)
                last_fin = shadow_tok
                continue
            vz2 = P.v("tensor_add", pview(lnb[:, 0:256]), pview(psB[:, 0:256]), b1bc,
                      deps=zctoks + [b1_tok, vgr])
            ssq2 = P.s("activation", lnb[:, 256:512], lnb[:, 0:256], AF.Square, deps=[vz2])
            st2 = []
            for p in range(16):
                rhs3 = lnb.rearrange("q (a b t) -> q a b t", a=2, b=16)[:, :, p, :]
                st2.append(P.t("matmul", psX[0:2, 32 * p:32 * (p + 1)], onesblk, rhs3,
                               deps=([ssq2, vgr, vx1] if p == 0 else []),
                               start=(p == 0), stop=(p == 15)))
            es3 = P.s("copy", statsS, psX[0:2, :], deps=st2 + [vc1, vc2])
            vmu2 = P.v("tensor_scalar", rows[:, 0:256].rearrange("q (a t) -> q a t", a=16),
                       statsS.rearrange("q (a b t) -> q a b t", a=16, b=2)[:, :, 0, :],
                       1.0 / 64, None, ALU.mult, deps=[es3, vnr])
            vm22 = P.v("tensor_mul", rows[:, 256:512], rows[:, 0:256], rows[:, 0:256],
                       deps=[vmu2])
            vva2 = P.v("scalar_tensor_tensor",
                       rows[:, 256:512].rearrange("q (a t) -> q a t", a=16),
                       statsS.rearrange("q (a b t) -> q a b t", a=16, b=2)[:, :, 1, :],
                       1.0 / 64, rows[:, 256:512].rearrange("q (a t) -> q a t", a=16),
                       ALU.mult, ALU.subtract, deps=[vm22])
            sln2 = P.s("activation", rows[:, 256:512], rows[:, 256:512], AF.Ln,
                       bias=TTT_EPS, deps=[vva2])
            srs2 = P.s("activation", rows[:, 256:512], rows[:, 256:512], AF.Exp,
                       scale=-0.5, deps=[sln2])
            vq1 = P.v("scalar_tensor_tensor", rows2[0:2, 0:256], rows[:, 0:256], -1.0,
                      rows[:, 256:512], ALU.mult, ALU.mult, deps=[srs2, vr0, vr1])
            vq2 = P.v("tensor_copy", rows2[0:2, 256:512], rows[:, 256:512], deps=[srs2])
            b3toks = []
            for p in range(16):
                rhs3 = rows2.rearrange("q (c a t) -> q c a t", c=2, a=16)[:, :, p, :]
                b3toks.append(P.t("matmul", psX[:, 32 * p:32 * (p + 1)],
                                  gbb3[:, 128 * p:128 * (p + 1)], rhs3,
                                  deps=([vq1, vq2, es3, l5] if p == 0 else []),
                                  start=(p == 0), stop=(p == 15)))
            qb = psX.rearrange("p (a c t) -> p a c t", a=16, c=2)[:, :, 0, :]
            pb = psX.rearrange("p (a c t) -> p a c t", a=16, c=2)[:, :, 1, :]
            vfl = P.v("tensor_mul", pview(lnt), pview(lnb[:, 0:256]), pb, deps=b3toks)
            vf2 = P.v("tensor_add", pview(lnt), pview(lnt), qb, deps=[vfl])
            qv = qtrb.rearrange("p (k t) -> p k t", k=16)[:, :, tc]
            vfin = P.v("tensor_add", qv, pview(lnt), qv, deps=[vf2])
            vbr = P.v("reduce_sum", b1red, pview(gradT), axis=AX.X, deps=[vgr, vz2])
            b1_tok = P.v("scalar_tensor_tensor", b1T, b1red, LE, b1T, ALU.mult, ALU.add,
                         deps=[vbr, vz1, vz2])
            vwu = P.v("tensor_add", w1f, w1f, psW, deps=wutoks + [shadow_tok])
            wprev = vwu
            shadow_tok = P.s("copy", W1b, w1f, deps=[vwu] + m1toks)
            last_fin = vfin
        if "tout" in _skip:
            st = P.dma("q", d["tout"][0:128, 0:16], qtrb[:, 0:16], deps=[last_fin], sem="stout")
        else:
            st = P.dma("q", d["tout"][:, bs].rearrange("(k p) t -> p k t", p=128),
                       qtrb.rearrange("p (k t) -> p k t", k=16),
                       deps=[last_fin], sem="stout")
        prev_store = st
        out_toks.append(st)
    if "w1dbg" in d and "dbg" not in _skip:
        out_toks.append(P.dma("q", d["w1dbg"], W1b, deps=[shadow_tok], sem="sdbg"))
        out_toks.append(P.dma("q", d["b1dbg"], b1T, deps=[b1_tok], sem="sdbg"))
    dbgbufs = dict(zb=zb, xh=xh, gx=gx, c1b=c1b, gradT=gradT, gradN3=gradN3,
                   knS3=knS3, app2=app2, attnTs=attnTs, app=app, lnb=lnb, lnt=lnt,
                   statsS=statsS, rows=rows, rows2=rows2, b1red=b1red, w1f=w1f)
    for nm, buf in dbgbufs.items():
        if "dbg_" + nm in d:
            out_toks.append(P.dma("q", d["dbg_" + nm], buf, deps=[shadow_tok, b1_tok],
                                  sem="sdbg"))
    return out_toks





def build_ph3a(nc, P, d, stk=None):
    sb = (lambda n, s, t: stk.enter_context(nc.sbuf_tensor("p3a_" + n, s, t)).ap()) if stk is not None else (lambda n, s, t: nc.alloc_sbuf_tensor("p3a_" + n, s, t).ap())
    ps = (lambda n, s, t: stk.enter_context(nc.psum_tensor("p3a_" + n, s, t)).ap()) if stk is not None else (lambda n, s, t: nc.alloc_psum_tensor("p3a_" + n, s, t).ap())
    wsb = sb("wsb", [128, 16 * D], BF16)
    tsl = sb("tsl", [128, 16 * 512], BF16)
    ev1 = sb("ev1", [128, 512], BF16)
    ev2 = sb("ev2", [128, 512], BF16)
    psm = [ps("psm0", [128, 512], F32), ps("psm1", [128, 512], F32)]

    wtoks = []
    for q in range(4):
        src = d["wo"][512 * q:512 * (q + 1), :].rearrange("(k p) o -> p k o", p=128)
        dst = wsb[:, D * 4 * q:D * 4 * (q + 1)].rearrange("p (k o) -> p k o", k=4)
        wtoks.append(P.dma("p", dst, src, sem="w" + str(q)))
    outs = []
    prev_st = {0: [], 1: []}
    prev_lt = []
    for nb in range(4):
        lt = P.dma("q", tsl.rearrange("p (k t) -> p k t", k=16),
                   d["tout"][:, 512 * nb:512 * (nb + 1)].rearrange("(k p) t -> p k t", p=128),
                   deps=prev_lt, sem="stsl")
        prev_lt = []
        for m in range(16):
            par = m % 2
            pp = psm[par]
            evb = ev1 if par == 0 else ev2
            mms = []
            for k in range(16):
                mms.append(P.t("matmul", pp, wsb[:, D * k + 128 * m:D * k + 128 * (m + 1)],
                               tsl[:, 512 * k:512 * (k + 1)],
                               deps=(wtoks + [lt] + prev_st[par] if k == 0 else []),
                               **mmf(k, 16)))
            e = P.s("copy", evb, pp, deps=mms + prev_st[par])
            st = P.dma("q", d["brT"][128 * m:128 * (m + 1), 512 * nb:512 * (nb + 1)], evb,
                       deps=[e], sem="sbr" + str(par))
            prev_st[par] = [st]
            prev_lt.append(st)
            outs.append(st)
    return outs


def build_ph3b(nc, P, d, stk=None):
    sb = (lambda n, s, t: stk.enter_context(nc.sbuf_tensor("p3b_" + n, s, t)).ap()) if stk is not None else (lambda n, s, t: nc.alloc_sbuf_tensor("p3b_" + n, s, t).ap())
    ps = (lambda n, s, t: stk.enter_context(nc.psum_tensor("p3b_" + n, s, t)).ap()) if stk is not None else (lambda n, s, t: nc.alloc_psum_tensor("p3b_" + n, s, t).ap())
    wtT = sb("wtT", [128, 16 * D], BF16)
    wld = [sb("wld0", [128, D], F32), sb("wld1", [128, D], F32)]
    wq1 = sb("wq1", [128, D], F32)
    wq2 = sb("wq2", [128, D], F32)
    wqb = sb("wqb", [128, D], BF16)
    absn = sb("absn", [128, 16], F32)
    acol = sb("acol", [128, 1], F32)
    rscol = sb("rscol", [128, 1], F32)
    sccol = sb("sccol", [128, 1], F32)
    sval = sb("sval", [1, 1], F32)
    svs = sb("svs", [1, 1], F32)
    onesc = sb("onesc", [128, 1], F32)
    ones128 = sb("ones128", [1, 128], F32)
    xslc = sb("xslc", [128, 16 * 512], BF16)
    brs = sb("brs", [128, 512], BF16)
    trb = sb("trb", [128, 512], BF16)
    bfull = sb("bfull", [128, 512], F32)
    x1buf = [sb("x1b" + str(i), [128, D], F32) for i in range(4)]
    x0t = sb("x0t", [128, D], F32)
    x1t = sb("x1t", [128, D], F32)
    x1n = sb("x1n", [128, D], BF16)
    fbcol = sb("fbcol", [128, 16], F32)
    stats = sb("stats", [128, 8], F32)
    sqjunk = sb("sqjunk", [128, D], F32)
    identb = sb("identb", [128, 128], BF16)
    identf = sb("identf", [128, 128], F32)
    ps_m = ps("ps_m", [128, 512], F32)
    ps_s = ps("ps_s", [128, 512], F32)
    ps_t = ps("ps_t", [128, 512], F32)
    ps_tb = ps("ps_tb", [128, 512], BF16)

    lI = P.dma("q", identb, d["ident"], sem="sc")
    lIf = P.dma("q", identf, d["identf"], sem="sc")
    lfb = P.dma("q", fbcol, d["fbias"].rearrange("(k p) o -> p (k o)", p=128), sem="sc",
                allow_slow_non_contiguous=True)
    lo = P.dma("q", ones128, d["ones128"], sem="sc")
    oz = P.v("memset", onesc, 1.0)
    abst = []
    for i in range(16):
        buf = wld[i % 2]
        war = [abst[i - 2]] if i >= 2 else []
        lw = P.dma("q", buf, d["fw"][128 * i:128 * (i + 1), :], deps=war, sem="sw" + str(i % 2))
        abst.append(P.s("activation", sqjunk, buf, AF.Abs, accum_out=absn[:, i:i + 1],
                        deps=[lw] + ([abst[i - 1]] if i else [])))
    r1 = P.v("reduce_sum", acol, absn, axis=AX.X, deps=abst)
    mm1 = P.t("matmul", ps_s[0:1, 0:1], onesc, acol, deps=[r1, oz], start=True, stop=True)
    e1 = P.s("activation", sval, ps_s[0:1, 0:1], AF.Copy, scale=1.0 / (D * D), deps=[mm1])
    e5 = P.s("activation", svs, ps_s[0:1, 0:1], AF.Copy, scale=1.0 / (D * D), deps=[mm1])
    e2 = P.v("tensor_scalar", sval, sval, 1e-10, None, ALU.add, deps=[e1])
    e3 = P.v("reciprocal", sval, sval, deps=[e2])
    mm2 = P.t("matmul", ps_s[0:128, 4:5], ones128, sval, deps=[e3, lo, e1, e5], start=True,
              stop=True)
    eb = P.v("tensor_copy", rscol, ps_s[0:128, 4:5], deps=[mm2])
    mm3 = P.t("matmul", ps_s[0:128, 6:7], ones128, svs, deps=[e5, mm2, eb], start=True, stop=True)
    eb2 = P.v("tensor_copy", sccol, ps_s[0:128, 6:7], deps=[mm3])
    trdone = []
    for i in range(16):
        buf = wld[i % 2]
        war = [trdone[i - 2]] if i >= 2 else [abst[-1]]
        lw = P.dma("q", buf, d["fw"][128 * i:128 * (i + 1), :], deps=war, sem="sw" + str(i % 2))
        q1 = P.v("tensor_scalar", wq1, buf, rscol, None, ALU.mult,
                 deps=[lw, eb] + ([trdone[i - 1]] if i else []))
        q2 = P.v("tensor_scalar", wq2, wq1, 0.5, None, ALU.is_gt, deps=[q1])
        q3 = P.v("tensor_scalar", wq1, wq1, -0.5, None, ALU.is_lt, deps=[q2])
        q4 = P.v("tensor_sub", wqb, wq2, wq1, deps=[q3])
        evs = []
        for j4 in range(4):
            trs = []
            for jj in range(4):
                j = 4 * j4 + jj
                war2 = [evs[-1]] if evs else ([trdone[i - 1]] if i else [])
                trs.append(P.t("matmul", ps_tb[:, 128 * jj:128 * (jj + 1)],
                               wqb[:, 128 * j:128 * (j + 1)], identb,
                               deps=[q4, lI] + war2, start=(jj == 0), stop=(jj == 3),
                               is_transpose=True))
            dv = wtT.rearrange("p (k o) -> p k o", k=16)[:, 4 * j4:4 * j4 + 4,
                                                        128 * i:128 * (i + 1)]
            evs.append(P.v("tensor_copy", dv, ps_tb.rearrange("p (k o) -> p k o", k=4),
                           deps=trs))
        trdone.append(evs[-1])
    outs = []
    prev = {}
    for nb in range(4):
        lx = P.dma("q", xslc.rearrange("p (k t) -> p k t", k=16),
                   d["xnt"].rearrange("p (k t) -> p k t", k=16)[:, :, 512 * nb:512 * (nb + 1)],
                   deps=prev.get("m", []), sem="sxs")
        for m in range(16):
            mms = []
            for k in range(16):
                mms.append(P.t("matmul", ps_m, wtT[:, D * k + 128 * m:D * k + 128 * (m + 1)],
                               xslc[:, 512 * k:512 * (k + 1)],
                               deps=([trdone[-1], lx] + prev.get("m", []) if k == 0 else []),
                               **mmf(k, 16)))
            lb = P.dma("q", brs, d["brT"][128 * m:128 * (m + 1), 512 * nb:512 * (nb + 1)],
                       deps=prev.get("b", []), sem="sbrl")
            c1 = P.v("scalar_tensor_tensor", bfull, ps_m, sccol, brs, ALU.mult, ALU.add,
                     deps=mms + [lb, eb2])
            c2 = P.v("tensor_scalar", bfull, bfull, fbcol[:, m:m + 1], None, ALU.add,
                     deps=[c1, lfb])
            prev["m"] = [c2]
            prev["b"] = [c2]
            lastav = prev.get("lastav", [])
            for tt in range(4):
                trt = P.t("matmul", ps_t[:, 128 * tt:128 * (tt + 1)],
                          bfull[:, 128 * tt:128 * (tt + 1)], identf,
                          deps=[c2, lIf] + prev.get("t" + str(tt), []) + lastav,
                          start=True, stop=True, is_transpose=True)
                av = P.v("tensor_copy", x1buf[tt][:, 128 * m:128 * (m + 1)],
                         ps_t[:, 128 * tt:128 * (tt + 1)], deps=[trt])
                lastav = [av]
                prev["t" + str(tt)] = [av]
            prev["lastav"] = lastav
        for tt in range(4):
            i = 4 * nb + tt
            lx0 = P.dma("q", x0t, d["x0"][128 * i:128 * (i + 1), :],
                        deps=prev.get("x0", []), sem="sx0")
            a1 = P.v("tensor_add", x1t, x1buf[tt], x0t,
                     deps=[lx0] + prev["t" + str(tt)] + prev.get("x1w", []))
            prev["x0"] = [a1]
            prev["t" + str(tt)].append(a1)
            st1 = P.dma("q", d["x1"][128 * i:128 * (i + 1), :], x1t, deps=[a1], sem="sx1")
            t_s = P.v("reduce_sum", stats[:, 0:1], x1t, axis=AX.X,
                      deps=[a1] + prev.get("stat", []))
            t_sq = P.s("activation", sqjunk, x1t, AF.Square, accum_out=stats[:, 1:2],
                       deps=[a1] + prev.get("sq", []))
            t_mu = P.v("tensor_scalar", stats[:, 2:3], stats[:, 0:1], 1.0 / D, None,
                       ALU.mult, deps=[t_s])
            t_m2 = P.v("tensor_mul", stats[:, 3:4], stats[:, 2:3], stats[:, 2:3], deps=[t_mu])
            t_va = P.v("scalar_tensor_tensor", stats[:, 4:5], stats[:, 1:2], 1.0 / D,
                       stats[:, 3:4], ALU.mult, ALU.subtract, deps=[t_sq, t_m2])
            t_ln = P.s("activation", stats[:, 5:6], stats[:, 4:5], AF.Ln, bias=LN_EPS,
                       deps=[t_va])
            t_rs = P.s("activation", stats[:, 6:7], stats[:, 5:6], AF.Exp, scale=-0.5,
                       deps=[t_ln])
            t_xn = P.v("tensor_scalar", x1n, x1t, stats[:, 2:3], stats[:, 6:7],
                       ALU.subtract, ALU.mult, deps=[t_rs, st1])
            prev["x1w"] = [t_xn]
            prev["sq"] = [t_sq]
            prev["stat"] = [t_xn]
            for j4 in range(4):
                trs = []
                for jj in range(4):
                    j = 4 * j4 + jj
                    trs.append(P.t("matmul", ps_tb[:, 128 * jj:128 * (jj + 1)],
                                   x1n[:, 128 * j:128 * (j + 1)], identb,
                                   deps=[t_xn] + prev.get("xtr", []), start=(jj == 0),
                                   stop=(jj == 3), is_transpose=True))
                bb2 = P.v("tensor_copy", trb.rearrange("p (f t) -> p f t", f=4),
                          ps_tb.rearrange("p (f t) -> p f t", f=4),
                          deps=trs + prev.get("xtrw", []))
                std = P.dma("q",
                            d["xn1t"].rearrange("p (k t) -> p k t", k=16)[:, 4 * j4:4 * j4 + 4,
                                                                          128 * i:128 * (i + 1)],
                            trb.rearrange("p (f t) -> p f t", f=4), deps=[bb2], sem="sxnw")
                prev["xtr"] = [bb2]
                prev["xtrw"] = [std]
                outs.append(std)
            outs.append(st1)
    return outs


def build_ph4(nc, P, d, stk=None):
    sb = (lambda n, s, t: stk.enter_context(nc.sbuf_tensor("p4_" + n, s, t)).ap()) if stk is not None else (lambda n, s, t: nc.alloc_sbuf_tensor("p4_" + n, s, t).ap())
    ps = (lambda n, s, t: stk.enter_context(nc.psum_tensor("p4_" + n, s, t)).ap()) if stk is not None else (lambda n, s, t: nc.alloc_psum_tensor("p4_" + n, s, t).ap())
    xn1t = sb("xn1t", [128, 16 * L], BF16)
    xcb = sb("xcb", [128, L], BF16)
    xcsl = [sb("xcsl0", [128, 512], BF16), sb("xcsl1", [128, 512], BF16)]
    wldA = sb("wldA", [128, D], F32)
    wldB = sb("wldB", [128, D], F32)
    wcolA = sb("wcolA", [128, 2048], BF16)
    wcolB = sb("wcolB", [128, 2048], BF16)
    wbf = sb("wbf", [128, D], BF16)
    hsb = sb("hsb", [128, L], F32)
    cacc = sb("cacc", [128, L], F32)
    xpwT = sb("xpwT", [128, 32 * 160], BF16)
    xpld = sb("xpld", [128, 4096], F32)
    wrT = sb("wrT", [128, 16 * 16], BF16)
    wrld = sb("wrld", [16, D], F32)
    brcol = sb("brcol", [16, 1], F32)
    cwcol = sb("cwcol", [128, 32 * 4], F32)
    cbcol = sb("cbcol", [128, 32], F32)
    lgT = sb("lgT", [16, L], F32)
    lgN = sb("lgN", [128, 16], F32)
    lgW = sb("lgW", [128, 16], F32)
    mx = sb("mx", [128, 1], F32)
    msk = sb("msk", [128, 16], F32)
    mskT = sb("mskT", [16, L], F32)
    identb = sb("identb", [128, 128], BF16)
    identf = sb("identf", [128, 128], F32)
    tsev = sb("tsev", [128, 512], BF16)
    bcev = sb("bcev", [32, 512], F32)
    ps_m = [ps("ps_m0", [128, 512], F32), ps("ps_m1", [128, 512], F32)]
    ps_r = ps("ps_r", [128, 512], F32)
    ps_tb = ps("ps_tb", [128, 512], BF16)
    ps_ts = ps("ps_ts", [128, 512], F32)
    ps_bc = ps("ps_bc", [32, 512], F32)

    lI = P.dma("q", identb, d["ident"], sem="sc")
    lIf = P.dma("q", identf, d["identf"], sem="sc")
    lxn = P.dma("q", xn1t, d["xn1t"], sem="sxn")
    lbr = P.dma("q", brcol, d["br"], sem="sc")
    lcw = P.dma("q", cwcol.rearrange("p (k c) -> p k c", k=32),
                d["convw"].rearrange("(k p) c -> p k c", p=128), sem="sc")
    lcb = P.dma("q", cbcol, d["convb"].rearrange("(k p) o -> p (k o)", p=128), sem="sc",
                allow_slow_non_contiguous=True)

    lwr = P.dma("q", wrld, d["wr"], sem="sc")
    wrbf = sb("wrbf", [16, D], BF16)
    cwr = P.v("tensor_copy", wrbf, wrld, deps=[lwr])
    wr_tr = []
    for k in range(16):
        t1 = P.t("matmul", ps_tb[:, 0:16], wrbf[0:16, 128 * k:128 * (k + 1)],
                 identb[0:16, 0:16],
                 deps=[cwr, lI] + wr_tr[-1:], start=True, stop=True, is_transpose=True)
        wr_tr.append(P.v("tensor_copy", wrT[:, 16 * k:16 * (k + 1)], ps_tb[:, 0:16],
                         deps=[t1]))
    mtoks = []
    for nb in range(4):
        mms = []
        for k in range(16):
            mms.append(P.t("matmul", ps_r[0:16, 0:512], wrT[:, 16 * k:16 * (k + 1)],
                           xn1t[:, L * k + 512 * nb:L * k + 512 * (nb + 1)],
                           deps=(wr_tr + [lxn] + mtoks[-1:] if k == 0 else []),
                           **mmf(k, 16)))
        mtoks.append(P.v("tensor_scalar", lgT[:, 512 * nb:512 * (nb + 1)],
                         ps_r[0:16, 0:512], brcol, None, ALU.add, deps=mms + [lbr]))
    mprev = mtoks[-1:]
    mskb = sb("mskb", [128, 16], BF16)
    for i in range(16):
        t1 = P.t("matmul", ps_ts[:, 0:16], lgT[0:16, 128 * i:128 * (i + 1)],
                 identf[0:16, 0:16],
                 deps=mprev + [lIf], start=True, stop=True, is_transpose=True)
        c1 = P.v("tensor_copy", lgN, ps_ts[:, 0:16], deps=[t1] + mprev)
        c2 = P.v("tensor_copy", lgW, lgN, deps=[c1])
        rprev = [c2]
        for it in range(3):
            r1 = P.v("reduce_max", mx, lgW, axis=AX.X, deps=rprev)
            r2 = P.v("tensor_scalar", msk, lgW, mx, None, ALU.is_ge, deps=[r1])
            r3 = P.v("scalar_tensor_tensor", lgW, msk, -1e30, lgW, ALU.mult, ALU.add,
                     deps=[r2])
            rprev = [r3]
        r4 = P.v("reduce_max", mx, lgW, axis=AX.X, deps=rprev)
        r5 = P.v("tensor_scalar", msk, lgN, mx, None, ALU.is_ge, deps=[r4, c2])
        cb = P.v("tensor_copy", mskb, msk, deps=[r5])
        t2 = P.t("matmul", ps_tb[0:16, 16:144], mskb, identb, deps=[cb] + mprev + [lI],
                 start=True, stop=True, is_transpose=True)
        mprev = [P.v("tensor_copy", mskT[0:16, 128 * i:128 * (i + 1)], ps_tb[0:16, 16:144],
                     deps=[t2])]
    smk = P.dma("q", d["maskt"], mskT, deps=mprev, sem="smk")

    lxp = P.dma("q", xpld, d["xpw"][0:128, :], sem="sxp")
    xpbf = sb("xpbf", [128, 4096], BF16)
    xpbf2 = sb("xpbf2", [32, 4096], BF16)
    xpld2 = sb("xpld2", [32, 4096], F32)
    cx1 = P.v("tensor_copy", xpbf, xpld, deps=[lxp])
    lxp2 = P.dma("q", xpld2, d["xpw"][128:160, :], sem="sxp2")
    cx2 = P.v("tensor_copy", xpbf2, xpld2, deps=[lxp2])
    xp_tr = []
    for k in range(32):
        t1 = P.t("matmul", ps_tb[:, 144:272], xpbf[:, 128 * k:128 * (k + 1)], identb,
                 deps=[cx1, lI] + (xp_tr[-1:] if xp_tr else mprev), start=True, stop=True,
                 is_transpose=True)
        e1 = P.v("tensor_copy", xpwT[:, 160 * k:160 * k + 128], ps_tb[:, 144:272], deps=[t1])
        t2 = P.t("matmul", ps_tb[:, 272:304], xpbf2[0:32, 128 * k:128 * (k + 1)],
                 identb[0:32, 0:32],
                 deps=[cx2, e1], start=True, stop=True, is_transpose=True)
        e2 = P.v("tensor_copy", xpwT[:, 160 * k + 128:160 * (k + 1)], ps_tb[:, 272:304],
                 deps=[t2])
        xp_tr.append(e2)

    outs = []
    wprev = {0: [], 1: []}
    cprev = []
    prev_trs = xp_tr[-1:]
    for m in range(64):
        par = m % 2
        wldb = wldA if par == 0 else wldB
        wcol = wcolA if par == 0 else wcolB
        lw = P.dma("q", wldb, d["ipw"][128 * m:128 * (m + 1), :], deps=wprev[par],
                   sem="swp" + str(par))
        cb_ = P.v("tensor_copy", wbf, wldb, deps=[lw] + prev_trs)
        trs = []
        for j4 in range(4):
            tt4 = []
            for jj in range(4):
                j = 4 * j4 + jj
                tt4.append(P.t("matmul", ps_tb[:, 128 * jj:128 * (jj + 1)],
                               wbf[:, 128 * j:128 * (j + 1)], identb,
                               deps=([cb_] + (trs[-1:] if trs else prev_trs) if jj == 0
                                     else []), start=(jj == 0),
                               stop=(jj == 3), is_transpose=True))
            dv = wcol.rearrange("p (k o) -> p k o", k=16)[:, 4 * j4:4 * j4 + 4, :]
            trs.append(P.v("tensor_copy", dv, ps_tb.rearrange("p (k o) -> p k o", k=4),
                           deps=tt4))
        prev_trs = [trs[-1]]
        mmdone = []
        for nb in range(4):
            pp = ps_m[nb % 2]
            mms = []
            for k in range(16):
                mms.append(P.t("matmul", pp, wcol[:, 128 * k:128 * (k + 1)],
                               xn1t[:, L * k + 512 * nb:L * k + 512 * (nb + 1)],
                               deps=(trs + [lxn] + wprev[nb % 2] if k == 0 else []),
                               **mmf(k, 16)))
            ev = P.s("copy", hsb[:, 512 * nb:512 * (nb + 1)], pp, deps=mms + cprev[-1:])
            wprev[nb % 2] = [ev]
            mmdone.append(ev)
        wprev[par] = wprev[par] + [trs[-1]]
        if m < 32:
            a0 = P.v("tensor_scalar", cacc, hsb, cwcol[:, 4 * m + 3:4 * m + 4], None,
                     ALU.mult, deps=mmdone + cprev[-1:])
            a1 = P.v("scalar_tensor_tensor", cacc[:, 1:L], hsb[:, 0:L - 1],
                     cwcol[:, 4 * m + 2:4 * m + 3], cacc[:, 1:L], ALU.mult, ALU.add,
                     deps=[a0, lcw])
            a2 = P.v("scalar_tensor_tensor", cacc[:, 2:L], hsb[:, 0:L - 2],
                     cwcol[:, 4 * m + 1:4 * m + 2], cacc[:, 2:L], ALU.mult, ALU.add,
                     deps=[a1])
            a3 = P.v("scalar_tensor_tensor", cacc[:, 3:L], hsb[:, 0:L - 3],
                     cwcol[:, 4 * m + 0:4 * m + 1], cacc[:, 3:L], ALU.mult, ALU.add,
                     deps=[a2])
            s1 = P.s("activation", cacc, cacc, AF.Silu, bias=cbcol[:, m:m + 1],
                     deps=[a3, lcb])
            st = P.dma("q", d["xct"][128 * m:128 * (m + 1), :], cacc, deps=[s1], sem="sxc")
            cc = P.v("tensor_copy", xcb, cacc, deps=[s1] + cprev[1:2])
            st2 = P.dma("q", d["xcbt"][128 * m:128 * (m + 1), :], xcb, deps=[cc], sem="sxcb")
            cprev = [st, st2]
            outs.append(st)
        else:
            s1 = P.s("activation", cacc, hsb, AF.Silu, deps=mmdone + cprev[-1:])
            st = P.dma("q", d["sgt"][128 * (m - 32):128 * (m - 31), :], cacc, deps=[s1],
                       sem="ssg")
            cprev = [st]
            outs.append(st)
    tprev = []
    lprev = {0: cprev[1:2], 1: cprev[1:2]}
    for nb in range(4):
        mms = []
        mms2 = []
        for k in range(32):
            par = k % 2
            lxs = P.dma("q", xcsl[par],
                        d["xcbt"][128 * k:128 * (k + 1), 512 * nb:512 * (nb + 1)],
                        deps=lprev[par], sem="sxl" + str(par))
            mms.append(P.t("matmul", ps_ts, xpwT[:, 160 * k:160 * k + 128],
                           xcsl[par],
                           deps=(xp_tr + [lxs] + tprev if k == 0 else [lxs]),
                           **mmf(k, 32)))
            mms2.append(P.t("matmul", ps_bc, xpwT[:, 160 * k + 128:160 * (k + 1)],
                            xcsl[par], deps=[], **mmf(k, 32)))
            lprev[par] = [mms2[-1]]
        e1 = P.s("copy", tsev, ps_ts, deps=mms + tprev)
        e2 = P.v("tensor_copy", bcev, ps_bc, deps=mms2 + tprev)
        st1 = P.dma("q", d["tst"][:, 512 * nb:512 * (nb + 1)], tsev, deps=[e1], sem="sts")
        st2 = P.dma("q", d["btct"][:, 512 * nb:512 * (nb + 1)], bcev, deps=[e2], sem="sbc")
        tprev = [st1, st2]
        outs += [st1, st2]
    outs.append(smk)
    return outs


def build_ph5(nc, P, d, stk=None, TB=512):
    import os as _os
    _ntb = int(_os.environ.get("NTB5", "99"))
    _ndk = int(_os.environ.get("NDK5", "32"))
    _sk5 = _os.environ.get("SKIP5", "").split(",")
    sb = (lambda n, s, t: stk.enter_context(nc.sbuf_tensor("p5_" + n, s, t)).ap()) if stk is not None else (lambda n, s, t: nc.alloc_sbuf_tensor("p5_" + n, s, t).ap())
    ps = (lambda n, s, t: stk.enter_context(nc.psum_tensor("p5_" + n, s, t)).ap()) if stk is not None else (lambda n, s, t: nc.alloc_psum_tensor("p5_" + n, s, t).ap())
    NTB = L // TB
    tst = sb("tst", [128, L], BF16)
    dtwT = sb("dtwT", [128, 32 * 128], BF16)
    dtld = sb("dtld", [128, 128], F32)
    dtbf = sb("dtbf", [128, 128], BF16)
    dtbcol = sb("dtbcol", [128, 32], F32)
    acol = sb("acol", [128, 32 * 16], F32)
    aload = sb("aload", [128, 16], F32)
    dcol = sb("dcol", [128, 32], F32)
    btcr = sb("btcr", [1, 32 * 512], F32)
    ones128 = sb("ones128", [1, 128], F32)
    Bbc = sb("Bbc", [128, 16 * TB], BF16)
    Cbc = sb("Cbc", [128, 16 * TB], BF16)
    xc = [sb("xc0", [128, TB], F32), sb("xc1", [128, TB], F32)]
    sg = [sb("sg0", [128, TB], F32), sb("sg1", [128, TB], F32)]
    dt_ = sb("dt", [128, TB], F32)
    dtev = sb("dtev", [128, TB], F32)
    dtt = sb("dtt", [128, TB], F32)
    dtx = sb("dtx", [128, TB], F32)
    dA = sb("dA", [128, TB], F32)
    uu = sb("uu", [128, TB], F32)
    hh = sb("hh", [128, TB], F32)
    ytmp = sb("ytmp", [128, TB], F32)
    yacc = sb("yacc", [128, TB], F32)
    y3 = sb("y3", [128, TB], BF16)
    state = sb("state", [128, 32 * 16], F32)
    ps_dt = ps("ps_dt", [128, 512], F32)
    ps_bc = ps("ps_bc", [128, 512], F32)
    ps_tb5 = ps("ps_tb5", [128, 128], BF16)

    identb5 = sb("identb5", [128, 128], BF16)
    lI5 = P.dma("q", identb5, d["ident"], sem="sc5")
    l1 = P.dma("q", tst, d["tst"], sem="sc5")
    l2 = P.dma("q", dtbcol, d["dtb"].rearrange("(k p) o -> p (k o)", p=128), sem="sc5",
               allow_slow_non_contiguous=True)
    l3 = P.dma("q", dcol, d["dcol"].rearrange("(k p) o -> p (k o)", p=128), sem="sc5",
               allow_slow_non_contiguous=True)
    l5 = P.dma("q", ones128, d["ones128"], sem="sc5")
    zst = P.v("memset", state, 0.0)
    atoks = []
    if "acol" in _sk5:
        atoks.append(P.v("memset", acol, -1.0))
    else:
        for k in range(32):
            la = P.dma("q", aload, d["alog"][128 * k:128 * (k + 1), :],
                       deps=atoks[-1:], sem="sal")
            e1 = P.s("activation", acol[:, 16 * k:16 * (k + 1)], aload, AF.Exp,
                     deps=[la] + atoks[-1:])
            atoks.append(P.v("tensor_scalar", acol[:, 16 * k:16 * (k + 1)],
                             acol[:, 16 * k:16 * (k + 1)], -1.0, None, ALU.mult, deps=[e1]))
    dtoks = []
    if "dtw" in _sk5:
        dtoks.append(P.v("memset", dtwT, 0.01))
    for k in (range(32) if "dtw" not in _sk5 else []):
        ld = P.dma("q", dtld, d["dtw"][128 * k:128 * (k + 1), :], deps=dtoks[-1:],
                   sem="sdw")
        cf = P.v("tensor_copy", dtbf, dtld, deps=[ld] + dtoks[-1:])
        tr = P.t("matmul", ps_tb5, dtbf, identb5, deps=[cf, lI5] + dtoks[-1:],
                 start=True, stop=True, is_transpose=True)
        dtoks.append(P.v("tensor_copy", dtwT[:, 128 * k:128 * (k + 1)], ps_tb5,
                         deps=[tr]))
    prev = {}
    outs = []
    for tb in range(min(NTB, _ntb)):
        ts_ = slice(TB * tb, TB * (tb + 1))
        if "bcdma" in _sk5:
            cbt = P.v("memset", btcr, 0.1, deps=prev.get("bcrow", []))
        else:
            cbt = P.dma("q", btcr.rearrange("o (r t) -> o r t", r=32), d["btct"][:, ts_],
                        deps=prev.get("bcrow", []), sem="sbcr")
        prev["bcrow"] = []
        bctoks = []
        if "bc" in _sk5 or "bcmm" in _sk5:
            bctoks.append(P.v("memset", Bbc, 0.1, deps=[cbt]))
            bctoks.append(P.v("memset", Cbc, 0.1))
        for n in (range(16) if ("bc" not in _sk5 and "bcmm" not in _sk5) else []):
            for which, buf in ((0, Bbc), (1, Cbc)):
                ri = 16 * which + n
                row = btcr[0:1, TB * ri:TB * (ri + 1)]
                m1 = P.t("matmul", ps_bc[:, 0:TB], ones128, row,
                         deps=([cbt, l5] + prev.get("bc", []) if (n == 0 and which == 0)
                               else bctoks[-1:]),
                         start=True, stop=True)
                bctoks.append(P.s("copy", buf[:, TB * n:TB * (n + 1)], ps_bc[:, 0:TB],
                                  deps=[m1]))
        prev["bcrow"] = [bctoks[-1]]
        prev["bc"] = []
        for dk in range(_ndk):
            par = dk % 2
            lxc = P.dma("q", xc[par], d["xct"][128 * dk:128 * (dk + 1), ts_],
                        deps=prev.get("xc" + str(par), []), sem="sxc" + str(par))
            lsg = P.dma("q", sg[par], d["sgt"][128 * dk:128 * (dk + 1), ts_],
                        deps=prev.get("sg" + str(par), []), sem="ssg" + str(par))
            if "sp" in _sk5:
                v2 = P.v("memset", dt_, 0.4, deps=prev.get("dt_", []) + prev.get("dt", []))
                s1 = v2
                prev["dt"] = [v2]
                prev["dtt"] = [v2]
            else:
                mdt = P.t("matmul", ps_dt[:, 0:TB], dtwT[:, 128 * dk:128 * (dk + 1)],
                          tst[:, ts_], deps=[dtoks[-1], l1] + prev.get("dt", []),
                          start=True, stop=True)
                ev0 = P.v("tensor_copy", dtev, ps_dt[:, 0:TB],
                          deps=[mdt] + prev.get("dtev", []))
                dtp = dtev
                s1 = P.s("activation", dtt, dtp, AF.Abs,
                         bias=dtbcol[:, dk:dk + 1],
                         deps=[mdt, l2] + prev.get("dtt", []) + ([ev0] if ev0 else []))
                s2 = P.s("activation", dtt, dtt, AF.Exp, scale=-1.0, deps=[s1])
                s3 = P.s("activation", dtt, dtt, AF.Ln, bias=1.0, deps=[s2])
                v1 = P.v("tensor_scalar", dt_, dtp, dtbcol[:, dk:dk + 1], 0.0,
                         ALU.add, ALU.max, deps=[mdt] + prev.get("dt_", []) +
                         ([ev0] if ev0 else []))
                v2 = P.v("tensor_add", dt_, dt_, dtt, deps=[v1, s3])
                prev["dtev"] = [s1, v1]
                prev["dt"] = [v2, s1]
                prev["dtt"] = [v2]
            v3 = P.v("tensor_mul", dtx, dt_, xc[par], deps=[v2, lxc] +
                     prev.get("dtx", []))
            prev["dtx"] = []
            yz = prev.get("yacc", [])
            if "scan" in _sk5:
                vy = P.v("memset", yacc, 0.5, deps=yz + prev.get("hh", []))
                prev["hh"] = [vy]
            for n in (range(16) if "scan" not in _sk5 else []):
                if "dA" in _sk5:
                    sA = P.v("memset", dA, 0.9, deps=[v2] + prev.get("dA", []))
                else:
                    sA = P.s("activation", dA, dt_, AF.Exp,
                             scale=acol[:, 16 * dk + n:16 * dk + n + 1],
                             deps=[v2, atoks[-1]] + prev.get("dA", []))
                vu = P.v("tensor_mul", uu, dtx, Bbc[:, TB * n:TB * (n + 1)],
                         deps=[v3, bctoks[-1]] + prev.get("uu", []))
                if "hsc" in _sk5:
                    vh = P.v("tensor_add", hh, dA, uu, deps=[sA, vu] + prev.get("hh", []))
                else:
                    vh = P.v("tensor_tensor_scan", hh, dA, uu,
                             state[:, 16 * dk + n:16 * dk + n + 1], ALU.mult, ALU.add,
                             deps=[sA, vu] + prev.get("hh", []))
                if "stc" in _sk5:
                    vst = vh
                else:
                    vst = P.s("copy", state[:, 16 * dk + n:16 * dk + n + 1],
                              hh[:, TB - 1:TB], deps=[vh])
                prev["dA"] = [vh]
                prev["uu"] = [vh]
                if n == 0:
                    vy = P.v("tensor_mul", yacc, hh, Cbc[:, TB * n:TB * (n + 1)],
                             deps=[vh] + yz)
                else:
                    vy1 = P.v("tensor_mul", ytmp, hh, Cbc[:, TB * n:TB * (n + 1)],
                              deps=[vh] + prev.get("yt", []))
                    if "yadd" in _sk5:
                        vy = P.v("tensor_add", yacc, yacc, ytmp, deps=[vy1, vy])
                    else:
                        vy = P.p("tensor_add", yacc, yacc, ytmp, deps=[vy1, vy])
                    prev["yt"] = [vy]
                prev["hh"] = [vy, vst]
            if "y3c" in _sk5:
                w1 = P.v("memset", yacc, 0.2, deps=[vy, l3, lxc])
            else:
                w1 = P.v("scalar_tensor_tensor", yacc, xc[par], dcol[:, dk:dk + 1], yacc,
                         ALU.mult, ALU.add, deps=[vy, l3, lxc])
            w2 = P.v("tensor_mul", y3, yacc, sg[par], deps=[w1, lsg] +
                     prev.get("y3", []))
            st = P.dma("q", d["y3t"][128 * dk:128 * (dk + 1), ts_], y3, deps=[w2],
                       sem="sy3")
            prev["y3"] = [st]
            prev["yacc"] = [w2]
            prev["xc" + str(par)] = [v3, w1]
            prev["sg" + str(par)] = [w2]
            prev["dt_"] = [v3, sA]
            outs.append(st)
        prev["bc"] = [w2]
    return outs


def build_ph6a(nc, P, d, stk=None):
    sb = (lambda n, s, t: stk.enter_context(nc.sbuf_tensor("p6a_" + n, s, t)).ap()) if stk is not None else (lambda n, s, t: nc.alloc_sbuf_tensor("p6a_" + n, s, t).ap())
    ps = (lambda n, s, t: stk.enter_context(nc.psum_tensor("p6a_" + n, s, t)).ap()) if stk is not None else (lambda n, s, t: nc.alloc_psum_tensor("p6a_" + n, s, t).ap())
    xn1t = sb("xn1t", [128, 16 * L], BF16)
    wld = [sb("wld0", [128, D], F32), sb("wld1", [128, D], F32)]
    wcol = [sb("wcol0", [128, D], BF16), sb("wcol1", [128, D], BF16)]
    wbf = sb("wbf", [128, D], BF16)
    mrow1 = sb("mrow1", [1, L], F32)
    maskp = sb("maskp", [128, L], BF16)
    gel = sb("gel", [128, 512], F32)
    hm = sb("hm", [128, 512], BF16)
    ones128 = sb("ones128", [1, 128], F32)
    identb = sb("identb", [128, 128], BF16)
    ps_m = [ps("ps_m0", [128, 512], F32), ps("ps_m1", [128, 512], F32)]
    ps_b = ps("ps_b", [128, 512], F32)
    ps_tb = ps("ps_tb", [128, 512], BF16)

    lI = P.dma("q", identb, d["ident"], sem="sc")
    lxn = P.dma("q", xn1t, d["xn1t"], sem="sxn")
    l5 = P.dma("q", ones128, d["ones128"], sem="sc")
    outs = []
    prev = {}
    mcur = [None]
    for m in range(64):
        par = m % 2
        blk = m // 4
        lw = P.dma("q", wld[par], d["w1f1"][128 * m:128 * (m + 1), :],
                   deps=prev.get("w" + str(par), []), sem="sw" + str(par))
        cb_ = P.v("tensor_copy", wbf, wld[par], deps=[lw] + prev.get("tr", []))
        trs = []
        for j4 in range(4):
            tt4 = []
            for jj in range(4):
                j = 4 * j4 + jj
                tt4.append(P.t("matmul", ps_tb[:, 128 * jj:128 * (jj + 1)],
                               wbf[:, 128 * j:128 * (j + 1)], identb,
                               deps=([cb_, lI] + (trs[-1:] if trs else prev.get("tr", []))
                                     if jj == 0 else []),
                               start=(jj == 0), stop=(jj == 3), is_transpose=True))
            dv = wcol[par].rearrange("p (k o) -> p k o", k=16)[:, 4 * j4:4 * j4 + 4, :]
            trs.append(P.v("tensor_copy", dv, ps_tb.rearrange("p (k o) -> p k o", k=4),
                           deps=tt4))
        prev["tr"] = [trs[-1]]
        prev["w" + str(par)] = [trs[-1]]
        if m % 4 == 0:
            lmk = P.dma("q", mrow1, d["maskt"][blk:blk + 1, :],
                        deps=prev.get("mk", []), sem="smk")
            mtk = prev.get("mk", [])
            for a_ in range(4):
                mm = P.t("matmul", ps_b, ones128, mrow1[0:1, 512 * a_:512 * (a_ + 1)],
                         deps=[l5, lmk] + mtk, start=True, stop=True)
                mtk = [P.s("copy", maskp[:, 512 * a_:512 * (a_ + 1)], ps_b, deps=[mm])]
            mcur[0] = mtk[0]
            prev["mk"] = [mcur[0]]
        for nb in range(4):
            pp = ps_m[nb % 2]
            mms = []
            for k in range(16):
                mms.append(P.t("matmul", pp, wcol[par][:, 128 * k:128 * (k + 1)],
                               xn1t[:, L * k + 512 * nb:L * k + 512 * (nb + 1)],
                               deps=(trs + [lxn] + prev.get("e" + str(nb % 2), [])
                                     if k == 0 else []), **mmf(k, 16)))
            g1 = P.s("activation", gel, pp, AF.Gelu, deps=mms + prev.get("gel", []))
            g2 = P.v("tensor_mul", hm, gel, maskp[:, 512 * nb:512 * (nb + 1)],
                     deps=[g1, mcur[0]] + prev.get("hm", []))
            st = P.dma("q", d["hmt"][128 * m:128 * (m + 1), 512 * nb:512 * (nb + 1)], hm,
                       deps=[g2], sem="shm")
            prev["e" + str(nb % 2)] = [g1]
            prev["gel"] = [g2]
            prev["hm"] = [st]
            outs.append(st)
    return outs


def build_ph6b(nc, P, d, stk=None):
    sb = (lambda n, s, t: stk.enter_context(nc.sbuf_tensor("p6b_" + n, s, t)).ap()) if stk is not None else (lambda n, s, t: nc.alloc_sbuf_tensor("p6b_" + n, s, t).ap())
    ps = (lambda n, s, t: stk.enter_context(nc.psum_tensor("p6b_" + n, s, t)).ap()) if stk is not None else (lambda n, s, t: nc.alloc_psum_tensor("p6b_" + n, s, t).ap())
    w2all = sb("w2all", [128, 64 * 512], BF16)
    opall = sb("opall", [128, 32 * 512], BF16)
    wld = sb("wld", [128, 4 * 128], F32)
    wbf = sb("wbf", [128, 4 * 128], BF16)
    rsl = [sb("rsl0", [128, 512], BF16), sb("rsl1", [128, 512], BF16)]
    brsl = sb("brsl", [128, 512], F32)
    trbuf = sb("trbuf", [128, 512], F32)
    x1t = sb("x1t", [128, D], F32)
    brt = sb("brt", [128, D], F32)
    outt = sb("outt", [128, D], F32)
    identf = sb("identf", [128, 128], F32)
    identb = sb("identb", [128, 128], BF16)
    ps_a = [ps("ps_a" + str(i), [128, 512], F32) for i in range(4)]
    ps_t = ps("ps_t", [128, 512], F32)
    ps_tb = ps("ps_tb", [128, 512], BF16)

    lIf = P.dma("q", identf, d["identf"], sem="sc")
    lI = P.dma("q", identb, d["ident"], sem="sc")
    outs = []
    prev = {}
    for fg in range(4):
        fr = slice(512 * fg, 512 * (fg + 1))
        ptoks = []
        for src_t, nk, wall in ((d["w2f"], 64, w2all), (d["opw"], 32, opall)):
            for kk in range(nk):
                lw = P.dma("q", wld.rearrange("p (g c) -> p g c", g=4),
                           src_t[fr, 128 * kk:128 * (kk + 1)].rearrange(
                               "(g p) c -> p g c", p=128),
                           deps=prev.get("wld", []), sem="swl")
                cw = P.v("tensor_copy", wbf, wld, deps=[lw] + prev.get("wtr", []))
                trs = []
                for g in range(4):
                    trs.append(P.t("matmul", ps_tb[:, 128 * g:128 * (g + 1)],
                                   wbf[:, 128 * g:128 * (g + 1)], identb,
                                   deps=([cw, lI] + prev.get("wcp", []) if g == 0 else []),
                                   start=(g == 0), stop=(g == 3), is_transpose=True))
                cp = P.v("tensor_copy", wall[:, 512 * kk:512 * (kk + 1)], ps_tb, deps=trs)
                prev["wld"] = [cp]
                prev["wtr"] = [cp]
                prev["wcp"] = [cp]
        wready = prev["wcp"]
        for nb in range(4):
            tsl = slice(512 * nb, 512 * (nb + 1))
            mmtoks = {0: [], 1: [], 2: [], 3: []}
            for k in range(96):
                par = k % 2
                if k < 64:
                    lr = P.dma("q", rsl[par],
                               d["hmt"][128 * k:128 * (k + 1), tsl],
                               deps=prev.get("r" + str(par), []), sem="sr" + str(par))
                    wsl = w2all[:, 512 * k:512 * (k + 1)]
                else:
                    dk = k - 64
                    lr = P.dma("q", rsl[par],
                               d["y3t"][128 * dk:128 * (dk + 1), tsl],
                               deps=prev.get("r" + str(par), []), sem="sr" + str(par))
                    wsl = opall[:, 512 * dk:512 * (dk + 1)]
                ftoks = []
                for fi in range(4):
                    ftoks.append(P.t("matmul", ps_a[fi], wsl[:, 128 * fi:128 * (fi + 1)],
                                     rsl[par],
                                     deps=(wready + [lr] + prev.get("ev", [])
                                           if k == 0 and fi == 0 else ([lr] if fi == 0
                                                                       else [])),
                                     start=(k == 0), stop=(k == 95)))
                    mmtoks[fi] = ftoks[-1:]
                prev["r" + str(par)] = [ftoks[-1]]
            evs = []
            for fi in range(4):
                e1 = P.v("tensor_copy", brsl, ps_a[fi], deps=mmtoks[3] + prev.get("br", []))
                trs = []
                for u in range(4):
                    trs.append(P.t("matmul", ps_t[:, 128 * u:128 * (u + 1)],
                                   brsl[:, 128 * u:128 * (u + 1)], identf,
                                   deps=([e1, lIf] + prev.get("trr", []) if u == 0 else []),
                                   start=(u == 0), stop=(u == 3), is_transpose=True))
                e2 = P.v("tensor_copy", trbuf, ps_t, deps=trs + prev.get("trdma", []))
                prev["trr"] = [e2]
                prev["br"] = [e2]
                st = P.dma("q",
                           d["brn"][512 * nb:512 * (nb + 1),
                                    512 * fg + 128 * fi:512 * fg + 128 * (fi + 1)].rearrange(
                               "(u p) f -> p u f", p=128),
                           trbuf.rearrange("p (u f) -> p u f", u=4), deps=[e2], sem="sbn")
                prev["trdma"] = [st]
                evs.append(st)
            prev["ev"] = evs
        prev["wcp"] = prev["wcp"] + prev["ev"]
    fprev = prev["ev"]
    for i in range(16):
        l1 = P.dma("q", x1t, d["x1"][128 * i:128 * (i + 1), :],
                   deps=fprev if i == 0 else prev.get("f1", []), sem="sfx")
        l2 = P.dma("q", brt, d["brn"][128 * i:128 * (i + 1), :],
                   deps=fprev if i == 0 else prev.get("f2", []), sem="sfb")
        a1 = P.v("tensor_add", outt, x1t, brt, deps=[l1, l2] + prev.get("fo", []))
        st = P.dma("q", d["out"][128 * i:128 * (i + 1), :], outt, deps=[a1], sem="sfo")
        prev["f1"] = [a1]
        prev["f2"] = [a1]
        prev["fo"] = [st]
        outs.append(st)
    return outs


def _ph1_nc(consts):
    nc = bass.Bass("TRN2", target_bir_lowering=False, debug=False)
    d = {}
    d["x0"] = nc.dram_tensor("x0", [L, D], F32, kind="ExternalInput").ap()
    for w in ["wq", "wk", "wv"]:
        d[w] = nc.dram_tensor(w, [D, D], F32, kind="ExternalInput").ap()
    d["gcol"] = nc.dram_tensor("gcol", [D, 1], F32, kind="ExternalInput").ap()
    d["bcol"] = nc.dram_tensor("bcol", [D, 1], F32, kind="ExternalInput").ap()
    for cn in ["ropeC", "ropeS", "swapP", "ident"]:
        d[cn] = nc.dram_tensor(cn, list(consts[cn].shape), BF16, kind="ExternalInput").ap()
    d["xnt"] = nc.dram_tensor("xnt", [128, 16 * L], BF16, kind="ExternalOutput").ap()
    d["qtr"] = nc.dram_tensor("qtr", [D, L], BF16, kind="ExternalOutput").ap()
    d["ktr"] = nc.dram_tensor("ktr", [D, L], BF16, kind="ExternalOutput").ap()
    d["bmtp"] = nc.dram_tensor("bmtp", [D, L], F32, kind="ExternalOutput").ap()
    P = Prog(nc)
    P.init_consts([LN_EPS, -0.5])
    P.finish(build_ph1(nc, P, d))
    return nc


def _ph2_nc():
    nc = bass.Bass("TRN2", target_bir_lowering=False, debug=False)
    d = {}
    d["qtr"] = nc.dram_tensor("qtr", [D, L], BF16, kind="ExternalInput").ap()
    d["ktr"] = nc.dram_tensor("ktr", [D, L], BF16, kind="ExternalInput").ap()
    d["bmtp"] = nc.dram_tensor("bmtp", [D, L], F32, kind="ExternalInput").ap()
    d["gcol"] = nc.dram_tensor("gcol", [D, 1], F32, kind="ExternalInput").ap()
    d["w1blk"] = nc.dram_tensor("w1blk", [128, 2048], F32, kind="ExternalInput").ap()
    d["b1t0"] = nc.dram_tensor("b1t0", [128, 16], F32, kind="ExternalInput").ap()
    d["gbb3"] = nc.dram_tensor("gbb3", [3, 2048], F32, kind="ExternalInput").ap()
    d["emx"] = nc.dram_tensor("emx", [16, 512], BF16, kind="ExternalInput").ap()
    d["ones128"] = nc.dram_tensor("ones128", [1, 128], F32, kind="ExternalInput").ap()
    d["onesblk"] = nc.dram_tensor("onesblk", [128, 2], F32, kind="ExternalInput").ap()
    d["onesblkT"] = nc.dram_tensor("onesblkT", [2, 128], F32, kind="ExternalInput").ap()
    d["ident"] = nc.dram_tensor("ident", [128, 128], BF16, kind="ExternalInput").ap()
    d["tout"] = nc.dram_tensor("tout", [D, L], BF16, kind="ExternalOutput").ap()
    P = Prog(nc)
    P.init_consts([TTT_EPS, -0.5, 1.0 / 16, 1.0 / 64])
    P.finish(build_ph2(nc, P, d))
    return nc


def _ph3a_nc():
    nc = bass.Bass("TRN2", target_bir_lowering=False, debug=False)
    d = {}
    d["tout"] = nc.dram_tensor("tout", [D, L], BF16, kind="ExternalInput").ap()
    d["wo"] = nc.dram_tensor("wo", [D, D], F32, kind="ExternalInput").ap()
    d["brT"] = nc.dram_tensor("brT", [D, L], BF16, kind="ExternalOutput").ap()
    P = Prog(nc)
    P.finish(build_ph3a(nc, P, d))
    return nc


def _ph3b_nc():
    nc = bass.Bass("TRN2", target_bir_lowering=False, debug=False)
    d = {}
    d["fw"] = nc.dram_tensor("fw", [D, D], F32, kind="ExternalInput").ap()
    d["fbias"] = nc.dram_tensor("fbias", [D, 1], F32, kind="ExternalInput").ap()
    d["xnt"] = nc.dram_tensor("xnt", [128, 16 * L], BF16, kind="ExternalInput").ap()
    d["brT"] = nc.dram_tensor("brT", [D, L], BF16, kind="ExternalInput").ap()
    d["x0"] = nc.dram_tensor("x0", [L, D], F32, kind="ExternalInput").ap()
    d["ident"] = nc.dram_tensor("ident", [128, 128], BF16, kind="ExternalInput").ap()
    d["identf"] = nc.dram_tensor("identf", [128, 128], F32, kind="ExternalInput").ap()
    d["ones128"] = nc.dram_tensor("ones128", [1, 128], F32, kind="ExternalInput").ap()
    d["x1"] = nc.dram_tensor("x1", [L, D], F32, kind="ExternalOutput").ap()
    d["xn1t"] = nc.dram_tensor("xn1t", [128, 16 * L], BF16, kind="ExternalOutput").ap()
    P = Prog(nc)
    P.init_consts([LN_EPS, -0.5])
    P.finish(build_ph3b(nc, P, d))
    return nc


def _ph4_nc():
    nc = bass.Bass("TRN2", target_bir_lowering=False, debug=False)
    d = {}
    d["xn1t"] = nc.dram_tensor("xn1t", [128, 16 * L], BF16, kind="ExternalInput").ap()
    d["ipw"] = nc.dram_tensor("ipw", [8192, D], F32, kind="ExternalInput").ap()
    d["convw"] = nc.dram_tensor("convw", [4096, 4], F32, kind="ExternalInput").ap()
    d["convb"] = nc.dram_tensor("convb", [4096, 1], F32, kind="ExternalInput").ap()
    d["xpw"] = nc.dram_tensor("xpw", [160, 4096], F32, kind="ExternalInput").ap()
    d["wr"] = nc.dram_tensor("wr", [16, D], F32, kind="ExternalInput").ap()
    d["br"] = nc.dram_tensor("br", [16, 1], F32, kind="ExternalInput").ap()
    d["ident"] = nc.dram_tensor("ident", [128, 128], BF16, kind="ExternalInput").ap()
    d["identf"] = nc.dram_tensor("identf", [128, 128], F32, kind="ExternalInput").ap()
    d["xct"] = nc.dram_tensor("xct", [4096, L], F32, kind="ExternalOutput").ap()
    d["sgt"] = nc.dram_tensor("sgt", [4096, L], F32, kind="ExternalOutput").ap()
    d["tst"] = nc.dram_tensor("tst", [128, L], BF16, kind="ExternalOutput").ap()
    d["btct"] = nc.dram_tensor("btct", [32, L], F32, kind="ExternalOutput").ap()
    d["maskt"] = nc.dram_tensor("maskt", [16, L], F32, kind="ExternalOutput").ap()
    d["xcbt"] = nc.dram_tensor("xcbt", [4096, L], BF16, kind="Internal").ap()
    P = Prog(nc)
    P.init_consts([LN_EPS, -0.5])
    P.finish(build_ph4(nc, P, d))
    return nc


def _ph5_nc():
    nc = bass.Bass("TRN2", target_bir_lowering=False, debug=False)
    d = {}
    d["xct"] = nc.dram_tensor("xct", [4096, L], F32, kind="ExternalInput").ap()
    d["sgt"] = nc.dram_tensor("sgt", [4096, L], F32, kind="ExternalInput").ap()
    d["tst"] = nc.dram_tensor("tst", [128, L], BF16, kind="ExternalInput").ap()
    d["btct"] = nc.dram_tensor("btct", [32, L], F32, kind="ExternalInput").ap()
    d["dtw"] = nc.dram_tensor("dtw", [4096, 128], F32, kind="ExternalInput").ap()
    d["dtb"] = nc.dram_tensor("dtb", [4096, 1], F32, kind="ExternalInput").ap()
    d["alog"] = nc.dram_tensor("alog", [4096, 16], F32, kind="ExternalInput").ap()
    d["dcol"] = nc.dram_tensor("dcol", [4096, 1], F32, kind="ExternalInput").ap()
    d["ones128"] = nc.dram_tensor("ones128", [1, 128], F32, kind="ExternalInput").ap()
    d["ident"] = nc.dram_tensor("ident", [128, 128], BF16, kind="ExternalInput").ap()
    d["y3t"] = nc.dram_tensor("y3t", [4096, L], BF16, kind="ExternalOutput").ap()
    P = Prog(nc)
    P.init_consts([1.0, -1.0, -0.5])
    P.finish(build_ph5(nc, P, d))
    return nc


def _ph6a_nc():
    nc = bass.Bass("TRN2", target_bir_lowering=False, debug=False)
    d = {}
    d["xn1t"] = nc.dram_tensor("xn1t", [128, 16 * L], BF16, kind="ExternalInput").ap()
    d["w1f1"] = nc.dram_tensor("w1f1", [8192, D], F32, kind="ExternalInput").ap()
    d["maskt"] = nc.dram_tensor("maskt", [16, L], F32, kind="ExternalInput").ap()
    d["ident"] = nc.dram_tensor("ident", [128, 128], BF16, kind="ExternalInput").ap()
    d["ones128"] = nc.dram_tensor("ones128", [1, 128], F32, kind="ExternalInput").ap()
    d["hmt"] = nc.dram_tensor("hmt", [8192, L], BF16, kind="ExternalOutput").ap()
    P = Prog(nc)
    P.finish(build_ph6a(nc, P, d))
    return nc


def _ph6b_nc():
    nc = bass.Bass("TRN2", target_bir_lowering=False, debug=False)
    d = {}
    d["hmt"] = nc.dram_tensor("hmt", [8192, L], BF16, kind="ExternalInput").ap()
    d["w2f"] = nc.dram_tensor("w2f", [D, 8192], F32, kind="ExternalInput").ap()
    d["y3t"] = nc.dram_tensor("y3t", [4096, L], BF16, kind="ExternalInput").ap()
    d["opw"] = nc.dram_tensor("opw", [D, 4096], F32, kind="ExternalInput").ap()
    d["x1"] = nc.dram_tensor("x1", [L, D], F32, kind="ExternalInput").ap()
    d["ident"] = nc.dram_tensor("ident", [128, 128], BF16, kind="ExternalInput").ap()
    d["identf"] = nc.dram_tensor("identf", [128, 128], F32, kind="ExternalInput").ap()
    d["brn"] = nc.dram_tensor("brn", [L, D], F32, kind="Internal").ap()
    d["out"] = nc.dram_tensor("out", [L, D], F32, kind="ExternalOutput").ap()
    P = Prog(nc)
    P.finish(build_ph6b(nc, P, d))
    return nc


def kernel_multi(**inputs):
    NC = 8
    cores = list(range(NC))
    consts = host_consts()
    x0s = [np.ascontiguousarray(np.asarray(inputs["L0_attn_hidden_states"])[b],
                                dtype=np.float32) for b in range(NC)]
    w1blk, b1t0, gcol, bcol, gbb3 = ttt_layouts(
        np.asarray(inputs["L0_attn_W1"], np.float32),
        np.asarray(inputs["L0_attn_b1"], np.float32),
        np.asarray(inputs["L0_attn_ttt_norm_weight"], np.float32),
        np.asarray(inputs["L0_attn_ttt_norm_bias"], np.float32))
    f32 = lambda k: np.ascontiguousarray(np.asarray(inputs[k], np.float32))
    col = lambda k: np.ascontiguousarray(np.asarray(inputs[k], np.float32).reshape(-1, 1))

    def launch(nc, maps, outnames):
        res = run_bass_kernel_spmd(nc, maps, core_ids=cores)
        return [{k: np.asarray(res.results[c][k]) for k in outnames} for c in range(NC)]

    base1 = {"wq": f32("L0_attn_wq"), "wk": f32("L0_attn_wk"), "wv": f32("L0_attn_wv"),
             "gcol": gcol, "bcol": bcol}
    base1.update({cn: consts[cn] for cn in ["ropeC", "ropeS", "swapP", "ident"]})
    r1 = launch(_ph1_nc(consts), [dict(base1, x0=x0s[b]) for b in range(NC)],
                ["xnt", "qtr", "ktr", "bmtp"])
    base2 = {"gcol": gcol, "w1blk": w1blk, "b1t0": b1t0, "gbb3": gbb3,
             "emx": consts["emx"], "ones128": consts["ones128"],
             "onesblk": consts["onesblk"], "onesblkT": consts["onesblkT"],
             "ident": consts["ident"]}
    r2 = launch(_ph2_nc(), [dict(base2, qtr=r1[b]["qtr"], ktr=r1[b]["ktr"],
                                 bmtp=r1[b]["bmtp"]) for b in range(NC)], ["tout"])
    r3a = launch(_ph3a_nc(), [{"tout": r2[b]["tout"], "wo": f32("L0_attn_wo")}
                              for b in range(NC)], ["brT"])
    base3 = {"fw": f32("L0_ffn_weight"), "fbias": col("L0_ffn_bias"),
             "ident": consts["ident"], "identf": consts["identf"],
             "ones128": consts["ones128"]}
    r3b = launch(_ph3b_nc(), [dict(base3, xnt=r1[b]["xnt"], brT=r3a[b]["brT"],
                                   x0=x0s[b]) for b in range(NC)], ["x1", "xn1t"])
    base4 = {"ipw": f32("L1_attn_in_proj_w"), "convw": f32("L1_attn_conv_w"),
             "convb": col("L1_attn_conv_b"), "xpw": f32("L1_attn_x_proj_w"),
             "wr": f32("L1_ffn_Wr"), "br": col("L1_ffn_br"),
             "ident": consts["ident"], "identf": consts["identf"]}
    r4 = launch(_ph4_nc(), [dict(base4, xn1t=r3b[b]["xn1t"]) for b in range(NC)],
                ["xct", "sgt", "tst", "btct", "maskt"])
    base5 = {"dtw": f32("L1_attn_dt_proj_w"), "dtb": col("L1_attn_dt_proj_b"),
             "alog": f32("L1_attn_A_log"), "dcol": col("L1_attn_D"),
             "ones128": consts["ones128"], "ident": consts["ident"]}
    r5 = launch(_ph5_nc(), [dict(base5, xct=r4[b]["xct"], sgt=r4[b]["sgt"],
                                 tst=r4[b]["tst"], btct=r4[b]["btct"])
                            for b in range(NC)], ["y3t"])
    base6a = {"w1f1": f32("L1_ffn_W1"), "ident": consts["ident"],
              "ones128": consts["ones128"]}
    r6a = launch(_ph6a_nc(), [dict(base6a, xn1t=r3b[b]["xn1t"], maskt=r4[b]["maskt"])
                              for b in range(NC)], ["hmt"])
    base6b = {"w2f": f32("L1_ffn_W2"), "opw": f32("L1_attn_out_proj_w"),
              "ident": consts["ident"], "identf": consts["identf"]}
    r6b = launch(_ph6b_nc(), [dict(base6b, hmt=r6a[b]["hmt"], y3t=r5[b]["y3t"],
                                   x1=r3b[b]["x1"]) for b in range(NC)], ["out"])
    return np.stack([r6b[b]["out"].astype(np.float32) for b in range(NC)], axis=0)


def build_fused():
    from contextlib import ExitStack
    nc = bass.Bass("TRN2", target_bir_lowering=False, debug=False)
    consts = host_consts()
    d = {}
    EI, IN, EO = "ExternalInput", "Internal", "ExternalOutput"
    def dt_(name, shape, dtype, kind):
        d[name] = nc.dram_tensor(name, shape, dtype, kind=kind).ap()
    dt_("x0", [L, D], F32, EI)
    for w in ["wq", "wk", "wv", "wo", "fw"]:
        dt_(w, [D, D], F32, EI)
    dt_("fbias", [D, 1], F32, EI)
    dt_("gcol", [D, 1], F32, EI)
    dt_("bcol", [D, 1], F32, EI)
    dt_("w1blk", [128, 2048], F32, EI)
    dt_("b1t0", [128, 16], F32, EI)
    dt_("gbb3", [3, 2048], F32, EI)
    dt_("emx", [16, 512], BF16, EI)
    dt_("ones128", [1, 128], F32, EI)
    dt_("onesblk", [128, 2], F32, EI)
    dt_("onesblkT", [2, 128], F32, EI)
    dt_("ropeC", [128, L], BF16, EI)
    dt_("ropeS", [128, L], BF16, EI)
    dt_("swapP", [128, 128], BF16, EI)
    dt_("ident", [128, 128], BF16, EI)
    dt_("identf", [128, 128], F32, EI)
    dt_("ipw", [8192, D], F32, EI)
    dt_("convw", [4096, 4], F32, EI)
    dt_("convb", [4096, 1], F32, EI)
    dt_("xpw", [160, 4096], F32, EI)
    dt_("wr", [16, D], F32, EI)
    dt_("br", [16, 1], F32, EI)
    dt_("dtw", [4096, 128], F32, EI)
    dt_("dtb", [4096, 1], F32, EI)
    dt_("alog", [4096, 16], F32, EI)
    dt_("dcol", [4096, 1], F32, EI)
    dt_("opw", [D, 4096], F32, EI)
    dt_("w1f1", [8192, D], F32, EI)
    dt_("w2f", [D, 8192], F32, EI)
    dt_("xnt", [128, 16 * L], BF16, IN)
    dt_("qtr", [D, L], BF16, IN)
    dt_("ktr", [D, L], BF16, IN)
    dt_("bmtp", [D, L], F32, IN)
    dt_("tout", [D, L], BF16, IN)
    dt_("brT", [D, L], BF16, IN)
    dt_("x1", [L, D], F32, IN)
    dt_("xn1t", [128, 16 * L], BF16, IN)
    dt_("xct", [4096, L], F32, IN)
    dt_("sgt", [4096, L], F32, IN)
    dt_("tst", [128, L], BF16, IN)
    dt_("btct", [32, L], F32, IN)
    dt_("maskt", [16, L], F32, IN)
    dt_("xcbt", [4096, L], BF16, IN)
    dt_("y3t", [4096, L], BF16, IN)
    dt_("hmt", [8192, L], BF16, IN)
    dt_("brn", [L, D], F32, IN)
    dt_("out", [L, D], F32, EO)

    P = Prog(nc)
    P.init_consts([LN_EPS, TTT_EPS, -0.5, -1.0, 1.0, 1.0 / 16, 1.0 / 64])
    final = None
    for builder in (build_ph1, build_ph2, build_ph3a, build_ph3b, build_ph4,
                    build_ph5, build_ph6a, build_ph6b):
        with ExitStack() as stk:
            final = builder(nc, P, d, stk=stk)
            P.barrier()
    P.finish(final)
    return nc


def kernel(**inputs):
    NC = 8
    cores = list(range(NC))
    consts = host_consts()
    w1blk, b1t0, gcol, bcol, gbb3 = ttt_layouts(
        np.asarray(inputs["L0_attn_W1"], np.float32),
        np.asarray(inputs["L0_attn_b1"], np.float32),
        np.asarray(inputs["L0_attn_ttt_norm_weight"], np.float32),
        np.asarray(inputs["L0_attn_ttt_norm_bias"], np.float32))
    f32 = lambda k: np.ascontiguousarray(np.asarray(inputs[k], np.float32))
    col = lambda k: np.ascontiguousarray(np.asarray(inputs[k], np.float32).reshape(-1, 1))
    base = {"wq": f32("L0_attn_wq"), "wk": f32("L0_attn_wk"), "wv": f32("L0_attn_wv"),
            "wo": f32("L0_attn_wo"), "fw": f32("L0_ffn_weight"),
            "fbias": col("L0_ffn_bias"), "gcol": gcol, "bcol": bcol,
            "w1blk": w1blk, "b1t0": b1t0, "gbb3": gbb3,
            "ipw": f32("L1_attn_in_proj_w"), "convw": f32("L1_attn_conv_w"),
            "convb": col("L1_attn_conv_b"), "xpw": f32("L1_attn_x_proj_w"),
            "wr": f32("L1_ffn_Wr"), "br": col("L1_ffn_br"),
            "dtw": f32("L1_attn_dt_proj_w"), "dtb": col("L1_attn_dt_proj_b"),
            "alog": f32("L1_attn_A_log"), "dcol": col("L1_attn_D"),
            "opw": f32("L1_attn_out_proj_w"), "w1f1": f32("L1_ffn_W1"),
            "w2f": f32("L1_ffn_W2")}
    for cn in ["ropeC", "ropeS", "swapP", "ident", "identf", "emx", "ones128",
               "onesblk", "onesblkT"]:
        base[cn] = consts[cn]
    x = np.asarray(inputs["L0_attn_hidden_states"])
    maps = [dict(base, x0=np.ascontiguousarray(x[b], dtype=np.float32))
            for b in range(NC)]
    nc = build_fused()
    res = run_bass_kernel_spmd(nc, maps, core_ids=cores)
    return np.stack([np.asarray(res.results[b]["out"], np.float32) for b in range(NC)],
                    axis=0)


# version 04
# speedup vs baseline: 1.2244x; 1.2205x over previous
import sys
sys.path.insert(0, "/opt/trn_rl_repo")
from collections import defaultdict

import numpy as np
import ml_dtypes

import concourse.bass as bass
import concourse.mybir as mybir
from concourse.bass_utils import run_bass_kernel_spmd

F32 = mybir.dt.float32
BF16 = mybir.dt.bfloat16
AF = mybir.ActivationFunctionType
ALU = mybir.AluOpType
AX = mybir.AxisListType
BF = ml_dtypes.bfloat16

L = 2048
D = 2048
NH, HD, CS = 32, 64, 16
NCH = L // CS
TTT_EPS = 1e-6
LN_EPS = 1e-5

ENGS = ("t", "v", "s", "p", "q")


class Tok:
    __slots__ = ("sem", "val")

    def __init__(self, sem, val):
        self.sem = sem
        self.val = val


class Prog:

    def __init__(self, nc):
        self.nc = nc
        self.ops = {e: [] for e in ENGS}
        self.sems = {}
        self.cnt = defaultdict(int)
        self.waited = {e: defaultdict(int) for e in ENGS}
        self._sem_ctxs = []
        for e in ENGS:
            self._mksem("c" + e)

    def _mksem(self, name):
        ctx = self.nc.semaphore(name)
        h = ctx.__enter__()
        self._sem_ctxs.append(ctx)
        self.sems[name] = h
        return h

    def _wait(self, eng_name, deps):
        need = {}
        for dd in deps:
            if dd is None:
                continue
            if self.waited[eng_name][dd.sem] < dd.val:
                need[dd.sem] = max(need.get(dd.sem, 0), dd.val)
        for sname, val in need.items():
            self.waited[eng_name][sname] = val
            h = self.sems[sname]

            def w(eng, h=h, val=val):
                eng.wait_ge(h, val)

            self.ops[eng_name].append(w)

    def op(self, eng_name, method, *args, deps=(), ret_tok=True, **kwargs):
        self._wait(eng_name, deps)
        if ret_tok:
            cs = "c" + eng_name
            self.cnt[cs] += 1
            val = self.cnt[cs]
            h = self.sems[cs]

            def f(eng, method=method, args=args, kwargs=kwargs, h=h):
                getattr(eng, method)(*args, **kwargs).then_inc(h, 1)

            self.ops[eng_name].append(f)
            return Tok(cs, val)
        else:
            def f(eng, method=method, args=args, kwargs=kwargs):
                getattr(eng, method)(*args, **kwargs)

            self.ops[eng_name].append(f)
            return None

    def dma(self, eng_name, out, in_, deps=(), sem=None, **kwargs):
        ds = sem if sem is not None else "d" + eng_name
        if ds not in self.sems:
            self._mksem(ds)
        prev = Tok(ds, self.cnt[ds]) if self.cnt[ds] else None
        self._wait(eng_name, list(deps) + ([prev] if prev else []))
        self.cnt[ds] += 16
        val = self.cnt[ds]
        h = self.sems[ds]

        def f(eng, out=out, in_=in_, kwargs=kwargs, h=h):
            eng.dma_start(out=out, in_=in_, **kwargs).then_inc(h, 16)

        self.ops[eng_name].append(f)
        return Tok(ds, val)

    def init_consts(self, values):
        toks = []
        for i, v in enumerate(values):
            ap = self.nc.alloc_sbuf_tensor(f"cst_{i}", [128, 1], F32).ap()
            try:
                self.nc.const_aps.register(F32, float(v), ap)
            except Exception:
                continue
            toks.append(self.v("memset", ap, float(v)))
        for e in ENGS:
            self._wait(e, toks)

    def t(self, method, *a, **k):
        return self.op("t", method, *a, **k)

    def v(self, method, *a, **k):
        return self.op("v", method, *a, **k)

    def s(self, method, *a, **k):
        return self.op("s", method, *a, **k)

    def p(self, method, *a, **k):
        return self.op("p", method, *a, **k)

    def barrier(self):
        toks = [Tok(s, self.cnt[s]) for s in list(self.sems) if self.cnt[s] > 0]
        for e in ENGS:
            self._wait(e, toks)

    def finish(self, final_deps):
        self._wait("q", final_deps)
        self.build()

    def build(self):
        with self.nc.Block() as block:
            for e, section in (("t", block.tensor), ("v", block.vector),
                               ("s", block.scalar), ("p", block.gpsimd),
                               ("q", block.sync)):
                ops = self.ops[e]
                if not ops:
                    continue

                def body(eng, ops=ops):
                    for f in ops:
                        f(eng)

                section(body)
        for ctx in self._sem_ctxs:
            ctx.__exit__(None, None, None)


def mmf(i, n):
    return dict(start=(i == 0), stop=(i == n - 1))


def host_consts():
    c = {}
    inv = 1.0 / 10000.0 ** (np.arange(0, HD, 2) / HD)
    th = np.outer(inv, np.arange(L))
    C64 = np.repeat(np.cos(th), 2, axis=0)
    S64 = np.repeat(np.sin(th), 2, axis=0) * np.tile([-1.0, 1.0], 32)[:, None]
    c["ropeC"] = np.tile(C64, (2, 1)).astype(BF)
    c["ropeS"] = np.tile(S64, (2, 1)).astype(BF)
    swap = np.zeros((128, 128), np.float32)
    for i in range(128):
        swap[i, i ^ 1] = 1.0
    c["swapP"] = swap.astype(BF)
    c["ident"] = np.eye(128, dtype=np.float32).astype(BF)
    c["identf"] = np.eye(128, dtype=np.float32)
    eta = 1.0 / np.arange(1, CS + 1)
    em = (np.tril(np.ones((CS, CS))) * eta[:, None]).T.astype(np.float32)
    c["emx"] = np.tile(em, (1, 32)).astype(BF)
    c["ones128"] = np.ones((1, 128), np.float32)
    ob = np.zeros((128, 2), np.float32)
    ob[0:64, 0] = 1.0
    ob[64:128, 1] = 1.0
    c["onesblk"] = ob
    c["onesblkT"] = np.ascontiguousarray(ob.T)
    return c


def ttt_layouts(W1, b1, gw, gb):
    w1blk = np.zeros((128, 2048), np.float32)
    b1t0 = np.zeros((128, 16), np.float32)
    for p in range(16):
        w1blk[0:64, 128 * p:128 * p + 64] = W1[2 * p]
        w1blk[64:128, 128 * p + 64:128 * p + 128] = W1[2 * p + 1]
        b1t0[0:64, p] = b1[2 * p, 0]
        b1t0[64:128, p] = b1[2 * p + 1, 0]
    gcol = np.ascontiguousarray(gw.reshape(2048, 1), dtype=np.float32)
    bcol = np.ascontiguousarray(gb.reshape(2048, 1), dtype=np.float32)
    gbb3 = np.zeros((3, 2048), np.float32)
    gbb3[2] = gb.reshape(2048)
    gflat = gw.reshape(2048)
    for p in range(16):
        gbb3[0, 128 * p:128 * p + 64] = gflat[128 * p:128 * p + 64]
        gbb3[1, 128 * p + 64:128 * p + 128] = gflat[128 * p + 64:128 * p + 128]
    return w1blk, b1t0, gcol, bcol, gbb3


def build_ph1(nc, P, d, stk=None):
    sb = (lambda name, shape, dt: stk.enter_context(nc.sbuf_tensor("p1_" + name, shape, dt)).ap()) if stk is not None else (lambda name, shape, dt: nc.alloc_sbuf_tensor("p1_" + name, shape, dt).ap())
    psum = (lambda name, shape, dt: stk.enter_context(nc.psum_tensor("p1_" + name, shape, dt)).ap()) if stk is not None else (lambda name, shape, dt: nc.alloc_psum_tensor("p1_" + name, shape, dt).ap())

    xnt = sb("xnt", [128, 16 * L], BF16)
    wsb = sb("wsb", [128, 16 * D], BF16)
    x0a = sb("x0a", [128, D], F32)
    x0b = sb("x0b", [128, D], F32)
    ropeCb = sb("ropeCb", [128, L], BF16)
    ropeSb = sb("ropeSb", [128, L], BF16)
    swapPb = sb("swapPb", [128, 128], BF16)
    identb = sb("identb", [128, 128], BF16)
    gcolb = sb("gcolb", [128, 16], F32)
    bcolb = sb("bcolb", [128, 16], F32)
    stats = sb("stats", [128, 8], F32)
    sqjunk = sb("sqjunk", [128, D], F32)
    xn_n = sb("xn_n", [128, D], BF16)
    qtb = [sb("qtb0", [128, 512], BF16), sb("qtb1", [128, 512], BF16)]
    qsw = [sb("qsw0", [128, 512], BF16), sb("qsw1", [128, 512], BF16)]
    ro1 = [sb("ro10", [128, 512], BF16), sb("ro11", [128, 512], BF16)]
    kre = [sb("kre0", [128, 512], BF16), sb("kre1", [128, 512], BF16)]
    kmv = [sb("kmv0", [128, 512], F32), sb("kmv1", [128, 512], F32)]

    ps_mm = [psum("ps_mm0", [128, 512], F32), psum("ps_mm1", [128, 512], F32)]
    ps_sw = [psum("ps_sw0", [128, 512], F32), psum("ps_sw1", [128, 512], F32)]
    ps_tr = psum("ps_tr", [128, 512], BF16)

    lC = P.dma("q", ropeCb, d["ropeC"], sem="sc1")
    lS = P.dma("q", ropeSb, d["ropeS"], sem="sc1")
    lP_ = P.dma("q", swapPb, d["swapP"], sem="sc1")
    lI = P.dma("q", identb, d["ident"], sem="sc1")
    lg = P.dma("q", gcolb, d["gcol"].rearrange("(k p) o -> p (k o)", p=128), sem="sc1",
               allow_slow_non_contiguous=True)
    lb = P.dma("q", bcolb, d["bcol"].rearrange("(k p) o -> p (k o)", p=128), sem="sc1",
               allow_slow_non_contiguous=True)

    def load_w(wap, deps):
        toks = []
        for q in range(4):
            src = wap[512 * q:512 * (q + 1), :].rearrange("(k p) o -> p k o", p=128)
            dst = wsb[:, D * 4 * q:D * 4 * (q + 1)].rearrange("p (k o) -> p k o", k=4)
            toks.append(P.dma("p", dst, src, deps=deps, sem="sw" + str(q)))
        return toks

    prev_chain = [lI]
    tr_done = []
    for i in range(16):
        buf = x0a if i % 2 == 0 else x0b
        war = [tr_done[i - 2]] if i >= 2 else []
        lt = P.dma("q", buf, d["x0"][128 * i:128 * (i + 1), :], deps=war, sem="sx" + str(i % 2))
        t_s = P.v("reduce_sum", stats[:, 0:1], buf, axis=AX.X, deps=[lt] + prev_chain)
        t_sq = P.s("activation", sqjunk, buf, AF.Square, accum_out=stats[:, 1:2],
                   deps=[lt] + prev_chain)
        t_mu = P.v("tensor_scalar", stats[:, 2:3], stats[:, 0:1], 1.0 / D, None, ALU.mult,
                   deps=[t_s])
        t_m2 = P.v("tensor_mul", stats[:, 3:4], stats[:, 2:3], stats[:, 2:3], deps=[t_mu])
        t_va = P.v("scalar_tensor_tensor", stats[:, 4:5], stats[:, 1:2], 1.0 / D, stats[:, 3:4],
                   ALU.mult, ALU.subtract, deps=[t_sq, t_m2])
        t_lnv = P.s("activation", stats[:, 5:6], stats[:, 4:5], AF.Ln, bias=LN_EPS, deps=[t_va])
        t_rs = P.s("activation", stats[:, 6:7], stats[:, 5:6], AF.Exp, scale=-0.5, deps=[t_lnv])
        t_xn = P.v("tensor_scalar", xn_n, buf, stats[:, 2:3], stats[:, 6:7],
                   ALU.subtract, ALU.mult, deps=[t_rs])
        prev_chain = [t_xn, t_sq]
        evs = []
        for j4 in range(4):
            trs = []
            for jj in range(4):
                j = 4 * j4 + jj
                war2 = [evs[-1]] if evs else ([tr_done[i - 1]] if i >= 1 else [])
                trs.append(P.t("transpose", ps_tr[:, 128 * jj:128 * (jj + 1)],
                               xn_n[:, 128 * j:128 * (j + 1)], identb,
                               deps=[t_xn] + war2))
            dview = xnt.rearrange("p (k t) -> p k t", k=16)[:, 4 * j4:4 * j4 + 4,
                                                           128 * i:128 * (i + 1)]
            evs.append(P.v("tensor_copy", dview, ps_tr.rearrange("p (k t) -> p k t", k=4),
                           deps=trs))
        tr_done.append(evs[-1])
    xnt_ready = tr_done + [prev_chain[0]]
    so_xnt = P.dma("q", d["xnt"], xnt, deps=xnt_ready, sem="soxnt")

    def proj(wname, out_dram, kind, wdeps):
        wtoks = load_w(d[wname], wdeps)
        outs = []
        pv = {}
        for m in range(16):
            for nb in range(4):
                i = (4 * m + nb) % 2
                mms = []
                for k in range(16):
                    mms.append(P.t("matmul", ps_mm[i],
                                   wsb[:, D * k + 128 * m:D * k + 128 * (m + 1)],
                                   xnt[:, L * k + 512 * nb:L * k + 512 * (nb + 1)],
                                   deps=((wtoks + xnt_ready if not outs else []) +
                                         pv.get("e" + str(i), []) if k == 0 else []),
                                   **mmf(k, 16)))
                if kind == "rope":
                    ev = P.s("copy", qtb[i], ps_mm[i], deps=[mms[-1]] + pv.get("v" + str(i), []))
                    sw = P.t("matmul", ps_sw[i], swapPb, qtb[i],
                             deps=[ev] + pv.get("e2" + str(i), []), start=True, stop=True)
                    ev2 = P.s("copy", qsw[i], ps_sw[i], deps=[sw] + pv.get("v2" + str(i), []))
                    v1 = P.v("tensor_mul", ro1[i], qtb[i], ropeCb[:, 512 * nb:512 * (nb + 1)],
                             deps=[ev, lC] + pv.get("v3" + str(i), []))
                    v2 = P.v("tensor_mul", qsw[i], qsw[i], ropeSb[:, 512 * nb:512 * (nb + 1)],
                             deps=[ev2, lS])
                    v3 = P.v("tensor_add", qtb[i], ro1[i], qsw[i], deps=[v1, v2])
                    st = P.dma("s", out_dram[128 * m:128 * (m + 1), 512 * nb:512 * (nb + 1)],
                               qtb[i], deps=[v3], sem="sout" + str(i))
                    pv["e" + str(i)] = [v1]
                    pv["e" + str(i)] = [v3]
                    pv["v" + str(i)] = [st, v3]
                    pv["e2" + str(i)] = [v2]
                    pv["v2" + str(i)] = [v3]
                    pv["v3" + str(i)] = [v3]
                    outs.append(st)
                else:
                    lk = P.dma("q", kre[i],
                               d["ktr"][128 * m:128 * (m + 1), 512 * nb:512 * (nb + 1)],
                               deps=(pv.get("k" + str(i), []) if outs else wdeps),
                               sem="skre" + str(i))
                    v1 = P.v("scalar_tensor_tensor", kmv[i], ps_mm[i], -1.0, kre[i],
                             ALU.mult, ALU.add,
                             deps=[mms[-1], lk] + pv.get("s" + str(i), []))
                    v2 = P.v("tensor_scalar", kmv[i], kmv[i], bcolb[:, m:m + 1],
                             gcolb[:, m:m + 1], ALU.add, ALU.mult, deps=[v1, lg, lb])
                    st = P.dma("s", d["bmtp"][128 * m:128 * (m + 1), 512 * nb:512 * (nb + 1)],
                               kmv[i], deps=[v2], sem="sout" + str(i))
                    pv["e" + str(i)] = [v1]
                    pv["k" + str(i)] = [v1]
                    pv["s" + str(i)] = [st]
                    outs.append(st)
        return outs, wtoks

    oq, wq_toks = proj("wq", d["qtr"], "rope", [])
    ok, wk_toks = proj("wk", d["ktr"], "rope", oq)
    ov, wv_toks = proj("wv", None, "bmt", ok)
    return [so_xnt] + oq + ok + ov


def build_ph2(nc, P, d, stk=None, nsteps=128, stage=99):
    import os as _os
    _skip = _os.environ.get("DEVSKIP", "").split(",")
    sb = (lambda name, shape, dt: stk.enter_context(nc.sbuf_tensor("p2_" + name, shape, dt)).ap()) if stk is not None else (lambda name, shape, dt: nc.alloc_sbuf_tensor("p2_" + name, shape, dt).ap())
    ps = (lambda name, shape, dt: stk.enter_context(nc.psum_tensor("p2_" + name, shape, dt)).ap()) if stk is not None else (lambda name, shape, dt: nc.alloc_psum_tensor("p2_" + name, shape, dt).ap())

    qtrb = sb("qtrb", [128, 16 * 512], BF16)
    ktrb = sb("ktrb", [128, 16 * 512], BF16)
    bmtpb = sb("bmtpb", [128, 16 * 512], F32)
    W1b = sb("W1b", [128, 2048], BF16)
    w1f = sb("w1f", [128, 2048], F32)
    b1T = sb("b1T", [128, 16], F32)
    gcolb = sb("gcolb", [128, 16], F32)
    g2b = sb("g2b", [128, 16], F32)
    emxb = sb("emxb", [16, 512], BF16)
    ones128 = sb("ones128", [1, 128], F32)
    onesblk = sb("onesblk", [128, 2], F32)
    onesblkT = sb("onesblkT", [2, 128], F32)
    gbb3 = sb("gbb3", [3, 2048], F32)
    identb = sb("identb", [128, 128], BF16)
    attnTs = sb("attnTs", [16, 512], BF16)
    kmA = sb("kmA", [128, 256], BF16)
    kmB = sb("kmB", [128, 256], BF16)
    app = sb("app", [16, 512], BF16)
    appt = sb("appt", [16, 512], BF16)
    zb = sb("zb", [128, 512], F32)
    xh = sb("xh", [128, 256], F32)
    xht = sb("xht", [128, 256], F32)
    gx = sb("gx", [128, 512], F32)
    c1b = sb("c1b", [128, 256], F32)
    gradT = sb("gradT", [128, 256], BF16)
    gradN3 = sb("gradN3", [48, 2048], BF16)
    knS3 = sb("knS3", [48, 2048], BF16)
    app2 = sb("app2", [48, 256], BF16)
    statsS = sb("statsS", [2, 512], F32)
    rows = sb("rows", [2, 512], F32)
    rows2 = sb("rows2", [3, 512], F32)
    lnb = sb("lnb", [128, 512], F32)
    lnt = sb("lnt", [128, 256], F32)
    b1red = sb("b1red", [128, 16], F32)

    psA = ps("psA", [128, 512], F32)
    psB = ps("psB", [128, 512], F32)
    psX = ps("psX", [128, 512], F32)
    psT = ps("psT", [16, 1024], BF16)
    psW = ps("psW", [128, 2048], F32)

    LE = 1.0 / 16.0

    if "gcol" in _skip:
        lq0 = P.v("memset", gcolb, 1.0)
    else:
        lq0 = P.dma("q", gcolb, d["gcol"].rearrange("(k p) o -> p (k o)", p=128),
                    sem="scst", allow_slow_non_contiguous=True)
    l1 = P.v("memset", b1T, 0.0) if "b1t0" in _skip else P.dma("q", b1T, d["b1t0"], sem="scst")
    l2 = P.v("memset", emxb, 0.1) if "emx" in _skip else P.dma("q", emxb, d["emx"], sem="scst")
    l3 = P.v("memset", ones128, 1.0) if "ones" in _skip else P.dma("q", ones128, d["ones128"], sem="scst")
    l4 = P.v("memset", onesblk, 0.5) if "ones" in _skip else P.dma("q", onesblk, d["onesblk"], sem="scst")
    l4b = P.v("memset", onesblkT, 0.5) if "ones" in _skip else P.dma("q", onesblkT, d["onesblkT"], sem="scst")
    l5 = P.v("memset", gbb3, 0.2) if "gbb3" in _skip else P.dma("q", gbb3, d["gbb3"], sem="scst")
    l6 = P.v("memset", identb, 0.0) if "ident" in _skip else P.dma("q", identb, d["ident"], sem="scst")
    l7 = P.v("memset", w1f, 0.01) if "w1ld" in _skip else P.dma("q", w1f, d["w1blk"], sem="scst")
    vg2 = P.v("tensor_mul", g2b, gcolb, gcolb, deps=[lq0])
    vr0 = P.v("memset", rows2[:, 0:256], 1.0)
    vr1 = P.v("memset", rows2[:, 256:512], 0.0)
    sh0 = P.s("copy", W1b, w1f, deps=[l7])
    wzero = P.v("memset", psW[0:1, 0:1], 0.0) if "wzero" in _skip else P.v("memset", psW, 0.0)
    mz1 = P.v("memset", gradN3, 0.0)
    mz2 = P.v("memset", knS3, 0.0)
    mz3 = P.v("memset", app2, 0.0)
    mz4 = P.v("memset", kmA[64:128, :], 0.0)
    mz5 = P.v("memset", kmB[0:64, :], 0.0)

    shadow_tok = sh0
    wprev = wzero
    b1_tok = l1
    prev_store = None
    out_toks = []

    def pview(buf, k=16):
        return buf.rearrange("p (k t) -> p k t", k=k)

    nblocks = (nsteps + 31) // 32
    for blk in range(nblocks):
        bs = slice(512 * blk, 512 * (blk + 1))
        war = [prev_store] if prev_store else []
        _e = "p" if "blkload" in _skip else "q"
        lqb = P.dma(_e, qtrb.rearrange("p (k t) -> p k t", k=16),
                    d["qtr"][:, bs].rearrange("(k p) t -> p k t", p=128),
                    deps=war, sem="sqb")
        lkb = P.dma(_e, ktrb.rearrange("p (k t) -> p k t", k=16),
                    d["ktr"][:, bs].rearrange("(k p) t -> p k t", p=128),
                    deps=war, sem="skb")
        lbb = P.dma(_e, bmtpb.rearrange("p (k t) -> p k t", k=16),
                    d["bmtp"][:, bs].rearrange("(k p) t -> p k t", p=128),
                    deps=war, sem="sbb")
        blk_loads = [lqb, lkb, lbb]
        last_fin = None
        for ci in range(min(32, nsteps - 32 * blk)):
            tc = slice(16 * ci, 16 * (ci + 1))

            def qsl(buf, p):
                return buf[:, 512 * p + 16 * ci:512 * p + 16 * (ci + 1)]

            m1toks = []
            dep0 = [shadow_tok] + blk_loads + ([last_fin] if last_fin else [])
            if stage < 11:
                last_fin = shadow_tok
                continue
            _np16 = int(_os.environ.get("NP16", "16"))
            for p in range(_np16):
                w = W1b[:, 128 * p:128 * (p + 1)]
                m1toks.append(P.t("matmul", psA[0:128, 16 * p:16 * (p + 1)], w, qsl(ktrb, p),
                                  deps=(dep0 if p == 0 else []), start=(p == 0),
                                  stop=(p == _np16 - 1)))
                m1toks.append(P.t("matmul", psB[0:128, 16 * p:16 * (p + 1)], w, qsl(qtrb, p),
                                  deps=[], start=(p == 0),
                                  stop=(stage < 12 and p == _np16 - 1)))
            if stage < 11:
                last_fin = m1toks[-1]
                continue
            kv = ktrb.rearrange("p (k t) -> p k t", k=16)[:, :, tc]
            vkA = P.v("tensor_copy", kmA[0:64, :].rearrange("p (k t) -> p k t", k=16),
                      kv[0:64], deps=dep0 + [mz4, mz5])
            vkB = P.v("tensor_copy", kmB[64:128, :].rearrange("p (k t) -> p k t", k=16),
                      kv[64:128], deps=dep0)
            attoks = []
            for h in range(32):
                p, h2 = divmod(h, 2)
                km = kmA if h2 == 0 else kmB
                attoks.append(P.t("matmul", psX[0:16, 32 * p + 16 * h2:32 * p + 16 * (h2 + 1)],
                                  km[:, 16 * p:16 * (p + 1)], qsl(qtrb, p),
                                  deps=([vkA, vkB] if h == 0 else []),
                                  start=(h == 0), stop=(h == 31)))
            eat = P.s("copy", attnTs, psX[0:16, :], deps=attoks)
            ga1 = P.p("tensor_mul", appt, attnTs, emxb, deps=[eat, l2])
            ga2 = P.p("tensor_add", app, appt, emxb, deps=[ga1])
            avA = app.rearrange("q (a c t) -> q a c t", a=16, c=2)[:, :, 0, :]
            avB = app.rearrange("q (a c t) -> q a c t", a=16, c=2)[:, :, 1, :]
            capA = P.p("tensor_copy", app2[0:16, :].rearrange("q (a t) -> q a t", a=16),
                       avA, deps=[ga2, mz3])
            capB = P.p("tensor_copy", app2[32:48, :].rearrange("q (a t) -> q a t", a=16),
                       avB, deps=[ga2])
            if stage < 12:
                last_fin = capB
                continue
            b1bc = b1T.unsqueeze(2).broadcast_to([128, 16, 16])
            vz1 = P.v("tensor_add", pview(zb[:, 0:256]), pview(psA[:, 0:256]), b1bc,
                      deps=m1toks + [b1_tok])
            ssq = P.s("activation", zb[:, 256:512], zb[:, 0:256], AF.Square, deps=[vz1])
            if stage < 13:
                last_fin = ssq
                continue
            sttoks = []
            for p in range(16):
                rhs3 = zb.rearrange("q (a b t) -> q a b t", a=2, b=16)[:, :, p, :]
                sttoks.append(P.t("matmul", psX[0:2, 32 * p:32 * (p + 1)], onesblk, rhs3,
                                  deps=([ssq, eat, l4] if p == 0 else []),
                                  start=(p == 0), stop=(p == 15)))
            est = P.s("copy", statsS, psX[0:2, :], deps=sttoks)
            if stage < 14:
                last_fin = est
                continue
            muA = statsS.rearrange("q (a b t) -> q a b t", a=16, b=2)[:, :, 0, :]
            sqA = statsS.rearrange("q (a b t) -> q a b t", a=16, b=2)[:, :, 1, :]
            vmu = P.v("tensor_scalar", rows[:, 0:256].rearrange("q (a t) -> q a t", a=16),
                      muA, 1.0 / 64, None, ALU.mult, deps=[est])
            vm2 = P.v("tensor_mul", rows[:, 256:512], rows[:, 0:256], rows[:, 0:256],
                      deps=[vmu])
            vva = P.v("scalar_tensor_tensor",
                      rows[:, 256:512].rearrange("q (a t) -> q a t", a=16),
                      sqA, 1.0 / 64, rows[:, 256:512].rearrange("q (a t) -> q a t", a=16),
                      ALU.mult, ALU.subtract, deps=[vm2])
            sln = P.s("activation", rows[:, 256:512], rows[:, 256:512], AF.Ln,
                      bias=TTT_EPS, deps=[vva])
            srs = P.s("activation", rows[:, 256:512], rows[:, 256:512], AF.Exp,
                      scale=-0.5, deps=[sln])
            vnr = P.v("tensor_scalar", rows[:, 0:256], rows[:, 0:256], -1.0, None,
                      ALU.mult, deps=[srs])
            if stage < 15:
                last_fin = vnr
                continue
            bctoks = []
            for p in range(16):
                rhs3 = rows.rearrange("q (c a t) -> q c a t", c=2, a=16)[:, :, p, :]
                bctoks.append(P.t("matmul", psX[:, 32 * p:32 * (p + 1)], onesblkT, rhs3,
                                  deps=([vnr, est, l4b] if p == 0 else []),
                                  start=(p == 0), stop=(p == 15)))
            nrb = psX.rearrange("p (a c t) -> p a c t", a=16, c=2)[:, :, 0, :]
            rstdb = psX.rearrange("p (a c t) -> p a c t", a=16, c=2)[:, :, 1, :]
            vx1 = P.v("tensor_add", pview(xht), pview(zb[:, 0:256]), nrb, deps=bctoks)
            vx2 = P.v("tensor_mul", pview(xh), pview(xht), rstdb, deps=[vx1])
            if stage < 16:
                last_fin = vx2
                continue
            g2bc = g2b.unsqueeze(2).broadcast_to([128, 16, 16])
            vg1 = P.v("tensor_mul", pview(c1b), pview(xh), g2bc, deps=[vx2, vg2])
            bsl = bmtpb.rearrange("p (k t) -> p k t", k=16)[:, :, tc]
            vg1b = P.v("tensor_add", pview(gx[:, 0:256]), pview(c1b), bsl, deps=[vg1])
            vg3 = P.v("tensor_mul", gx[:, 256:512], gx[:, 0:256], xh, deps=[vg1b])
            if stage < 17:
                last_fin = vg3
                continue
            s2toks = []
            for p in range(16):
                rhs3 = gx.rearrange("q (a b t) -> q a b t", a=2, b=16)[:, :, p, :]
                s2toks.append(P.t("matmul", psA[0:2, 32 * p:32 * (p + 1)], onesblk, rhs3,
                                  deps=([vg3, vz1] if p == 0 else []),
                                  start=(p == 0), stop=(p == 15)))
            es2 = P.s("activation", statsS, psA[0:2, :], AF.Copy, scale=1.0 / 64,
                      deps=s2toks + [vmu, vva])
            b2toks = []
            for p in range(16):
                rhs3 = statsS.rearrange("q (a b t) -> q a b t", a=16, b=2)[:, p, :, :]
                b2toks.append(P.t("matmul", psA[:, 32 * p:32 * (p + 1)], onesblkT, rhs3,
                                  deps=([es2] if p == 0 else []), start=(p == 0), stop=(p == 15)))
            s1b = psA.rearrange("p (a c t) -> p a c t", a=16, c=2)[:, :, 0, :]
            s2b = psA.rearrange("p (a c t) -> p a c t", a=16, c=2)[:, :, 1, :]
            if stage < 18:
                last_fin = b2toks[-1]
                continue
            vc1 = P.v("tensor_sub", pview(c1b), s1b, pview(gx[:, 0:256]), deps=b2toks)
            vc2 = P.v("tensor_mul", pview(xht), pview(xh), s2b, deps=b2toks + [vx2])
            vc3 = P.v("tensor_add", c1b, c1b, xht, deps=[vc1, vc2])
            vgr = P.v("tensor_mul", pview(gradT), pview(c1b), rstdb, deps=[vc3])
            if stage < 2:
                last_fin = vgr
                shadow_tok = shadow_tok
                continue
            trdeps = [vgr]
            evt = []
            for half in range(2):
                trs = []
                for pp in range(8):
                    p = 8 * half + pp
                    trs.append(P.t("matmul", psT[0:16, 128 * pp:128 * (pp + 1)],
                                   gradT[:, 16 * p:16 * (p + 1)], identb,
                                   deps=(trdeps if pp == 0 else []),
                                   start=(pp == 0), stop=(pp == 7), is_transpose=True))
                pv = psT.rearrange("q (a c f) -> q a c f", a=8, c=2)
                hsl = slice(1024 * half, 1024 * (half + 1))
                e1 = P.s("copy",
                         gradN3[0:16, hsl].rearrange("q (a c f) -> q a c f", a=8, c=2)[:, :, 0, :],
                         pv[:, :, 0, :], deps=trs + [mz1])
                e2 = P.s("copy",
                         gradN3[32:48, hsl].rearrange("q (a c f) -> q a c f", a=8, c=2)[:, :, 1, :],
                         pv[:, :, 1, :], deps=trs)
                evt += [e1, e2]
                trdeps = [e2]
            kn_evt = []
            for half in range(2):
                trs = []
                for pp in range(8):
                    p = 8 * half + pp
                    trs.append(P.t("matmul", psT[0:16, 128 * pp:128 * (pp + 1)],
                                   qsl(ktrb, p), identb,
                                   deps=(trdeps if pp == 0 else []),
                                   start=(pp == 0), stop=(pp == 7), is_transpose=True))
                pv = psT.rearrange("q (a c f) -> q a c f", a=8, c=2)
                hsl = slice(1024 * half, 1024 * (half + 1))
                e1 = P.s("activation",
                         knS3[0:16, hsl].rearrange("q (a c f) -> q a c f", a=8, c=2)[:, :, 0, :],
                         pv[:, :, 0, :], AF.Copy, scale=LE, deps=trs + [mz2])
                e2 = P.s("activation",
                         knS3[32:48, hsl].rearrange("q (a c f) -> q a c f", a=8, c=2)[:, :, 1, :],
                         pv[:, :, 1, :], AF.Copy, scale=LE, deps=trs)
                kn_evt += [e1, e2]
                trdeps = [e2]
            if stage < 3:
                last_fin = kn_evt[-1]
                continue
            zctoks = []
            wutoks = []
            for p in range(16):
                gsl = gradN3[0:48, 128 * p:128 * (p + 1)]
                asl = app2[0:48, 16 * p:16 * (p + 1)]
                zctoks.append(P.t("matmul", psB[0:128, 16 * p:16 * (p + 1)],
                                  gsl, asl,
                                  deps=([evt[-1], capA, capB] if p == 0 else []),
                                  start=False, stop=(p == 15)))
                ksl = knS3[0:48, 128 * p:128 * (p + 1)]
                wutoks.append(P.t("matmul", psW[0:128, 128 * p:128 * (p + 1)],
                                  ksl, gsl,
                                  deps=([kn_evt[-1], wprev] if p == 0 else []),
                                  start=True, stop=True))
            if stage < 4:
                vwu = P.v("tensor_add", w1f, w1f, psW, deps=wutoks + [shadow_tok])
                wprev = vwu
                shadow_tok = P.s("copy", W1b, w1f, deps=[vwu] + m1toks)
                last_fin = shadow_tok
                continue
            vz2 = P.v("tensor_add", pview(lnb[:, 0:256]), pview(psB[:, 0:256]), b1bc,
                      deps=zctoks + [b1_tok, vgr])
            ssq2 = P.s("activation", lnb[:, 256:512], lnb[:, 0:256], AF.Square, deps=[vz2])
            st2 = []
            for p in range(16):
                rhs3 = lnb.rearrange("q (a b t) -> q a b t", a=2, b=16)[:, :, p, :]
                st2.append(P.t("matmul", psX[0:2, 32 * p:32 * (p + 1)], onesblk, rhs3,
                               deps=([ssq2, vgr, vx1] if p == 0 else []),
                               start=(p == 0), stop=(p == 15)))
            es3 = P.s("copy", statsS, psX[0:2, :], deps=st2 + [vc1, vc2])
            vmu2 = P.v("tensor_scalar", rows[:, 0:256].rearrange("q (a t) -> q a t", a=16),
                       statsS.rearrange("q (a b t) -> q a b t", a=16, b=2)[:, :, 0, :],
                       1.0 / 64, None, ALU.mult, deps=[es3, vnr])
            vm22 = P.v("tensor_mul", rows[:, 256:512], rows[:, 0:256], rows[:, 0:256],
                       deps=[vmu2])
            vva2 = P.v("scalar_tensor_tensor",
                       rows[:, 256:512].rearrange("q (a t) -> q a t", a=16),
                       statsS.rearrange("q (a b t) -> q a b t", a=16, b=2)[:, :, 1, :],
                       1.0 / 64, rows[:, 256:512].rearrange("q (a t) -> q a t", a=16),
                       ALU.mult, ALU.subtract, deps=[vm22])
            sln2 = P.s("activation", rows[:, 256:512], rows[:, 256:512], AF.Ln,
                       bias=TTT_EPS, deps=[vva2])
            srs2 = P.s("activation", rows[:, 256:512], rows[:, 256:512], AF.Exp,
                       scale=-0.5, deps=[sln2])
            vq1 = P.v("scalar_tensor_tensor", rows2[0:2, 0:256], rows[:, 0:256], -1.0,
                      rows[:, 256:512], ALU.mult, ALU.mult, deps=[srs2, vr0, vr1])
            vq2 = P.v("tensor_copy", rows2[0:2, 256:512], rows[:, 256:512], deps=[srs2])
            b3toks = []
            for p in range(16):
                rhs3 = rows2.rearrange("q (c a t) -> q c a t", c=2, a=16)[:, :, p, :]
                b3toks.append(P.t("matmul", psX[:, 32 * p:32 * (p + 1)],
                                  gbb3[:, 128 * p:128 * (p + 1)], rhs3,
                                  deps=([vq1, vq2, es3, l5] if p == 0 else []),
                                  start=(p == 0), stop=(p == 15)))
            qb = psX.rearrange("p (a c t) -> p a c t", a=16, c=2)[:, :, 0, :]
            pb = psX.rearrange("p (a c t) -> p a c t", a=16, c=2)[:, :, 1, :]
            vfl = P.v("tensor_mul", pview(lnt), pview(lnb[:, 0:256]), pb, deps=b3toks)
            vf2 = P.v("tensor_add", pview(lnt), pview(lnt), qb, deps=[vfl])
            qv = qtrb.rearrange("p (k t) -> p k t", k=16)[:, :, tc]
            vfin = P.v("tensor_add", qv, pview(lnt), qv, deps=[vf2])
            vbr = P.v("reduce_sum", b1red, pview(gradT), axis=AX.X, deps=[vgr, vz2])
            b1_tok = P.v("scalar_tensor_tensor", b1T, b1red, LE, b1T, ALU.mult, ALU.add,
                         deps=[vbr, vz1, vz2])
            vwu = P.v("tensor_add", w1f, w1f, psW, deps=wutoks + [shadow_tok])
            wprev = vwu
            shadow_tok = P.s("copy", W1b, w1f, deps=[vwu] + m1toks)
            last_fin = vfin
        if "tout" in _skip:
            st = P.dma("q", d["tout"][0:128, 0:16], qtrb[:, 0:16], deps=[last_fin], sem="stout")
        else:
            st = P.dma("q", d["tout"][:, bs].rearrange("(k p) t -> p k t", p=128),
                       qtrb.rearrange("p (k t) -> p k t", k=16),
                       deps=[last_fin], sem="stout")
        prev_store = st
        out_toks.append(st)
    if "w1dbg" in d and "dbg" not in _skip:
        out_toks.append(P.dma("q", d["w1dbg"], W1b, deps=[shadow_tok], sem="sdbg"))
        out_toks.append(P.dma("q", d["b1dbg"], b1T, deps=[b1_tok], sem="sdbg"))
    dbgbufs = dict(zb=zb, xh=xh, gx=gx, c1b=c1b, gradT=gradT, gradN3=gradN3,
                   knS3=knS3, app2=app2, attnTs=attnTs, app=app, lnb=lnb, lnt=lnt,
                   statsS=statsS, rows=rows, rows2=rows2, b1red=b1red, w1f=w1f)
    for nm, buf in dbgbufs.items():
        if "dbg_" + nm in d:
            out_toks.append(P.dma("q", d["dbg_" + nm], buf, deps=[shadow_tok, b1_tok],
                                  sem="sdbg"))
    return out_toks





def build_ph3a(nc, P, d, stk=None):
    sb = (lambda n, s, t: stk.enter_context(nc.sbuf_tensor("p3a_" + n, s, t)).ap()) if stk is not None else (lambda n, s, t: nc.alloc_sbuf_tensor("p3a_" + n, s, t).ap())
    ps = (lambda n, s, t: stk.enter_context(nc.psum_tensor("p3a_" + n, s, t)).ap()) if stk is not None else (lambda n, s, t: nc.alloc_psum_tensor("p3a_" + n, s, t).ap())
    wsb = sb("wsb", [128, 16 * D], BF16)
    tsl = sb("tsl", [128, 16 * 512], BF16)
    ev1 = sb("ev1", [128, 512], BF16)
    ev2 = sb("ev2", [128, 512], BF16)
    psm = [ps("psm0", [128, 512], F32), ps("psm1", [128, 512], F32)]

    wtoks = []
    for q in range(4):
        src = d["wo"][512 * q:512 * (q + 1), :].rearrange("(k p) o -> p k o", p=128)
        dst = wsb[:, D * 4 * q:D * 4 * (q + 1)].rearrange("p (k o) -> p k o", k=4)
        wtoks.append(P.dma("p", dst, src, sem="w" + str(q)))
    outs = []
    prev_st = {0: [], 1: []}
    prev_lt = []
    for nb in range(4):
        lt = P.dma("q", tsl.rearrange("p (k t) -> p k t", k=16),
                   d["tout"][:, 512 * nb:512 * (nb + 1)].rearrange("(k p) t -> p k t", p=128),
                   deps=prev_lt, sem="stsl")
        prev_lt = []
        for m in range(16):
            par = m % 2
            pp = psm[par]
            evb = ev1 if par == 0 else ev2
            mms = []
            for k in range(16):
                mms.append(P.t("matmul", pp, wsb[:, D * k + 128 * m:D * k + 128 * (m + 1)],
                               tsl[:, 512 * k:512 * (k + 1)],
                               deps=(wtoks + [lt] + prev_st[par] if k == 0 else []),
                               **mmf(k, 16)))
            e = P.s("copy", evb, pp, deps=mms + prev_st[par])
            st = P.dma("q", d["brT"][128 * m:128 * (m + 1), 512 * nb:512 * (nb + 1)], evb,
                       deps=[e], sem="sbr" + str(par))
            prev_st[par] = [st]
            prev_lt.append(st)
            outs.append(st)
    return outs


def build_ph3b(nc, P, d, stk=None):
    sb = (lambda n, s, t: stk.enter_context(nc.sbuf_tensor("p3b_" + n, s, t)).ap()) if stk is not None else (lambda n, s, t: nc.alloc_sbuf_tensor("p3b_" + n, s, t).ap())
    ps = (lambda n, s, t: stk.enter_context(nc.psum_tensor("p3b_" + n, s, t)).ap()) if stk is not None else (lambda n, s, t: nc.alloc_psum_tensor("p3b_" + n, s, t).ap())
    wtT = sb("wtT", [128, 16 * D], BF16)
    wld = [sb("wld0", [128, D], F32), sb("wld1", [128, D], F32)]
    wq1 = sb("wq1", [128, D], F32)
    wq2 = sb("wq2", [128, D], F32)
    wqb = sb("wqb", [128, D], BF16)
    absn = sb("absn", [128, 16], F32)
    acol = sb("acol", [128, 1], F32)
    rscol = sb("rscol", [128, 1], F32)
    sccol = sb("sccol", [128, 1], F32)
    sval = sb("sval", [1, 1], F32)
    svs = sb("svs", [1, 1], F32)
    onesc = sb("onesc", [128, 1], F32)
    ones128 = sb("ones128", [1, 128], F32)
    xslc = sb("xslc", [128, 16 * 512], BF16)
    brs = sb("brs", [128, 512], BF16)
    trb = sb("trb", [128, 512], BF16)
    bfull = sb("bfull", [128, 512], F32)
    x1buf = [sb("x1b" + str(i), [128, D], F32) for i in range(4)]
    x0t = sb("x0t", [128, D], F32)
    x1t = sb("x1t", [128, D], F32)
    x1n = sb("x1n", [128, D], BF16)
    fbcol = sb("fbcol", [128, 16], F32)
    stats = sb("stats", [128, 8], F32)
    sqjunk = sb("sqjunk", [128, D], F32)
    identb = sb("identb", [128, 128], BF16)
    identf = sb("identf", [128, 128], F32)
    ps_m = ps("ps_m", [128, 512], F32)
    ps_s = ps("ps_s", [128, 512], F32)
    ps_t = ps("ps_t", [128, 512], F32)
    ps_tb = ps("ps_tb", [128, 512], BF16)

    lI = P.dma("q", identb, d["ident"], sem="sc")
    lIf = P.dma("q", identf, d["identf"], sem="sc")
    lfb = P.dma("q", fbcol, d["fbias"].rearrange("(k p) o -> p (k o)", p=128), sem="sc",
                allow_slow_non_contiguous=True)
    lo = P.dma("q", ones128, d["ones128"], sem="sc")
    oz = P.v("memset", onesc, 1.0)
    abst = []
    for i in range(16):
        buf = wld[i % 2]
        war = [abst[i - 2]] if i >= 2 else []
        lw = P.dma("q", buf, d["fw"][128 * i:128 * (i + 1), :], deps=war, sem="sw" + str(i % 2))
        abst.append(P.s("activation", sqjunk, buf, AF.Abs, accum_out=absn[:, i:i + 1],
                        deps=[lw] + ([abst[i - 1]] if i else [])))
    r1 = P.v("reduce_sum", acol, absn, axis=AX.X, deps=abst)
    mm1 = P.t("matmul", ps_s[0:1, 0:1], onesc, acol, deps=[r1, oz], start=True, stop=True)
    e1 = P.s("activation", sval, ps_s[0:1, 0:1], AF.Copy, scale=1.0 / (D * D), deps=[mm1])
    e5 = P.s("activation", svs, ps_s[0:1, 0:1], AF.Copy, scale=1.0 / (D * D), deps=[mm1])
    e2 = P.v("tensor_scalar", sval, sval, 1e-10, None, ALU.add, deps=[e1])
    e3 = P.v("reciprocal", sval, sval, deps=[e2])
    mm2 = P.t("matmul", ps_s[0:128, 4:5], ones128, sval, deps=[e3, lo, e1, e5], start=True,
              stop=True)
    eb = P.v("tensor_copy", rscol, ps_s[0:128, 4:5], deps=[mm2])
    mm3 = P.t("matmul", ps_s[0:128, 6:7], ones128, svs, deps=[e5, mm2, eb], start=True, stop=True)
    eb2 = P.v("tensor_copy", sccol, ps_s[0:128, 6:7], deps=[mm3])
    trdone = []
    for i in range(16):
        buf = wld[i % 2]
        war = [trdone[i - 2]] if i >= 2 else [abst[-1]]
        lw = P.dma("q", buf, d["fw"][128 * i:128 * (i + 1), :], deps=war, sem="sw" + str(i % 2))
        q1 = P.v("tensor_scalar", wq1, buf, rscol, None, ALU.mult,
                 deps=[lw, eb] + ([trdone[i - 1]] if i else []))
        q2 = P.v("tensor_scalar", wq2, wq1, 0.5, None, ALU.is_gt, deps=[q1])
        q3 = P.v("tensor_scalar", wq1, wq1, -0.5, None, ALU.is_lt, deps=[q2])
        q4 = P.v("tensor_sub", wqb, wq2, wq1, deps=[q3])
        evs = []
        for j4 in range(4):
            trs = []
            for jj in range(4):
                j = 4 * j4 + jj
                war2 = [evs[-1]] if evs else ([trdone[i - 1]] if i else [])
                trs.append(P.t("matmul", ps_tb[:, 128 * jj:128 * (jj + 1)],
                               wqb[:, 128 * j:128 * (j + 1)], identb,
                               deps=[q4, lI] + war2, start=(jj == 0), stop=(jj == 3),
                               is_transpose=True))
            dv = wtT.rearrange("p (k o) -> p k o", k=16)[:, 4 * j4:4 * j4 + 4,
                                                        128 * i:128 * (i + 1)]
            evs.append(P.v("tensor_copy", dv, ps_tb.rearrange("p (k o) -> p k o", k=4),
                           deps=trs))
        trdone.append(evs[-1])
    outs = []
    prev = {}
    for nb in range(4):
        lx = P.dma("q", xslc.rearrange("p (k t) -> p k t", k=16),
                   d["xnt"].rearrange("p (k t) -> p k t", k=16)[:, :, 512 * nb:512 * (nb + 1)],
                   deps=prev.get("m", []), sem="sxs")
        for m in range(16):
            mms = []
            for k in range(16):
                mms.append(P.t("matmul", ps_m, wtT[:, D * k + 128 * m:D * k + 128 * (m + 1)],
                               xslc[:, 512 * k:512 * (k + 1)],
                               deps=([trdone[-1], lx] + prev.get("m", []) if k == 0 else []),
                               **mmf(k, 16)))
            lb = P.dma("q", brs, d["brT"][128 * m:128 * (m + 1), 512 * nb:512 * (nb + 1)],
                       deps=prev.get("b", []), sem="sbrl")
            c1 = P.v("scalar_tensor_tensor", bfull, ps_m, sccol, brs, ALU.mult, ALU.add,
                     deps=mms + [lb, eb2])
            c2 = P.v("tensor_scalar", bfull, bfull, fbcol[:, m:m + 1], None, ALU.add,
                     deps=[c1, lfb])
            prev["m"] = [c2]
            prev["b"] = [c2]
            lastav = prev.get("lastav", [])
            for tt in range(4):
                trt = P.t("matmul", ps_t[:, 128 * tt:128 * (tt + 1)],
                          bfull[:, 128 * tt:128 * (tt + 1)], identf,
                          deps=[c2, lIf] + prev.get("t" + str(tt), []) + lastav,
                          start=True, stop=True, is_transpose=True)
                av = P.v("tensor_copy", x1buf[tt][:, 128 * m:128 * (m + 1)],
                         ps_t[:, 128 * tt:128 * (tt + 1)], deps=[trt])
                lastav = [av]
                prev["t" + str(tt)] = [av]
            prev["lastav"] = lastav
        for tt in range(4):
            i = 4 * nb + tt
            lx0 = P.dma("q", x0t, d["x0"][128 * i:128 * (i + 1), :],
                        deps=prev.get("x0", []), sem="sx0")
            a1 = P.v("tensor_add", x1t, x1buf[tt], x0t,
                     deps=[lx0] + prev["t" + str(tt)] + prev.get("x1w", []))
            prev["x0"] = [a1]
            prev["t" + str(tt)].append(a1)
            st1 = P.dma("q", d["x1"][128 * i:128 * (i + 1), :], x1t, deps=[a1], sem="sx1")
            t_s = P.v("reduce_sum", stats[:, 0:1], x1t, axis=AX.X,
                      deps=[a1] + prev.get("stat", []))
            t_sq = P.s("activation", sqjunk, x1t, AF.Square, accum_out=stats[:, 1:2],
                       deps=[a1] + prev.get("sq", []))
            t_mu = P.v("tensor_scalar", stats[:, 2:3], stats[:, 0:1], 1.0 / D, None,
                       ALU.mult, deps=[t_s])
            t_m2 = P.v("tensor_mul", stats[:, 3:4], stats[:, 2:3], stats[:, 2:3], deps=[t_mu])
            t_va = P.v("scalar_tensor_tensor", stats[:, 4:5], stats[:, 1:2], 1.0 / D,
                       stats[:, 3:4], ALU.mult, ALU.subtract, deps=[t_sq, t_m2])
            t_ln = P.s("activation", stats[:, 5:6], stats[:, 4:5], AF.Ln, bias=LN_EPS,
                       deps=[t_va])
            t_rs = P.s("activation", stats[:, 6:7], stats[:, 5:6], AF.Exp, scale=-0.5,
                       deps=[t_ln])
            t_xn = P.v("tensor_scalar", x1n, x1t, stats[:, 2:3], stats[:, 6:7],
                       ALU.subtract, ALU.mult, deps=[t_rs, st1])
            prev["x1w"] = [t_xn]
            prev["sq"] = [t_sq]
            prev["stat"] = [t_xn]
            for j4 in range(4):
                trs = []
                for jj in range(4):
                    j = 4 * j4 + jj
                    trs.append(P.t("matmul", ps_tb[:, 128 * jj:128 * (jj + 1)],
                                   x1n[:, 128 * j:128 * (j + 1)], identb,
                                   deps=[t_xn] + prev.get("xtr", []), start=(jj == 0),
                                   stop=(jj == 3), is_transpose=True))
                bb2 = P.v("tensor_copy", trb.rearrange("p (f t) -> p f t", f=4),
                          ps_tb.rearrange("p (f t) -> p f t", f=4),
                          deps=trs + prev.get("xtrw", []))
                std = P.dma("q",
                            d["xn1t"].rearrange("p (k t) -> p k t", k=16)[:, 4 * j4:4 * j4 + 4,
                                                                          128 * i:128 * (i + 1)],
                            trb.rearrange("p (f t) -> p f t", f=4), deps=[bb2], sem="sxnw")
                prev["xtr"] = [bb2]
                prev["xtrw"] = [std]
                outs.append(std)
            outs.append(st1)
    return outs


def build_ph4(nc, P, d, stk=None):
    sb = (lambda n, s, t: stk.enter_context(nc.sbuf_tensor("p4_" + n, s, t)).ap()) if stk is not None else (lambda n, s, t: nc.alloc_sbuf_tensor("p4_" + n, s, t).ap())
    ps = (lambda n, s, t: stk.enter_context(nc.psum_tensor("p4_" + n, s, t)).ap()) if stk is not None else (lambda n, s, t: nc.alloc_psum_tensor("p4_" + n, s, t).ap())
    xn1t = sb("xn1t", [128, 16 * L], BF16)
    xcb = sb("xcb", [128, L], BF16)
    xcsl = [sb("xcsl0", [128, 512], BF16), sb("xcsl1", [128, 512], BF16)]
    wldA = sb("wldA", [128, D], F32)
    wldB = sb("wldB", [128, D], F32)
    wcolA = sb("wcolA", [128, 2048], BF16)
    wcolB = sb("wcolB", [128, 2048], BF16)
    wbf = sb("wbf", [128, D], BF16)
    hsb = sb("hsb", [128, L], F32)
    cacc = sb("cacc", [128, L], F32)
    xpwT = sb("xpwT", [128, 32 * 160], BF16)
    xpld = sb("xpld", [128, 4096], F32)
    wrT = sb("wrT", [128, 16 * 16], BF16)
    wrld = sb("wrld", [16, D], F32)
    brcol = sb("brcol", [16, 1], F32)
    cwcol = sb("cwcol", [128, 32 * 4], F32)
    cbcol = sb("cbcol", [128, 32], F32)
    lgT = sb("lgT", [16, L], F32)
    lgN = sb("lgN", [128, 16], F32)
    lgW = sb("lgW", [128, 16], F32)
    mx = sb("mx", [128, 1], F32)
    msk = sb("msk", [128, 16], F32)
    mskT = sb("mskT", [16, L], F32)
    identb = sb("identb", [128, 128], BF16)
    identf = sb("identf", [128, 128], F32)
    tsev = sb("tsev", [128, 512], BF16)
    bcev = sb("bcev", [32, 512], F32)
    ps_m = [ps("ps_m0", [128, 512], F32), ps("ps_m1", [128, 512], F32)]
    ps_r = ps("ps_r", [128, 512], F32)
    ps_tb = ps("ps_tb", [128, 512], BF16)
    ps_ts = ps("ps_ts", [128, 512], F32)
    ps_bc = ps("ps_bc", [32, 512], F32)

    lI = P.dma("q", identb, d["ident"], sem="sc")
    lIf = P.dma("q", identf, d["identf"], sem="sc")
    lxn = P.dma("q", xn1t, d["xn1t"], sem="sxn")
    lbr = P.dma("q", brcol, d["br"], sem="sc")
    lcw = P.dma("q", cwcol.rearrange("p (k c) -> p k c", k=32),
                d["convw"].rearrange("(k p) c -> p k c", p=128), sem="sc")
    lcb = P.dma("q", cbcol, d["convb"].rearrange("(k p) o -> p (k o)", p=128), sem="sc",
                allow_slow_non_contiguous=True)

    lwr = P.dma("q", wrld, d["wr"], sem="sc")
    wrbf = sb("wrbf", [16, D], BF16)
    cwr = P.v("tensor_copy", wrbf, wrld, deps=[lwr])
    wr_tr = []
    for k in range(16):
        t1 = P.t("matmul", ps_tb[:, 0:16], wrbf[0:16, 128 * k:128 * (k + 1)],
                 identb[0:16, 0:16],
                 deps=[cwr, lI] + wr_tr[-1:], start=True, stop=True, is_transpose=True)
        wr_tr.append(P.v("tensor_copy", wrT[:, 16 * k:16 * (k + 1)], ps_tb[:, 0:16],
                         deps=[t1]))
    mtoks = []
    for nb in range(4):
        mms = []
        for k in range(16):
            mms.append(P.t("matmul", ps_r[0:16, 0:512], wrT[:, 16 * k:16 * (k + 1)],
                           xn1t[:, L * k + 512 * nb:L * k + 512 * (nb + 1)],
                           deps=(wr_tr + [lxn] + mtoks[-1:] if k == 0 else []),
                           **mmf(k, 16)))
        mtoks.append(P.v("tensor_scalar", lgT[:, 512 * nb:512 * (nb + 1)],
                         ps_r[0:16, 0:512], brcol, None, ALU.add, deps=mms + [lbr]))
    mprev = mtoks[-1:]
    mskb = sb("mskb", [128, 16], BF16)
    for i in range(16):
        t1 = P.t("matmul", ps_ts[:, 0:16], lgT[0:16, 128 * i:128 * (i + 1)],
                 identf[0:16, 0:16],
                 deps=mprev + [lIf], start=True, stop=True, is_transpose=True)
        c1 = P.v("tensor_copy", lgN, ps_ts[:, 0:16], deps=[t1] + mprev)
        c2 = P.v("tensor_copy", lgW, lgN, deps=[c1])
        rprev = [c2]
        for it in range(3):
            r1 = P.v("reduce_max", mx, lgW, axis=AX.X, deps=rprev)
            r2 = P.v("tensor_scalar", msk, lgW, mx, None, ALU.is_ge, deps=[r1])
            r3 = P.v("scalar_tensor_tensor", lgW, msk, -1e30, lgW, ALU.mult, ALU.add,
                     deps=[r2])
            rprev = [r3]
        r4 = P.v("reduce_max", mx, lgW, axis=AX.X, deps=rprev)
        r5 = P.v("tensor_scalar", msk, lgN, mx, None, ALU.is_ge, deps=[r4, c2])
        cb = P.v("tensor_copy", mskb, msk, deps=[r5])
        t2 = P.t("matmul", ps_tb[0:16, 16:144], mskb, identb, deps=[cb] + mprev + [lI],
                 start=True, stop=True, is_transpose=True)
        mprev = [P.v("tensor_copy", mskT[0:16, 128 * i:128 * (i + 1)], ps_tb[0:16, 16:144],
                     deps=[t2])]
    smk = P.dma("q", d["maskt"], mskT, deps=mprev, sem="smk")

    lxp = P.dma("q", xpld, d["xpw"][0:128, :], sem="sxp")
    xpbf = sb("xpbf", [128, 4096], BF16)
    xpbf2 = sb("xpbf2", [32, 4096], BF16)
    xpld2 = sb("xpld2", [32, 4096], F32)
    cx1 = P.v("tensor_copy", xpbf, xpld, deps=[lxp])
    lxp2 = P.dma("q", xpld2, d["xpw"][128:160, :], sem="sxp2")
    cx2 = P.v("tensor_copy", xpbf2, xpld2, deps=[lxp2])
    xp_tr = []
    for k in range(32):
        t1 = P.t("matmul", ps_tb[:, 144:272], xpbf[:, 128 * k:128 * (k + 1)], identb,
                 deps=[cx1, lI] + (xp_tr[-1:] if xp_tr else mprev), start=True, stop=True,
                 is_transpose=True)
        e1 = P.v("tensor_copy", xpwT[:, 160 * k:160 * k + 128], ps_tb[:, 144:272], deps=[t1])
        t2 = P.t("matmul", ps_tb[:, 272:304], xpbf2[0:32, 128 * k:128 * (k + 1)],
                 identb[0:32, 0:32],
                 deps=[cx2, e1], start=True, stop=True, is_transpose=True)
        e2 = P.v("tensor_copy", xpwT[:, 160 * k + 128:160 * (k + 1)], ps_tb[:, 272:304],
                 deps=[t2])
        xp_tr.append(e2)

    outs = []
    wprev = {0: [], 1: []}
    cprev = []
    prev_trs = xp_tr[-1:]
    for m in range(64):
        par = m % 2
        wldb = wldA if par == 0 else wldB
        wcol = wcolA if par == 0 else wcolB
        lw = P.dma("q", wldb, d["ipw"][128 * m:128 * (m + 1), :], deps=wprev[par],
                   sem="swp" + str(par))
        cb_ = P.v("tensor_copy", wbf, wldb, deps=[lw] + prev_trs)
        trs = []
        for j4 in range(4):
            tt4 = []
            for jj in range(4):
                j = 4 * j4 + jj
                tt4.append(P.t("matmul", ps_tb[:, 128 * jj:128 * (jj + 1)],
                               wbf[:, 128 * j:128 * (j + 1)], identb,
                               deps=([cb_] + (trs[-1:] if trs else prev_trs) if jj == 0
                                     else []), start=(jj == 0),
                               stop=(jj == 3), is_transpose=True))
            dv = wcol.rearrange("p (k o) -> p k o", k=16)[:, 4 * j4:4 * j4 + 4, :]
            trs.append(P.v("tensor_copy", dv, ps_tb.rearrange("p (k o) -> p k o", k=4),
                           deps=tt4))
        prev_trs = [trs[-1]]
        mmdone = []
        for nb in range(4):
            pp = ps_m[nb % 2]
            mms = []
            for k in range(16):
                mms.append(P.t("matmul", pp, wcol[:, 128 * k:128 * (k + 1)],
                               xn1t[:, L * k + 512 * nb:L * k + 512 * (nb + 1)],
                               deps=(trs + [lxn] + wprev[nb % 2] if k == 0 else []),
                               **mmf(k, 16)))
            ev = P.s("copy", hsb[:, 512 * nb:512 * (nb + 1)], pp, deps=mms + cprev[-1:])
            wprev[nb % 2] = [ev]
            mmdone.append(ev)
        wprev[par] = wprev[par] + [trs[-1]]
        if m < 32:
            a0 = P.v("tensor_scalar", cacc, hsb, cwcol[:, 4 * m + 3:4 * m + 4], None,
                     ALU.mult, deps=mmdone + cprev[-1:])
            a1 = P.v("scalar_tensor_tensor", cacc[:, 1:L], hsb[:, 0:L - 1],
                     cwcol[:, 4 * m + 2:4 * m + 3], cacc[:, 1:L], ALU.mult, ALU.add,
                     deps=[a0, lcw])
            a2 = P.v("scalar_tensor_tensor", cacc[:, 2:L], hsb[:, 0:L - 2],
                     cwcol[:, 4 * m + 1:4 * m + 2], cacc[:, 2:L], ALU.mult, ALU.add,
                     deps=[a1])
            a3 = P.v("scalar_tensor_tensor", cacc[:, 3:L], hsb[:, 0:L - 3],
                     cwcol[:, 4 * m + 0:4 * m + 1], cacc[:, 3:L], ALU.mult, ALU.add,
                     deps=[a2])
            s1 = P.s("activation", cacc, cacc, AF.Silu, bias=cbcol[:, m:m + 1],
                     deps=[a3, lcb])
            st = P.dma("q", d["xct"][128 * m:128 * (m + 1), :], cacc, deps=[s1], sem="sxc")
            cc = P.v("tensor_copy", xcb, cacc, deps=[s1] + cprev[1:2])
            st2 = P.dma("q", d["xcbt"][128 * m:128 * (m + 1), :], xcb, deps=[cc], sem="sxcb")
            cprev = [st, st2]
            outs.append(st)
        else:
            s1 = P.s("activation", cacc, hsb, AF.Silu, deps=mmdone + cprev[-1:])
            st = P.dma("q", d["sgt"][128 * (m - 32):128 * (m - 31), :], cacc, deps=[s1],
                       sem="ssg")
            cprev = [st]
            outs.append(st)
    tprev = []
    lprev = {0: cprev[1:2], 1: cprev[1:2]}
    for nb in range(4):
        mms = []
        mms2 = []
        for k in range(32):
            par = k % 2
            lxs = P.dma("q", xcsl[par],
                        d["xcbt"][128 * k:128 * (k + 1), 512 * nb:512 * (nb + 1)],
                        deps=lprev[par], sem="sxl" + str(par))
            mms.append(P.t("matmul", ps_ts, xpwT[:, 160 * k:160 * k + 128],
                           xcsl[par],
                           deps=(xp_tr + [lxs] + tprev if k == 0 else [lxs]),
                           **mmf(k, 32)))
            mms2.append(P.t("matmul", ps_bc, xpwT[:, 160 * k + 128:160 * (k + 1)],
                            xcsl[par], deps=[], **mmf(k, 32)))
            lprev[par] = [mms2[-1]]
        e1 = P.s("copy", tsev, ps_ts, deps=mms + tprev)
        e2 = P.v("tensor_copy", bcev, ps_bc, deps=mms2 + tprev)
        st1 = P.dma("q", d["tst"][:, 512 * nb:512 * (nb + 1)], tsev, deps=[e1], sem="sts")
        st2 = P.dma("q", d["btct"][:, 512 * nb:512 * (nb + 1)], bcev, deps=[e2], sem="sbc")
        tprev = [st1, st2]
        outs += [st1, st2]
    outs.append(smk)
    return outs


def build_ph5(nc, P, d, stk=None, TB=512):
    import os as _os
    _ntb = int(_os.environ.get("NTB5", "99"))
    _ndk = int(_os.environ.get("NDK5", "32"))
    _sk5 = _os.environ.get("SKIP5", "").split(",")
    sb = (lambda n, s, t: stk.enter_context(nc.sbuf_tensor("p5_" + n, s, t)).ap()) if stk is not None else (lambda n, s, t: nc.alloc_sbuf_tensor("p5_" + n, s, t).ap())
    ps = (lambda n, s, t: stk.enter_context(nc.psum_tensor("p5_" + n, s, t)).ap()) if stk is not None else (lambda n, s, t: nc.alloc_psum_tensor("p5_" + n, s, t).ap())
    NTB = L // TB
    tst = sb("tst", [128, L], BF16)
    dtwT = sb("dtwT", [128, 32 * 128], BF16)
    dtld = sb("dtld", [128, 128], F32)
    dtbf = sb("dtbf", [128, 128], BF16)
    dtbcol = sb("dtbcol", [128, 32], F32)
    acol = sb("acol", [128, 32 * 16], F32)
    aload = sb("aload", [128, 16], F32)
    dcol = sb("dcol", [128, 32], F32)
    btcr = sb("btcr", [1, 32 * 512], F32)
    ones128 = sb("ones128", [1, 128], F32)
    Bbc = sb("Bbc", [128, 16 * TB], BF16)
    Cbc = sb("Cbc", [128, 16 * TB], BF16)
    xc = [sb("xc0", [128, TB], F32), sb("xc1", [128, TB], F32)]
    sg = [sb("sg0", [128, TB], F32), sb("sg1", [128, TB], F32)]
    dt_ = sb("dt", [128, TB], F32)
    dtev = sb("dtev", [128, TB], F32)
    dtt = sb("dtt", [128, TB], F32)
    dtx = sb("dtx", [128, TB], F32)
    dAall = sb("dAall", [128, 16 * TB], F32)
    uu = sb("uu", [128, TB], F32)
    hh = [sb("hh0", [128, TB], F32), sb("hh1", [128, TB], F32)]
    ytmp = [sb("ytmp0", [128, TB], F32), sb("ytmp1", [128, TB], F32)]
    yacc = sb("yacc", [128, TB], F32)
    y3 = sb("y3", [128, TB], BF16)
    state = sb("state", [128, 32 * 16], F32)
    ps_dt = ps("ps_dt", [128, 512], F32)
    ps_bc = ps("ps_bc", [128, 512], F32)
    ps_tb5 = ps("ps_tb5", [128, 128], BF16)

    identb5 = sb("identb5", [128, 128], BF16)
    lI5 = P.dma("q", identb5, d["ident"], sem="sc5")
    l1 = P.dma("q", tst, d["tst"], sem="sc5")
    l2 = P.dma("q", dtbcol, d["dtb"].rearrange("(k p) o -> p (k o)", p=128), sem="sc5",
               allow_slow_non_contiguous=True)
    l3 = P.dma("q", dcol, d["dcol"].rearrange("(k p) o -> p (k o)", p=128), sem="sc5",
               allow_slow_non_contiguous=True)
    l5 = P.dma("q", ones128, d["ones128"], sem="sc5")
    zst = P.v("memset", state, 0.0)
    atoks = []
    if "acol" in _sk5:
        atoks.append(P.v("memset", acol, -1.0))
    else:
        for k in range(32):
            la = P.dma("q", aload, d["alog"][128 * k:128 * (k + 1), :],
                       deps=atoks[-1:], sem="sal")
            e1 = P.s("activation", acol[:, 16 * k:16 * (k + 1)], aload, AF.Exp,
                     deps=[la] + atoks[-1:])
            atoks.append(P.v("tensor_scalar", acol[:, 16 * k:16 * (k + 1)],
                             acol[:, 16 * k:16 * (k + 1)], -1.0, None, ALU.mult, deps=[e1]))
    dtoks = []
    if "dtw" in _sk5:
        dtoks.append(P.v("memset", dtwT, 0.01))
    for k in (range(32) if "dtw" not in _sk5 else []):
        ld = P.dma("q", dtld, d["dtw"][128 * k:128 * (k + 1), :], deps=dtoks[-1:],
                   sem="sdw")
        cf = P.v("tensor_copy", dtbf, dtld, deps=[ld] + dtoks[-1:])
        tr = P.t("matmul", ps_tb5, dtbf, identb5, deps=[cf, lI5] + dtoks[-1:],
                 start=True, stop=True, is_transpose=True)
        dtoks.append(P.v("tensor_copy", dtwT[:, 128 * k:128 * (k + 1)], ps_tb5,
                         deps=[tr]))
    prev = {}
    outs = []
    for tb in range(min(NTB, _ntb)):
        ts_ = slice(TB * tb, TB * (tb + 1))
        if "bcdma" in _sk5:
            cbt = P.v("memset", btcr, 0.1, deps=prev.get("bcrow", []))
        else:
            cbt = P.dma("q", btcr.rearrange("o (r t) -> o r t", r=32), d["btct"][:, ts_],
                        deps=prev.get("bcrow", []), sem="sbcr")
        prev["bcrow"] = []
        bctoks = []
        if "bc" in _sk5 or "bcmm" in _sk5:
            bctoks.append(P.v("memset", Bbc, 0.1, deps=[cbt]))
            bctoks.append(P.v("memset", Cbc, 0.1))
        for n in (range(16) if ("bc" not in _sk5 and "bcmm" not in _sk5) else []):
            for which, buf in ((0, Bbc), (1, Cbc)):
                ri = 16 * which + n
                row = btcr[0:1, TB * ri:TB * (ri + 1)]
                m1 = P.t("matmul", ps_bc[:, 0:TB], ones128, row,
                         deps=([cbt, l5] + prev.get("bc", []) if (n == 0 and which == 0)
                               else bctoks[-1:]),
                         start=True, stop=True)
                bctoks.append(P.s("copy", buf[:, TB * n:TB * (n + 1)], ps_bc[:, 0:TB],
                                  deps=[m1]))
        prev["bcrow"] = [bctoks[-1]]
        prev["bc"] = []
        for dk in range(_ndk):
            par = dk % 2
            lxc = P.dma("q", xc[par], d["xct"][128 * dk:128 * (dk + 1), ts_],
                        deps=prev.get("xc" + str(par), []), sem="sxc" + str(par))
            lsg = P.dma("q", sg[par], d["sgt"][128 * dk:128 * (dk + 1), ts_],
                        deps=prev.get("sg" + str(par), []), sem="ssg" + str(par))
            if "sp" in _sk5:
                v2 = P.v("memset", dt_, 0.4, deps=prev.get("dt_", []) + prev.get("dt", []))
                s1 = v2
                prev["dt"] = [v2]
                prev["dtt"] = [v2]
            else:
                mdt = P.t("matmul", ps_dt[:, 0:TB], dtwT[:, 128 * dk:128 * (dk + 1)],
                          tst[:, ts_], deps=[dtoks[-1], l1] + prev.get("dt", []),
                          start=True, stop=True)
                ev0 = P.v("tensor_copy", dtev, ps_dt[:, 0:TB],
                          deps=[mdt] + prev.get("dtev", []))
                dtp = dtev
                s1 = P.s("activation", dtt, dtp, AF.Abs,
                         bias=dtbcol[:, dk:dk + 1],
                         deps=[mdt, l2] + prev.get("dtt", []) + ([ev0] if ev0 else []))
                s2 = P.s("activation", dtt, dtt, AF.Exp, scale=-1.0, deps=[s1])
                s3 = P.s("activation", dtt, dtt, AF.Ln, bias=1.0, deps=[s2])
                v1 = P.v("tensor_scalar", dt_, dtp, dtbcol[:, dk:dk + 1], 0.0,
                         ALU.add, ALU.max, deps=[mdt] + prev.get("dt_", []) +
                         ([ev0] if ev0 else []))
                v2 = P.v("tensor_add", dt_, dt_, dtt, deps=[v1, s3])
                prev["dtev"] = [s1, v1]
                prev["dt"] = [v2, s1]
                prev["dtt"] = [v2]
            v3 = P.v("tensor_mul", dtx, dt_, xc[par], deps=[v2, lxc] +
                     prev.get("dtx", []))
            prev["dtx"] = []
            yz = prev.get("yacc", [])
            satoks = []
            for n in range(16):
                satoks.append(P.s("activation", dAall[:, TB * n:TB * (n + 1)], dt_, AF.Exp,
                                  scale=acol[:, 16 * dk + n:16 * dk + n + 1],
                                  deps=([v2, atoks[-1]] + prev.get("dA", [])
                                        if n == 0 else [])))
            prev["dA"] = [Tok("cv", 0)]
            vys = prev.get("vys", {})
            vsts = prev.get("vsts", {})
            last_vh = None
            for n in range(16):
                sp = n % 2
                vu = P.v("tensor_mul", uu, dtx, Bbc[:, TB * n:TB * (n + 1)],
                         deps=[v3, bctoks[-1], satoks[n]] + prev.get("uu", []))
                prev["uu"] = []
                vh = P.v("tensor_tensor_scan", hh[sp], dAall[:, TB * n:TB * (n + 1)], uu,
                         state[:, 16 * dk + n:16 * dk + n + 1], ALU.mult, ALU.add,
                         deps=[vu] + vys.get(sp, []) + vsts.get(sp, []))
                vst = P.s("copy", state[:, 16 * dk + n:16 * dk + n + 1],
                          hh[sp][:, TB - 1:TB], deps=[vh])
                vsts[sp] = [vst]
                if n == 0:
                    vy = P.v("tensor_mul", yacc, hh[sp], Cbc[:, TB * n:TB * (n + 1)],
                             deps=[vh] + yz)
                    vys[sp] = [vy]
                    vya = vy
                else:
                    vy1 = P.v("tensor_mul", ytmp[sp], hh[sp], Cbc[:, TB * n:TB * (n + 1)],
                              deps=[vh] + prev.get("yt" + str(sp), []))
                    vys[sp] = [vy1]
                    vya = P.p("tensor_add", yacc, yacc, ytmp[sp], deps=[vy1, vya])
                    prev["yt" + str(sp)] = [vya]
                last_vh = vh
            prev["vys"] = vys
            prev["vsts"] = vsts
            prev["dA"] = [last_vh]
            vy = vya
            if "y3c" in _sk5:
                w1 = P.v("memset", yacc, 0.2, deps=[vy, l3, lxc])
            else:
                w1 = P.v("scalar_tensor_tensor", yacc, xc[par], dcol[:, dk:dk + 1], yacc,
                         ALU.mult, ALU.add, deps=[vy, l3, lxc])
            w2 = P.v("tensor_mul", y3, yacc, sg[par], deps=[w1, lsg] +
                     prev.get("y3", []))
            st = P.dma("q", d["y3t"][128 * dk:128 * (dk + 1), ts_], y3, deps=[w2],
                       sem="sy3")
            prev["y3"] = [st]
            prev["yacc"] = [w2]
            prev["xc" + str(par)] = [v3, w1]
            prev["sg" + str(par)] = [w2]
            prev["dt_"] = [v3, satoks[-1]]
            outs.append(st)
        prev["bc"] = [w2]
    return outs


def build_ph6a(nc, P, d, stk=None):
    sb = (lambda n, s, t: stk.enter_context(nc.sbuf_tensor("p6a_" + n, s, t)).ap()) if stk is not None else (lambda n, s, t: nc.alloc_sbuf_tensor("p6a_" + n, s, t).ap())
    ps = (lambda n, s, t: stk.enter_context(nc.psum_tensor("p6a_" + n, s, t)).ap()) if stk is not None else (lambda n, s, t: nc.alloc_psum_tensor("p6a_" + n, s, t).ap())
    xn1t = sb("xn1t", [128, 16 * L], BF16)
    wld = [sb("wld0", [128, D], F32), sb("wld1", [128, D], F32)]
    wcol = [sb("wcol0", [128, D], BF16), sb("wcol1", [128, D], BF16)]
    wbf = sb("wbf", [128, D], BF16)
    mrow1 = sb("mrow1", [1, L], F32)
    maskp = sb("maskp", [128, L], BF16)
    gel = sb("gel", [128, 512], F32)
    hm = sb("hm", [128, 512], BF16)
    ones128 = sb("ones128", [1, 128], F32)
    identb = sb("identb", [128, 128], BF16)
    ps_m = [ps("ps_m0", [128, 512], F32), ps("ps_m1", [128, 512], F32)]
    ps_b = ps("ps_b", [128, 512], F32)
    ps_tb = ps("ps_tb", [128, 512], BF16)

    lI = P.dma("q", identb, d["ident"], sem="sc")
    lxn = P.dma("q", xn1t, d["xn1t"], sem="sxn")
    l5 = P.dma("q", ones128, d["ones128"], sem="sc")
    outs = []
    prev = {}
    mcur = [None]
    for m in range(64):
        par = m % 2
        blk = m // 4
        lw = P.dma("q", wld[par], d["w1f1"][128 * m:128 * (m + 1), :],
                   deps=prev.get("w" + str(par), []), sem="sw" + str(par))
        cb_ = P.v("tensor_copy", wbf, wld[par], deps=[lw] + prev.get("tr", []))
        trs = []
        for j4 in range(4):
            tt4 = []
            for jj in range(4):
                j = 4 * j4 + jj
                tt4.append(P.t("matmul", ps_tb[:, 128 * jj:128 * (jj + 1)],
                               wbf[:, 128 * j:128 * (j + 1)], identb,
                               deps=([cb_, lI] + (trs[-1:] if trs else prev.get("tr", []))
                                     if jj == 0 else []),
                               start=(jj == 0), stop=(jj == 3), is_transpose=True))
            dv = wcol[par].rearrange("p (k o) -> p k o", k=16)[:, 4 * j4:4 * j4 + 4, :]
            trs.append(P.v("tensor_copy", dv, ps_tb.rearrange("p (k o) -> p k o", k=4),
                           deps=tt4))
        prev["tr"] = [trs[-1]]
        prev["w" + str(par)] = [trs[-1]]
        if m % 4 == 0:
            lmk = P.dma("q", mrow1, d["maskt"][blk:blk + 1, :],
                        deps=prev.get("mk", []), sem="smk")
            mtk = prev.get("mk", [])
            for a_ in range(4):
                mm = P.t("matmul", ps_b, ones128, mrow1[0:1, 512 * a_:512 * (a_ + 1)],
                         deps=[l5, lmk] + mtk, start=True, stop=True)
                mtk = [P.s("copy", maskp[:, 512 * a_:512 * (a_ + 1)], ps_b, deps=[mm])]
            mcur[0] = mtk[0]
            prev["mk"] = [mcur[0]]
        for nb in range(4):
            pp = ps_m[nb % 2]
            mms = []
            for k in range(16):
                mms.append(P.t("matmul", pp, wcol[par][:, 128 * k:128 * (k + 1)],
                               xn1t[:, L * k + 512 * nb:L * k + 512 * (nb + 1)],
                               deps=(trs + [lxn] + prev.get("e" + str(nb % 2), [])
                                     if k == 0 else []), **mmf(k, 16)))
            g1 = P.s("activation", gel, pp, AF.Gelu, deps=mms + prev.get("gel", []))
            g2 = P.v("tensor_mul", hm, gel, maskp[:, 512 * nb:512 * (nb + 1)],
                     deps=[g1, mcur[0]] + prev.get("hm", []))
            st = P.dma("q", d["hmt"][128 * m:128 * (m + 1), 512 * nb:512 * (nb + 1)], hm,
                       deps=[g2], sem="shm")
            prev["e" + str(nb % 2)] = [g1]
            prev["gel"] = [g2]
            prev["hm"] = [st]
            outs.append(st)
    return outs


def build_ph6b(nc, P, d, stk=None):
    sb = (lambda n, s, t: stk.enter_context(nc.sbuf_tensor("p6b_" + n, s, t)).ap()) if stk is not None else (lambda n, s, t: nc.alloc_sbuf_tensor("p6b_" + n, s, t).ap())
    ps = (lambda n, s, t: stk.enter_context(nc.psum_tensor("p6b_" + n, s, t)).ap()) if stk is not None else (lambda n, s, t: nc.alloc_psum_tensor("p6b_" + n, s, t).ap())
    w2all = sb("w2all", [128, 64 * 512], BF16)
    opall = sb("opall", [128, 32 * 512], BF16)
    wld = sb("wld", [128, 4 * 128], F32)
    wbf = sb("wbf", [128, 4 * 128], BF16)
    rsl = [sb("rsl0", [128, 512], BF16), sb("rsl1", [128, 512], BF16)]
    brsl = sb("brsl", [128, 512], F32)
    trbuf = sb("trbuf", [128, 512], F32)
    x1t = sb("x1t", [128, D], F32)
    brt = sb("brt", [128, D], F32)
    outt = sb("outt", [128, D], F32)
    identf = sb("identf", [128, 128], F32)
    identb = sb("identb", [128, 128], BF16)
    ps_a = [ps("ps_a" + str(i), [128, 512], F32) for i in range(4)]
    ps_t = ps("ps_t", [128, 512], F32)
    ps_tb = ps("ps_tb", [128, 512], BF16)

    lIf = P.dma("q", identf, d["identf"], sem="sc")
    lI = P.dma("q", identb, d["ident"], sem="sc")
    outs = []
    prev = {}
    for fg in range(4):
        fr = slice(512 * fg, 512 * (fg + 1))
        ptoks = []
        for src_t, nk, wall in ((d["w2f"], 64, w2all), (d["opw"], 32, opall)):
            for kk in range(nk):
                lw = P.dma("q", wld.rearrange("p (g c) -> p g c", g=4),
                           src_t[fr, 128 * kk:128 * (kk + 1)].rearrange(
                               "(g p) c -> p g c", p=128),
                           deps=prev.get("wld", []), sem="swl")
                cw = P.v("tensor_copy", wbf, wld, deps=[lw] + prev.get("wtr", []))
                trs = []
                for g in range(4):
                    trs.append(P.t("matmul", ps_tb[:, 128 * g:128 * (g + 1)],
                                   wbf[:, 128 * g:128 * (g + 1)], identb,
                                   deps=([cw, lI] + prev.get("wcp", []) if g == 0 else []),
                                   start=(g == 0), stop=(g == 3), is_transpose=True))
                cp = P.v("tensor_copy", wall[:, 512 * kk:512 * (kk + 1)], ps_tb, deps=trs)
                prev["wld"] = [cp]
                prev["wtr"] = [cp]
                prev["wcp"] = [cp]
        wready = prev["wcp"]
        for nb in range(4):
            tsl = slice(512 * nb, 512 * (nb + 1))
            mmtoks = {0: [], 1: [], 2: [], 3: []}
            for k in range(96):
                par = k % 2
                if k < 64:
                    lr = P.dma("q", rsl[par],
                               d["hmt"][128 * k:128 * (k + 1), tsl],
                               deps=prev.get("r" + str(par), []), sem="sr" + str(par))
                    wsl = w2all[:, 512 * k:512 * (k + 1)]
                else:
                    dk = k - 64
                    lr = P.dma("q", rsl[par],
                               d["y3t"][128 * dk:128 * (dk + 1), tsl],
                               deps=prev.get("r" + str(par), []), sem="sr" + str(par))
                    wsl = opall[:, 512 * dk:512 * (dk + 1)]
                ftoks = []
                for fi in range(4):
                    ftoks.append(P.t("matmul", ps_a[fi], wsl[:, 128 * fi:128 * (fi + 1)],
                                     rsl[par],
                                     deps=(wready + [lr] + prev.get("ev", [])
                                           if k == 0 and fi == 0 else ([lr] if fi == 0
                                                                       else [])),
                                     start=(k == 0), stop=(k == 95)))
                    mmtoks[fi] = ftoks[-1:]
                prev["r" + str(par)] = [ftoks[-1]]
            evs = []
            for fi in range(4):
                e1 = P.v("tensor_copy", brsl, ps_a[fi], deps=mmtoks[3] + prev.get("br", []))
                trs = []
                for u in range(4):
                    trs.append(P.t("matmul", ps_t[:, 128 * u:128 * (u + 1)],
                                   brsl[:, 128 * u:128 * (u + 1)], identf,
                                   deps=([e1, lIf] + prev.get("trr", []) if u == 0 else []),
                                   start=(u == 0), stop=(u == 3), is_transpose=True))
                e2 = P.v("tensor_copy", trbuf, ps_t, deps=trs + prev.get("trdma", []))
                prev["trr"] = [e2]
                prev["br"] = [e2]
                st = P.dma("q",
                           d["brn"][512 * nb:512 * (nb + 1),
                                    512 * fg + 128 * fi:512 * fg + 128 * (fi + 1)].rearrange(
                               "(u p) f -> p u f", p=128),
                           trbuf.rearrange("p (u f) -> p u f", u=4), deps=[e2], sem="sbn")
                prev["trdma"] = [st]
                evs.append(st)
            prev["ev"] = evs
        prev["wcp"] = prev["wcp"] + prev["ev"]
    fprev = prev["ev"]
    for i in range(16):
        l1 = P.dma("q", x1t, d["x1"][128 * i:128 * (i + 1), :],
                   deps=fprev if i == 0 else prev.get("f1", []), sem="sfx")
        l2 = P.dma("q", brt, d["brn"][128 * i:128 * (i + 1), :],
                   deps=fprev if i == 0 else prev.get("f2", []), sem="sfb")
        a1 = P.v("tensor_add", outt, x1t, brt, deps=[l1, l2] + prev.get("fo", []))
        st = P.dma("q", d["out"][128 * i:128 * (i + 1), :], outt, deps=[a1], sem="sfo")
        prev["f1"] = [a1]
        prev["f2"] = [a1]
        prev["fo"] = [st]
        outs.append(st)
    return outs


def _ph1_nc(consts):
    nc = bass.Bass("TRN2", target_bir_lowering=False, debug=False)
    d = {}
    d["x0"] = nc.dram_tensor("x0", [L, D], F32, kind="ExternalInput").ap()
    for w in ["wq", "wk", "wv"]:
        d[w] = nc.dram_tensor(w, [D, D], F32, kind="ExternalInput").ap()
    d["gcol"] = nc.dram_tensor("gcol", [D, 1], F32, kind="ExternalInput").ap()
    d["bcol"] = nc.dram_tensor("bcol", [D, 1], F32, kind="ExternalInput").ap()
    for cn in ["ropeC", "ropeS", "swapP", "ident"]:
        d[cn] = nc.dram_tensor(cn, list(consts[cn].shape), BF16, kind="ExternalInput").ap()
    d["xnt"] = nc.dram_tensor("xnt", [128, 16 * L], BF16, kind="ExternalOutput").ap()
    d["qtr"] = nc.dram_tensor("qtr", [D, L], BF16, kind="ExternalOutput").ap()
    d["ktr"] = nc.dram_tensor("ktr", [D, L], BF16, kind="ExternalOutput").ap()
    d["bmtp"] = nc.dram_tensor("bmtp", [D, L], F32, kind="ExternalOutput").ap()
    P = Prog(nc)
    P.init_consts([LN_EPS, -0.5])
    P.finish(build_ph1(nc, P, d))
    return nc


def _ph2_nc():
    nc = bass.Bass("TRN2", target_bir_lowering=False, debug=False)
    d = {}
    d["qtr"] = nc.dram_tensor("qtr", [D, L], BF16, kind="ExternalInput").ap()
    d["ktr"] = nc.dram_tensor("ktr", [D, L], BF16, kind="ExternalInput").ap()
    d["bmtp"] = nc.dram_tensor("bmtp", [D, L], F32, kind="ExternalInput").ap()
    d["gcol"] = nc.dram_tensor("gcol", [D, 1], F32, kind="ExternalInput").ap()
    d["w1blk"] = nc.dram_tensor("w1blk", [128, 2048], F32, kind="ExternalInput").ap()
    d["b1t0"] = nc.dram_tensor("b1t0", [128, 16], F32, kind="ExternalInput").ap()
    d["gbb3"] = nc.dram_tensor("gbb3", [3, 2048], F32, kind="ExternalInput").ap()
    d["emx"] = nc.dram_tensor("emx", [16, 512], BF16, kind="ExternalInput").ap()
    d["ones128"] = nc.dram_tensor("ones128", [1, 128], F32, kind="ExternalInput").ap()
    d["onesblk"] = nc.dram_tensor("onesblk", [128, 2], F32, kind="ExternalInput").ap()
    d["onesblkT"] = nc.dram_tensor("onesblkT", [2, 128], F32, kind="ExternalInput").ap()
    d["ident"] = nc.dram_tensor("ident", [128, 128], BF16, kind="ExternalInput").ap()
    d["tout"] = nc.dram_tensor("tout", [D, L], BF16, kind="ExternalOutput").ap()
    P = Prog(nc)
    P.init_consts([TTT_EPS, -0.5, 1.0 / 16, 1.0 / 64])
    P.finish(build_ph2(nc, P, d))
    return nc


def _ph3a_nc():
    nc = bass.Bass("TRN2", target_bir_lowering=False, debug=False)
    d = {}
    d["tout"] = nc.dram_tensor("tout", [D, L], BF16, kind="ExternalInput").ap()
    d["wo"] = nc.dram_tensor("wo", [D, D], F32, kind="ExternalInput").ap()
    d["brT"] = nc.dram_tensor("brT", [D, L], BF16, kind="ExternalOutput").ap()
    P = Prog(nc)
    P.finish(build_ph3a(nc, P, d))
    return nc


def _ph3b_nc():
    nc = bass.Bass("TRN2", target_bir_lowering=False, debug=False)
    d = {}
    d["fw"] = nc.dram_tensor("fw", [D, D], F32, kind="ExternalInput").ap()
    d["fbias"] = nc.dram_tensor("fbias", [D, 1], F32, kind="ExternalInput").ap()
    d["xnt"] = nc.dram_tensor("xnt", [128, 16 * L], BF16, kind="ExternalInput").ap()
    d["brT"] = nc.dram_tensor("brT", [D, L], BF16, kind="ExternalInput").ap()
    d["x0"] = nc.dram_tensor("x0", [L, D], F32, kind="ExternalInput").ap()
    d["ident"] = nc.dram_tensor("ident", [128, 128], BF16, kind="ExternalInput").ap()
    d["identf"] = nc.dram_tensor("identf", [128, 128], F32, kind="ExternalInput").ap()
    d["ones128"] = nc.dram_tensor("ones128", [1, 128], F32, kind="ExternalInput").ap()
    d["x1"] = nc.dram_tensor("x1", [L, D], F32, kind="ExternalOutput").ap()
    d["xn1t"] = nc.dram_tensor("xn1t", [128, 16 * L], BF16, kind="ExternalOutput").ap()
    P = Prog(nc)
    P.init_consts([LN_EPS, -0.5])
    P.finish(build_ph3b(nc, P, d))
    return nc


def _ph4_nc():
    nc = bass.Bass("TRN2", target_bir_lowering=False, debug=False)
    d = {}
    d["xn1t"] = nc.dram_tensor("xn1t", [128, 16 * L], BF16, kind="ExternalInput").ap()
    d["ipw"] = nc.dram_tensor("ipw", [8192, D], F32, kind="ExternalInput").ap()
    d["convw"] = nc.dram_tensor("convw", [4096, 4], F32, kind="ExternalInput").ap()
    d["convb"] = nc.dram_tensor("convb", [4096, 1], F32, kind="ExternalInput").ap()
    d["xpw"] = nc.dram_tensor("xpw", [160, 4096], F32, kind="ExternalInput").ap()
    d["wr"] = nc.dram_tensor("wr", [16, D], F32, kind="ExternalInput").ap()
    d["br"] = nc.dram_tensor("br", [16, 1], F32, kind="ExternalInput").ap()
    d["ident"] = nc.dram_tensor("ident", [128, 128], BF16, kind="ExternalInput").ap()
    d["identf"] = nc.dram_tensor("identf", [128, 128], F32, kind="ExternalInput").ap()
    d["xct"] = nc.dram_tensor("xct", [4096, L], F32, kind="ExternalOutput").ap()
    d["sgt"] = nc.dram_tensor("sgt", [4096, L], F32, kind="ExternalOutput").ap()
    d["tst"] = nc.dram_tensor("tst", [128, L], BF16, kind="ExternalOutput").ap()
    d["btct"] = nc.dram_tensor("btct", [32, L], F32, kind="ExternalOutput").ap()
    d["maskt"] = nc.dram_tensor("maskt", [16, L], F32, kind="ExternalOutput").ap()
    d["xcbt"] = nc.dram_tensor("xcbt", [4096, L], BF16, kind="Internal").ap()
    P = Prog(nc)
    P.init_consts([LN_EPS, -0.5])
    P.finish(build_ph4(nc, P, d))
    return nc


def _ph5_nc():
    nc = bass.Bass("TRN2", target_bir_lowering=False, debug=False)
    d = {}
    d["xct"] = nc.dram_tensor("xct", [4096, L], F32, kind="ExternalInput").ap()
    d["sgt"] = nc.dram_tensor("sgt", [4096, L], F32, kind="ExternalInput").ap()
    d["tst"] = nc.dram_tensor("tst", [128, L], BF16, kind="ExternalInput").ap()
    d["btct"] = nc.dram_tensor("btct", [32, L], F32, kind="ExternalInput").ap()
    d["dtw"] = nc.dram_tensor("dtw", [4096, 128], F32, kind="ExternalInput").ap()
    d["dtb"] = nc.dram_tensor("dtb", [4096, 1], F32, kind="ExternalInput").ap()
    d["alog"] = nc.dram_tensor("alog", [4096, 16], F32, kind="ExternalInput").ap()
    d["dcol"] = nc.dram_tensor("dcol", [4096, 1], F32, kind="ExternalInput").ap()
    d["ones128"] = nc.dram_tensor("ones128", [1, 128], F32, kind="ExternalInput").ap()
    d["ident"] = nc.dram_tensor("ident", [128, 128], BF16, kind="ExternalInput").ap()
    d["y3t"] = nc.dram_tensor("y3t", [4096, L], BF16, kind="ExternalOutput").ap()
    P = Prog(nc)
    P.init_consts([1.0, -1.0, -0.5])
    P.finish(build_ph5(nc, P, d))
    return nc


def _ph6a_nc():
    nc = bass.Bass("TRN2", target_bir_lowering=False, debug=False)
    d = {}
    d["xn1t"] = nc.dram_tensor("xn1t", [128, 16 * L], BF16, kind="ExternalInput").ap()
    d["w1f1"] = nc.dram_tensor("w1f1", [8192, D], F32, kind="ExternalInput").ap()
    d["maskt"] = nc.dram_tensor("maskt", [16, L], F32, kind="ExternalInput").ap()
    d["ident"] = nc.dram_tensor("ident", [128, 128], BF16, kind="ExternalInput").ap()
    d["ones128"] = nc.dram_tensor("ones128", [1, 128], F32, kind="ExternalInput").ap()
    d["hmt"] = nc.dram_tensor("hmt", [8192, L], BF16, kind="ExternalOutput").ap()
    P = Prog(nc)
    P.finish(build_ph6a(nc, P, d))
    return nc


def _ph6b_nc():
    nc = bass.Bass("TRN2", target_bir_lowering=False, debug=False)
    d = {}
    d["hmt"] = nc.dram_tensor("hmt", [8192, L], BF16, kind="ExternalInput").ap()
    d["w2f"] = nc.dram_tensor("w2f", [D, 8192], F32, kind="ExternalInput").ap()
    d["y3t"] = nc.dram_tensor("y3t", [4096, L], BF16, kind="ExternalInput").ap()
    d["opw"] = nc.dram_tensor("opw", [D, 4096], F32, kind="ExternalInput").ap()
    d["x1"] = nc.dram_tensor("x1", [L, D], F32, kind="ExternalInput").ap()
    d["ident"] = nc.dram_tensor("ident", [128, 128], BF16, kind="ExternalInput").ap()
    d["identf"] = nc.dram_tensor("identf", [128, 128], F32, kind="ExternalInput").ap()
    d["brn"] = nc.dram_tensor("brn", [L, D], F32, kind="Internal").ap()
    d["out"] = nc.dram_tensor("out", [L, D], F32, kind="ExternalOutput").ap()
    P = Prog(nc)
    P.finish(build_ph6b(nc, P, d))
    return nc


def kernel_multi(**inputs):
    NC = 8
    cores = list(range(NC))
    consts = host_consts()
    x0s = [np.ascontiguousarray(np.asarray(inputs["L0_attn_hidden_states"])[b],
                                dtype=np.float32) for b in range(NC)]
    w1blk, b1t0, gcol, bcol, gbb3 = ttt_layouts(
        np.asarray(inputs["L0_attn_W1"], np.float32),
        np.asarray(inputs["L0_attn_b1"], np.float32),
        np.asarray(inputs["L0_attn_ttt_norm_weight"], np.float32),
        np.asarray(inputs["L0_attn_ttt_norm_bias"], np.float32))
    f32 = lambda k: np.ascontiguousarray(np.asarray(inputs[k], np.float32))
    col = lambda k: np.ascontiguousarray(np.asarray(inputs[k], np.float32).reshape(-1, 1))

    def launch(nc, maps, outnames):
        res = run_bass_kernel_spmd(nc, maps, core_ids=cores)
        return [{k: np.asarray(res.results[c][k]) for k in outnames} for c in range(NC)]

    base1 = {"wq": f32("L0_attn_wq"), "wk": f32("L0_attn_wk"), "wv": f32("L0_attn_wv"),
             "gcol": gcol, "bcol": bcol}
    base1.update({cn: consts[cn] for cn in ["ropeC", "ropeS", "swapP", "ident"]})
    r1 = launch(_ph1_nc(consts), [dict(base1, x0=x0s[b]) for b in range(NC)],
                ["xnt", "qtr", "ktr", "bmtp"])
    base2 = {"gcol": gcol, "w1blk": w1blk, "b1t0": b1t0, "gbb3": gbb3,
             "emx": consts["emx"], "ones128": consts["ones128"],
             "onesblk": consts["onesblk"], "onesblkT": consts["onesblkT"],
             "ident": consts["ident"]}
    r2 = launch(_ph2_nc(), [dict(base2, qtr=r1[b]["qtr"], ktr=r1[b]["ktr"],
                                 bmtp=r1[b]["bmtp"]) for b in range(NC)], ["tout"])
    r3a = launch(_ph3a_nc(), [{"tout": r2[b]["tout"], "wo": f32("L0_attn_wo")}
                              for b in range(NC)], ["brT"])
    base3 = {"fw": f32("L0_ffn_weight"), "fbias": col("L0_ffn_bias"),
             "ident": consts["ident"], "identf": consts["identf"],
             "ones128": consts["ones128"]}
    r3b = launch(_ph3b_nc(), [dict(base3, xnt=r1[b]["xnt"], brT=r3a[b]["brT"],
                                   x0=x0s[b]) for b in range(NC)], ["x1", "xn1t"])
    base4 = {"ipw": f32("L1_attn_in_proj_w"), "convw": f32("L1_attn_conv_w"),
             "convb": col("L1_attn_conv_b"), "xpw": f32("L1_attn_x_proj_w"),
             "wr": f32("L1_ffn_Wr"), "br": col("L1_ffn_br"),
             "ident": consts["ident"], "identf": consts["identf"]}
    r4 = launch(_ph4_nc(), [dict(base4, xn1t=r3b[b]["xn1t"]) for b in range(NC)],
                ["xct", "sgt", "tst", "btct", "maskt"])
    base5 = {"dtw": f32("L1_attn_dt_proj_w"), "dtb": col("L1_attn_dt_proj_b"),
             "alog": f32("L1_attn_A_log"), "dcol": col("L1_attn_D"),
             "ones128": consts["ones128"], "ident": consts["ident"]}
    r5 = launch(_ph5_nc(), [dict(base5, xct=r4[b]["xct"], sgt=r4[b]["sgt"],
                                 tst=r4[b]["tst"], btct=r4[b]["btct"])
                            for b in range(NC)], ["y3t"])
    base6a = {"w1f1": f32("L1_ffn_W1"), "ident": consts["ident"],
              "ones128": consts["ones128"]}
    r6a = launch(_ph6a_nc(), [dict(base6a, xn1t=r3b[b]["xn1t"], maskt=r4[b]["maskt"])
                              for b in range(NC)], ["hmt"])
    base6b = {"w2f": f32("L1_ffn_W2"), "opw": f32("L1_attn_out_proj_w"),
              "ident": consts["ident"], "identf": consts["identf"]}
    r6b = launch(_ph6b_nc(), [dict(base6b, hmt=r6a[b]["hmt"], y3t=r5[b]["y3t"],
                                   x1=r3b[b]["x1"]) for b in range(NC)], ["out"])
    return np.stack([r6b[b]["out"].astype(np.float32) for b in range(NC)], axis=0)


def build_fused():
    from contextlib import ExitStack
    nc = bass.Bass("TRN2", target_bir_lowering=False, debug=False)
    consts = host_consts()
    d = {}
    EI, IN, EO = "ExternalInput", "Internal", "ExternalOutput"
    def dt_(name, shape, dtype, kind):
        d[name] = nc.dram_tensor(name, shape, dtype, kind=kind).ap()
    dt_("x0", [L, D], F32, EI)
    for w in ["wq", "wk", "wv", "wo", "fw"]:
        dt_(w, [D, D], F32, EI)
    dt_("fbias", [D, 1], F32, EI)
    dt_("gcol", [D, 1], F32, EI)
    dt_("bcol", [D, 1], F32, EI)
    dt_("w1blk", [128, 2048], F32, EI)
    dt_("b1t0", [128, 16], F32, EI)
    dt_("gbb3", [3, 2048], F32, EI)
    dt_("emx", [16, 512], BF16, EI)
    dt_("ones128", [1, 128], F32, EI)
    dt_("onesblk", [128, 2], F32, EI)
    dt_("onesblkT", [2, 128], F32, EI)
    dt_("ropeC", [128, L], BF16, EI)
    dt_("ropeS", [128, L], BF16, EI)
    dt_("swapP", [128, 128], BF16, EI)
    dt_("ident", [128, 128], BF16, EI)
    dt_("identf", [128, 128], F32, EI)
    dt_("ipw", [8192, D], F32, EI)
    dt_("convw", [4096, 4], F32, EI)
    dt_("convb", [4096, 1], F32, EI)
    dt_("xpw", [160, 4096], F32, EI)
    dt_("wr", [16, D], F32, EI)
    dt_("br", [16, 1], F32, EI)
    dt_("dtw", [4096, 128], F32, EI)
    dt_("dtb", [4096, 1], F32, EI)
    dt_("alog", [4096, 16], F32, EI)
    dt_("dcol", [4096, 1], F32, EI)
    dt_("opw", [D, 4096], F32, EI)
    dt_("w1f1", [8192, D], F32, EI)
    dt_("w2f", [D, 8192], F32, EI)
    dt_("xnt", [128, 16 * L], BF16, IN)
    dt_("qtr", [D, L], BF16, IN)
    dt_("ktr", [D, L], BF16, IN)
    dt_("bmtp", [D, L], F32, IN)
    dt_("tout", [D, L], BF16, IN)
    dt_("brT", [D, L], BF16, IN)
    dt_("x1", [L, D], F32, IN)
    dt_("xn1t", [128, 16 * L], BF16, IN)
    dt_("xct", [4096, L], F32, IN)
    dt_("sgt", [4096, L], F32, IN)
    dt_("tst", [128, L], BF16, IN)
    dt_("btct", [32, L], F32, IN)
    dt_("maskt", [16, L], F32, IN)
    dt_("xcbt", [4096, L], BF16, IN)
    dt_("y3t", [4096, L], BF16, IN)
    dt_("hmt", [8192, L], BF16, IN)
    dt_("brn", [L, D], F32, IN)
    dt_("out", [L, D], F32, EO)

    P = Prog(nc)
    P.init_consts([LN_EPS, TTT_EPS, -0.5, -1.0, 1.0, 1.0 / 16, 1.0 / 64])
    final = None
    for builder in (build_ph1, build_ph2, build_ph3a, build_ph3b, build_ph4,
                    build_ph5, build_ph6a, build_ph6b):
        with ExitStack() as stk:
            final = builder(nc, P, d, stk=stk)
            P.barrier()
    P.finish(final)
    return nc


def kernel(**inputs):
    NC = 8
    cores = list(range(NC))
    consts = host_consts()
    w1blk, b1t0, gcol, bcol, gbb3 = ttt_layouts(
        np.asarray(inputs["L0_attn_W1"], np.float32),
        np.asarray(inputs["L0_attn_b1"], np.float32),
        np.asarray(inputs["L0_attn_ttt_norm_weight"], np.float32),
        np.asarray(inputs["L0_attn_ttt_norm_bias"], np.float32))
    f32 = lambda k: np.ascontiguousarray(np.asarray(inputs[k], np.float32))
    col = lambda k: np.ascontiguousarray(np.asarray(inputs[k], np.float32).reshape(-1, 1))
    base = {"wq": f32("L0_attn_wq"), "wk": f32("L0_attn_wk"), "wv": f32("L0_attn_wv"),
            "wo": f32("L0_attn_wo"), "fw": f32("L0_ffn_weight"),
            "fbias": col("L0_ffn_bias"), "gcol": gcol, "bcol": bcol,
            "w1blk": w1blk, "b1t0": b1t0, "gbb3": gbb3,
            "ipw": f32("L1_attn_in_proj_w"), "convw": f32("L1_attn_conv_w"),
            "convb": col("L1_attn_conv_b"), "xpw": f32("L1_attn_x_proj_w"),
            "wr": f32("L1_ffn_Wr"), "br": col("L1_ffn_br"),
            "dtw": f32("L1_attn_dt_proj_w"), "dtb": col("L1_attn_dt_proj_b"),
            "alog": f32("L1_attn_A_log"), "dcol": col("L1_attn_D"),
            "opw": f32("L1_attn_out_proj_w"), "w1f1": f32("L1_ffn_W1"),
            "w2f": f32("L1_ffn_W2")}
    for cn in ["ropeC", "ropeS", "swapP", "ident", "identf", "emx", "ones128",
               "onesblk", "onesblkT"]:
        base[cn] = consts[cn]
    x = np.asarray(inputs["L0_attn_hidden_states"])
    maps = [dict(base, x0=np.ascontiguousarray(x[b], dtype=np.float32))
            for b in range(NC)]
    nc = build_fused()
    res = run_bass_kernel_spmd(nc, maps, core_ids=cores)
    return np.stack([np.asarray(res.results[b]["out"], np.float32) for b in range(NC)],
                    axis=0)
